# version 05
# speedup vs baseline: 1.2274x; 1.0556x over previous
import contextlib
import numpy as np
import ml_dtypes
import concourse.bass as bass
import concourse.mybir as mybir
from concourse.bass_isa import ReduceOp
from concourse import library_config as LC
from concourse.bass_utils import run_bass_kernel_spmd

F32 = mybir.dt.float32
BF16 = mybir.dt.bfloat16
AF = mybir.ActivationFunctionType
ALU = mybir.AluOpType
AX = mybir.AxisListType
BFnp = ml_dtypes.bfloat16

B = 2
T = 2048
D = 1024
TT = 512
NTT = T // TT
NC = D // 128

NEG = -1.0e30
LN_EPS = 1e-5
STRICT_SYNC = True



def _rope0_tables():
    d = np.arange(64)
    i = (d % 32) // 2
    inv = 1.0 / 10000.0 ** (np.arange(0, 32, 2) / 32.0)
    t = np.arange(T, dtype=np.float64)
    th = np.outer(inv[i], t)
    cos, sin = np.cos(th), np.sin(th)
    sgn = np.where(d % 2 == 0, -1.0, 1.0)[:, None]
    s = 32.0 ** -0.5
    return ((cos * s).astype(np.float32), (sgn * sin * s).astype(np.float32),
            cos.astype(np.float32), (sgn * sin).astype(np.float32))


def _xpos_tables():
    m = np.arange(32)
    inv = 1.0 / 10000.0 ** (2.0 * m / 64.0)
    base = (2.0 * m + 0.4 * 64.0) / (1.4 * 64.0)
    t = np.arange(T, dtype=np.float64)
    power = (t - T // 2) / (512 // 2)
    th = np.outer(inv, t)
    sc = base[:, None] ** power[None, :]
    cos, sin = np.cos(th), np.sin(th)
    e = np.arange(64)
    mm = e // 2
    sgn = np.where(e % 2 == 0, -1.0, 1.0)[:, None]
    return ((cos[mm] * sc[mm] * 8.0).astype(np.float32),
            (sgn * sin[mm] * sc[mm] * 8.0).astype(np.float32),
            (cos[mm] / sc[mm]).astype(np.float32),
            (sgn * sin[mm] / sc[mm]).astype(np.float32))


def build_consts():
    c = {}
    c["c_ident_f32"] = np.eye(128, dtype=np.float32)
    c["c_ident_bf16"] = np.eye(128, dtype=np.float32).astype(BFnp)
    c["c_ones_col"] = np.ones((128, 1), np.float32)
    c["c_ones_row"] = np.ones((1, 128), np.float32)
    c["c_ones64_col"] = np.ones((64, 1), np.float32)
    seg2 = np.zeros((128, 2), np.float32)
    seg2[0:64, 0] = 1.0
    seg2[64:128, 1] = 1.0
    c["c_seg64"] = seg2
    seg4 = np.zeros((128, 4), np.float32)
    for g in range(4):
        seg4[g * 32:(g + 1) * 32, g] = 1.0
    c["c_seg32x4"] = seg4
    ii, jj = np.meshgrid(np.arange(128), np.arange(128), indexing="ij")
    c["c_causal"] = np.where(jj > ii, NEG, 0.0).astype(np.float32)
    c["c_anticausal"] = np.where(jj < ii, NEG, 0.0).astype(np.float32)
    a, b_, ck, dk = _rope0_tables()
    c["c_r0_aq"], c["c_r0_bq"], c["c_r0_ak"], c["c_r0_bk"] = a, b_, ck, dk
    aq, bq, ak, bk = _xpos_tables()
    c["c_xp_aq"], c["c_xp_bq"], c["c_xp_ak"], c["c_xp_bk"] = aq, bq, ak, bk
    return c


CONSTS = build_consts()
SWAP_MASK = [i ^ 1 for i in range(32)]


BUF_SPECS = {
    "x_fm":   ([B, D, T], F32),
    "stats0": ([B, 2, T], F32),
    "wqT":    ([D, D], BF16),
    "wkvT":   ([D, 512], BF16),
    "woutT":  ([D, D], BF16),
    "q0":     ([B, D, T], BF16),
    "k0":     ([B, 256, T], BF16),
    "v0t":    ([B, 4, T, 64], BF16),
    "qn0":    ([B, 32, T], F32),
    "kn0":    ([B, 8, T], F32),
    "y0":     ([B, D, T], BF16),
    "x1":     ([B, D, T], F32),
    "xf":     ([B, D, T], F32),
    "fq":     ([B, 512, T], BF16),
    "fk":     ([B, 512, T], BF16),
    "fvt":    ([B, 8, T, 64], BF16),
    "fo":     ([B, 512, T], BF16),
    "fhn":    ([B, D, T], BF16),
    "x1b":    ([B, D, T], F32),
    "xfi":    ([B, D, T], F32),
    "stats1": ([B, 2, T], F32),
    "xn1":    ([B, D, T], BF16),
    "q1":     ([B, D, T], BF16),
    "k1":     ([B, 512, T], BF16),
    "v1t":    ([B, 8, T, 64], BF16),
    "qn1":    ([B, 16, T], F32),
    "kn1":    ([B, 8, T], F32),
    "amem":   ([B, D, T], BF16),
    "a1":     ([B, D, T], BF16),
}

WEIGHT_SPECS = {
    "L0_attn_Wq": ([1024, 1024], F32),
    "L0_attn_Wkv": ([512, 1024], F32),
    "L0_attn_Wout": ([1024, 1024], F32),
    "L0_attn_lambda_init": ([16], F32),
    "L0_attn_rms_weight": ([64], F32),
    "L0_ffn_wqkv": ([2, 1024, 1536], F32),
    "L0_ffn_q_scale": ([2, 64], F32),
    "L0_ffn_k_scale": ([2, 64], F32),
    "L0_ffn_wo": ([2, 512, 1024], F32),
    "L0_ffn_ln_g": ([2, 1024], F32),
    "L0_ffn_ln_b": ([2, 1024], F32),
    "L0_ffn_w1": ([2, 1024, 5460], F32),
    "L0_ffn_b1": ([2, 5460], F32),
    "L0_ffn_w2": ([2, 2730, 1024], F32),
    "L0_ffn_b2": ([2, 1024], F32),
    "L1_attn_Wq": ([1024, 1024], F32),
    "L1_attn_Wk": ([1024, 512], F32),
    "L1_attn_Wv": ([1024, 512], F32),
    "L1_attn_memory": ([512, 512], F32),
    "L1_attn_memory_norm": ([512], F32),
    "L1_attn_memory_weight": ([1], F32),
    "L1_ffn_gate_w": ([1024, 8], F32),
    "L1_ffn_w1": ([8, 1024, 4096], F32),
    "L1_ffn_w3": ([8, 1024, 4096], F32),
    "L1_ffn_w2": ([8, 4096, 1024], F32),
}

ENGINES = ("sync", "tensor", "vector", "scalar", "gpsimd")


def ts(i, n):
    return slice(i * n, (i + 1) * n)


DMA_POOL = {"sync": 20, "gpsimd": 20, "scalar": 8}


class MK:

    def __init__(self, reads=(), writes=()):
        self.nc = bass.Bass("TRN2", target_bir_lowering=False, debug=False)
        self.reads = set(reads)
        self.writes = set(writes)
        self._bufs = {}
        self.used_inputs = set()
        self._sems = {}
        self._sem_guards = []
        for e in ENGINES:
            g = self.nc.semaphore(f"prog_{e}")
            self._sems[e] = g.__enter__()
            self._sem_guards.append(g)
        self.base = {e: 0 for e in ENGINES}
        self.dpool_idx = {q: 0 for q in DMA_POOL}
        self.duses = {}
        for q, n in DMA_POOL.items():
            for j in range(n):
                g = self.nc.semaphore(f"dma_{q}_{j}")
                self._sems[("dma", q, j)] = g.__enter__()
                self._sem_guards.append(g)
                self.duses[(q, j)] = 0
        self._phase_names = set()

    def buf(self, name):
        if name in self._bufs:
            return self._bufs[name]
        if name in BUF_SPECS:
            shape, dt = BUF_SPECS[name]
            if name in self.writes:
                kind = "ExternalOutput"
            elif name in self.reads:
                kind = "ExternalInput"
                self.used_inputs.add(name)
            else:
                kind = "Internal"
        elif name in WEIGHT_SPECS:
            shape, dt = WEIGHT_SPECS[name]
            kind = "ExternalInput"
            self.used_inputs.add(name)
        elif name in CONSTS:
            arr = CONSTS[name]
            shape, dt = list(arr.shape), (BF16 if arr.dtype == BFnp else F32)
            kind = "ExternalInput"
            self.used_inputs.add(name)
        elif name == "x_in":
            shape, dt, kind = [B, T, D], F32, "ExternalInput"
            self.used_inputs.add(name)
        elif name == "y_out":
            shape, dt, kind = [B, T, D], F32, "ExternalOutput"
        else:
            raise KeyError(name)
        ap = self.nc.dram_tensor(name, list(shape), dt, kind=kind).ap()
        self._bufs[name] = ap
        return ap


class Phase:

    def __init__(self, mk: MK, name: str):
        self.mk = mk
        self.name = name
        assert name not in mk._phase_names
        mk._phase_names.add(name)
        self.ops = {e: [] for e in ENGINES}
        self.cnt = {e: 0 for e in ENGINES}
        self.dma_used = set()

    def _emit(self, eng, fn, after, amt, inc_key=None, pre_waits=()):
        seen = {}
        for tok in after:
            if tok is None:
                continue
            k2, v2 = tok
            if k2 != eng or STRICT_SYNC:
                if v2 > seen.get(k2, -1):
                    seen[k2] = v2
        waits = list(pre_waits) + list(seen.items())
        if inc_key is None:
            self.cnt[eng] += amt
            tok = (eng, self.mk.base[eng] + self.cnt[eng])
            self.ops[eng].append((fn, waits, self.mk._sems[eng], amt))
        else:
            self.mk.duses[inc_key[1:]] = self.mk.duses[inc_key[1:]] + 1
            tok = (inc_key, 16 * self.mk.duses[inc_key[1:]])
            self.ops[eng].append((fn, waits, self.mk._sems[inc_key], amt))
        return tok

    def t(self, fn, after=()):
        return self._emit("tensor", fn, after, 1)

    def v(self, fn, after=()):
        return self._emit("vector", fn, after, 1)

    def s(self, fn, after=()):
        return self._emit("scalar", fn, after, 1)

    def g(self, fn, after=()):
        return self._emit("gpsimd", fn, after, 1)

    def dma(self, fn, after=(), q="sync"):
        j = self.mk.dpool_idx[q]
        self.mk.dpool_idx[q] = (j + 1) % DMA_POOL[q]
        key = ("dma", q, j)
        prev = self.mk.duses[(q, j)]
        pre = [(key, 16 * prev)] if prev else []
        self.dma_used.add(key)
        return self._emit(q, fn, after, 16, inc_key=key, pre_waits=pre)

    def finalize(self):
        mk = self.mk
        nc = mk.nc
        by_q = {}
        for key in sorted(self.dma_used, key=str):
            by_q.setdefault(key[1], []).append(key)
        with nc.Block() as blk:
            def mkbody(eng):
                ops = self.ops[eng]
                drains = [(k, 16 * mk.duses[k[1:]]) for k in by_q.get(eng, [])]

                def body(eobj):
                    hwm = {}
                    for fn, waits, sem, amt in ops:
                        for k2, v2 in waits:
                            if v2 > hwm.get(k2, -1):
                                eobj.wait_ge(mk._sems[k2], v2)
                                hwm[k2] = v2
                        inst = fn(eobj)
                        inst.then_inc(sem, amt)
                    for k, v in drains:
                        eobj.wait_ge(mk._sems[k], v)
                return body
            for eng in ENGINES:
                if self.ops[eng] or eng in by_q:
                    getattr(blk, eng)(mkbody(eng))
        for e in ENGINES:
            mk.base[e] += self.cnt[e]
        nc.all_engine_barrier()


class Ring:

    def __init__(self, n):
        self.n = n
        self.slots = [{"w": None, "r": []} for _ in range(n)]
        self.i = -1

    def acquire(self):
        self.i = (self.i + 1) % self.n
        s = self.slots[self.i]
        deps = list(s["r"]) + ([s["w"]] if s["w"] else [])
        s["r"] = []
        s["w"] = None
        return self.i, deps

    def wrote(self, tok):
        self.slots[self.i]["w"] = tok

    def read(self, idx, tok):
        assert tok is not None
        self.slots[idx]["r"].append(tok)



def emit_init(mk: MK, stages=3, max_tiles=99):
    nc = mk.nc
    x_in, x_fm, stats0, xf = mk.buf("x_in"), mk.buf("x_fm"), mk.buf("stats0"), mk.buf("xf")
    ph = Phase(mk, "init")
    with (
        nc.sbuf_tensor("i_idf", [128, 128], F32) as idf,
        nc.sbuf_tensor("i_onc", [128, 1], F32) as onc,
        nc.sbuf_tensor("i_onr", [1, 128], F32) as onr,
        nc.sbuf_tensor("i_tm", [128, 2 * 4 * D], F32) as tm,
        nc.sbuf_tensor("i_fm", [128, 2 * NC * TT], F32) as fm,
        nc.sbuf_tensor("i_sq", [128, TT], F32) as sq,
        nc.sbuf_tensor("i_rows", [1, 8 * TT], F32) as rows,
        nc.sbuf_tensor("i_xn", [128, TT], F32) as xnt,
        nc.psum_tensor("i_pt0", [128, 512], F32) as pt0,
        nc.psum_tensor("i_pt1", [128, 512], F32) as pt1,
        nc.psum_tensor("i_psum", [1, TT], F32) as ps_sum,
        nc.psum_tensor("i_psq", [1, TT], F32) as ps_sq,
        nc.psum_tensor("i_pb0", [128, TT], F32) as pb0,
        nc.psum_tensor("i_pb1", [128, TT], F32) as pb1,
    ):
        pt = [pt0, pt1]
        tC = ph.dma(lambda e: e.dma_start(out=idf[:], in_=mk.buf("c_ident_f32")[:]))
        tC2 = ph.dma(lambda e: e.dma_start(out=onc[:], in_=mk.buf("c_ones_col")[:]))
        tC3 = ph.dma(lambda e: e.dma_start(out=onr[:], in_=mk.buf("c_ones_row")[:]))
        r_tm, r_fm, r_pt = Ring(2), Ring(2), Ring(2)
        r_sq = Ring(1)
        r_xn = Ring(1)
        r_pb = Ring(1)
        r_st = Ring(1)
        r_pss = Ring(1)

        ntile = 0
        for b in range(B):
            for tt in range(NTT):
                ntile += 1
                if ntile > max_tiles:
                    continue
                i_tm, d_tm = r_tm.acquire()
                tload = ph.dma(lambda e, b=b, tt=tt, i=i_tm: e.dma_start(
                    out=tm[:, i * 4 * D:(i + 1) * 4 * D].rearrange("p (s d) -> p s d", s=4),
                    in_=x_in[b, ts(tt, TT), :].rearrange("(s p) d -> p s d", p=128)),
                    after=d_tm)
                r_tm.wrote(tload)
                if stages < 1:
                    st0 = ph.dma(lambda e, b=b, tt=tt, i=i_tm: e.dma_start(
                        out=x_fm[b, 0:128, ts(tt, TT)], in_=tm[:, i * 4 * D: i * 4 * D + TT]),
                        after=[tload], q="gpsimd")
                    r_tm.read(i_tm, st0)
                    continue
                i_fm, d_fm = r_fm.acquire()
                ev_toks = {}
                for s_ in range(4):
                    for c in range(NC):
                        i_pt, d_pt = r_pt.acquire()
                        if stages == 25:
                            ttr = ph.v(lambda e, s_=s_, c=c, i=i_tm, ip=i_pt: e.tensor_copy(
                                pt[ip][:, 0:128],
                                tm[:, i * 4 * D + s_ * D + c * 128: i * 4 * D + s_ * D + (c + 1) * 128]),
                                after=[tload, tC] + d_pt + d_fm)
                        else:
                            ttr = ph.t(lambda e, s_=s_, c=c, i=i_tm, ip=i_pt: e.matmul(
                                pt[ip][:, 0:128],
                                tm[:, i * 4 * D + s_ * D + c * 128: i * 4 * D + s_ * D + (c + 1) * 128],
                                idf[:], start=True, stop=True),
                                after=[tload, tC] + d_pt + d_fm)
                        r_pt.wrote(ttr)
                        r_tm.read(i_tm, ttr)
                        tev = ph.v(lambda e, s_=s_, c=c, i=i_fm, ip=i_pt: e.tensor_copy(
                            fm[:, i * NC * TT + c * TT + s_ * 128: i * NC * TT + c * TT + s_ * 128 + 128],
                            pt[ip][:, 0:128]),
                            after=[ttr])
                        r_pt.read(i_pt, tev)
                        ev_toks.setdefault(c, []).append(tev)
                        r_fm.wrote(tev)
                if stages < 2:
                    for c in range(NC):
                        st1 = ph.dma(lambda e, b=b, tt=tt, c=c, i=i_fm: e.dma_start(
                            out=x_fm[b, ts(c, 128), ts(tt, TT)],
                            in_=fm[:, i * NC * TT + c * TT: i * NC * TT + (c + 1) * TT]),
                            after=ev_toks[c], q="gpsimd")
                        r_fm.read(i_fm, st1)
                    continue
                st_toks = []
                i_ps, d_ps = r_pss.acquire()
                for c in range(NC):
                    i_sq, d_sq = r_sq.acquire()
                    tsq = ph.s(lambda e, c=c, i=i_fm: e.activation(
                        sq[:], fm[:, i * NC * TT + c * TT: i * NC * TT + (c + 1) * TT], AF.Square),
                        after=ev_toks[c] + d_sq)
                    r_sq.wrote(tsq)
                    m1 = ph.t(lambda e, c=c, i=i_fm: e.matmul(
                        ps_sum[:], onc[:], fm[:, i * NC * TT + c * TT: i * NC * TT + (c + 1) * TT],
                        start=(c == 0), stop=(c == NC - 1)),
                        after=ev_toks[c] + [tC2] + d_ps)
                    m2 = ph.t(lambda e, c=c: e.matmul(
                        ps_sq[:], onc[:], sq[:], start=(c == 0), stop=(c == NC - 1)),
                        after=[tsq] + d_ps)
                    r_sq.read(i_sq, m2)
                    r_fm.read(i_fm, m1)
                    r_fm.read(i_fm, m2)
                    st_toks += [m1, m2]
                R = lambda k: rows[:, k * TT:(k + 1) * TT]
                i_st, d_st = r_st.acquire()
                if stages in (21,):
                    e1 = ph.v(lambda e: e.tensor_copy(R(3), ps_sum[:]), after=st_toks + d_st)
                    e2 = ph.v(lambda e: e.tensor_copy(R(4), ps_sq[:]), after=st_toks + d_st + [e1])
                    r_pss.wrote(e2)
                    r_pss.read(i_ps, e1)
                    r_pss.read(i_ps, e2)
                    r_st.wrote(e2)
                    d1 = ph.dma(lambda e, b=b, tt=tt: e.dma_start(out=stats0[b, 0:1, ts(tt, TT)], in_=R(4)), after=[e2], q="gpsimd")
                    d2 = ph.dma(lambda e, b=b, tt=tt: e.dma_start(out=stats0[b, 1:2, ts(tt, TT)], in_=R(3)), after=[e1, e2], q="gpsimd")
                    r_st.read(i_st, d1)
                    r_st.read(i_st, d2)
                    for c in range(NC):
                        st1 = ph.dma(lambda e, b=b, tt=tt, c=c, i=i_fm: e.dma_start(
                            out=x_fm[b, ts(c, 128), ts(tt, TT)],
                            in_=fm[:, i * NC * TT + c * TT: i * NC * TT + (c + 1) * TT]),
                            after=ev_toks[c], q="gpsimd")
                        r_fm.read(i_fm, st1)
                    continue
                v1 = ph.v(lambda e: e.tensor_scalar_mul(R(0), ps_sum[:], -1.0 / D), after=st_toks + d_st)
                s1 = ph.s(lambda e: e.activation(R(1), ps_sum[:], AF.Square, scale=1.0 / D), after=st_toks + d_st + [v1])
                v2 = ph.v(lambda e: e.tensor_scalar(R(2), ps_sq[:], 1.0 / D, LN_EPS, ALU.mult, ALU.add), after=st_toks + d_st)
                v3 = ph.v(lambda e: e.tensor_sub(R(2), R(2), R(1)), after=[v2, s1])
                if stages == 26:
                    s2 = ph.s(lambda e: e.activation(R(3), R(2), AF.Copy), after=[v3])
                    v4 = ph.v(lambda e: e.tensor_copy(R(3), R(3)), after=[s2])
                elif stages == 27:
                    s2 = ph.s(lambda e: e.activation(R(3), R(2), AF.Sqrt), after=[v3])
                    v4 = ph.v(lambda e: e.tensor_copy(R(3), R(3)), after=[s2])
                else:
                    s2 = ph.s(lambda e: e.activation(R(3), R(2), AF.Sqrt), after=[v3])
                    v4 = ph.v(lambda e: e.reciprocal(R(3), R(3)), after=[s2])
                v5 = ph.v(lambda e: e.tensor_mul(R(4), R(0), R(3)), after=[v1, v4])
                r_pss.wrote(v5)
                r_pss.read(i_ps, v1)
                r_pss.read(i_ps, s1)
                r_pss.read(i_ps, v2)
                d1 = ph.dma(lambda e, b=b, tt=tt: e.dma_start(out=stats0[b, 0:1, ts(tt, TT)], in_=R(4)), after=[v5], q="gpsimd")
                d2 = ph.dma(lambda e, b=b, tt=tt: e.dma_start(out=stats0[b, 1:2, ts(tt, TT)], in_=R(3)), after=[v4, v5], q="gpsimd")
                if stages < 3 or stages in (25, 26, 27):
                    for c in range(NC):
                        st1 = ph.dma(lambda e, b=b, tt=tt, c=c, i=i_fm: e.dma_start(
                            out=x_fm[b, ts(c, 128), ts(tt, TT)],
                            in_=fm[:, i * NC * TT + c * TT: i * NC * TT + (c + 1) * TT]),
                            after=ev_toks[c], q="gpsimd")
                        r_fm.read(i_fm, st1)
                    continue
                i_pb, d_pb = r_pb.acquire()
                bc1 = ph.t(lambda e: e.matmul(pb0[:], onr[:], R(3), start=True, stop=True),
                           after=[v4, v5, tC3] + d_pb)
                bc2 = ph.t(lambda e: e.matmul(pb1[:], onr[:], R(4), start=True, stop=True),
                           after=[v5, tC3] + d_pb)
                r_pb.wrote(bc2)
                r_st.wrote(v5)
                for tok in (d1, d2, bc1, bc2):
                    r_st.read(i_st, tok)
                for c in range(NC):
                    st1 = ph.dma(lambda e, b=b, tt=tt, c=c, i=i_fm: e.dma_start(
                        out=x_fm[b, ts(c, 128), ts(tt, TT)],
                        in_=fm[:, i * NC * TT + c * TT: i * NC * TT + (c + 1) * TT]),
                        after=ev_toks[c], q="gpsimd")
                    r_fm.read(i_fm, st1)
                    i_xn, d_xn = r_xn.acquire()
                    x1_ = ph.v(lambda e, c=c, i=i_fm: e.tensor_mul(
                        xnt[:], fm[:, i * NC * TT + c * TT: i * NC * TT + (c + 1) * TT], pb0[:]),
                        after=[bc1] + ev_toks[c] + d_xn)
                    x2_ = ph.v(lambda e: e.tensor_add(xnt[:], xnt[:], pb1[:]),
                               after=[x1_, bc2])
                    r_fm.read(i_fm, x1_)
                    r_pb.read(i_pb, x1_)
                    r_pb.read(i_pb, x2_)
                    st2 = ph.dma(lambda e, b=b, tt=tt, c=c: e.dma_start(
                        out=xf[b, ts(c, 128), ts(tt, TT)], in_=xnt[:]),
                        after=[x2_], q="gpsimd")
                    r_xn.wrote(st2)
        ph.finalize()



def emit_wtrans(mk: MK):
    nc = mk.nc
    jobs = [("L0_attn_Wq", "wqT", 1024, 1024),
            ("L0_attn_Wkv", "wkvT", 512, 1024),
            ("L0_attn_Wout", "woutT", 1024, 1024)]
    ph = Phase(mk, "wtrans")
    with (
        nc.sbuf_tensor("w_idb", [128, 128], BF16) as idb,
        nc.sbuf_tensor("w_in", [128, 2 * 1024], BF16) as win,
        nc.sbuf_tensor("w_out", [128, 2 * 128], BF16) as wout,
        nc.psum_tensor("w_pt0", [128, 128], BF16) as pt0,
        nc.psum_tensor("w_pt1", [128, 128], BF16) as pt1,
    ):
        pt = [pt0, pt1]
        tC = ph.dma(lambda e: e.dma_start(out=idb[:], in_=mk.buf("c_ident_bf16")[:]))
        r_in, r_pt, r_out = Ring(2), Ring(2), Ring(2)
        for wname, oname, R_, C_ in jobs:
            w_h = mk.buf(wname)
            o_h = mk.buf(oname)
            for rc in range(R_ // 128):
                i_in, d_in = r_in.acquire()
                ld = ph.dma(lambda e, rc=rc, i=i_in, w_h=w_h, C_=C_: e.dma_start(
                    out=win[:, i * 1024: i * 1024 + C_], in_=w_h[ts(rc, 128), :]),
                    after=d_in, q="gpsimd")
                r_in.wrote(ld)
                for cc in range(C_ // 128):
                    i_pt, d_pt = r_pt.acquire()
                    tr = ph.t(lambda e, cc=cc, i=i_in, ip=i_pt: e.transpose(
                        pt[ip][:],
                        win[:, i * 1024 + cc * 128: i * 1024 + (cc + 1) * 128], idb[:]),
                        after=[ld, tC] + d_pt)
                    r_pt.wrote(tr)
                    r_in.read(i_in, tr)
                    i_o, d_o = r_out.acquire()
                    ev = ph.v(lambda e, io=i_o, ip=i_pt: e.tensor_copy(
                        wout[:, io * 128:(io + 1) * 128], pt[ip][:]),
                        after=[tr] + d_o)
                    r_pt.read(i_pt, ev)
                    st = ph.dma(lambda e, rc=rc, cc=cc, io=i_o, o_h=o_h: e.dma_start(
                        out=o_h[ts(cc, 128), ts(rc, 128)], in_=wout[:, io * 128:(io + 1) * 128]),
                        after=[ev], q="gpsimd")
                    r_out.wrote(st)
        ph.finalize()



def emit_p1(mk: MK):
    nc = mk.nc
    xf = mk.buf("xf")
    q0, k0, v0t = mk.buf("q0"), mk.buf("k0"), mk.buf("v0t")
    qn0, kn0 = mk.buf("qn0"), mk.buf("kn0")
    wqT_h, wkvT_h = mk.buf("wqT"), mk.buf("wkvT")
    ph = Phase(mk, "p1")
    with (
        nc.sbuf_tensor("p1_wq", [128, NC * D], BF16) as wq,
        nc.sbuf_tensor("p1_wkv", [128, NC * 512], BF16) as wkv,
        nc.sbuf_tensor("p1_tab", [128, 4 * T], F32) as tab,
        nc.sbuf_tensor("p1_idb", [128, 128], BF16) as idb,
        nc.sbuf_tensor("p1_seg", [128, 4], F32) as seg,
        nc.sbuf_tensor("p1_xnb", [128, 2 * NC * TT], BF16) as xnb,
        nc.sbuf_tensor("p1_f", [128, 4 * TT], F32) as fwork,
        nc.sbuf_tensor("p1_qb", [128, TT], BF16) as qb,
        nc.sbuf_tensor("p1_sq", [128, TT], F32) as sqt,
        nc.sbuf_tensor("p1_nr", [4, TT], F32) as nrows,
        nc.sbuf_tensor("p1_vb", [128, TT], BF16) as vb,
        nc.sbuf_tensor("p1_vt", [128, 2 * 128], BF16) as vt,
        nc.psum_tensor("p1_p0", [128, TT], F32) as pj0,
        nc.psum_tensor("p1_p1", [128, TT], F32) as pj1,
        nc.psum_tensor("p1_n0", [4, TT], F32) as pn0,
        nc.psum_tensor("p1_t0", [128, 128], BF16) as pt0,
        nc.psum_tensor("p1_t1", [128, 128], BF16) as pt1,
    ):
        pj = [pj0, pj1]
        ptt = [pt0, pt1]
        TAB = {"aq": 0, "bq": 1, "ak": 2, "bk": 3}

        wtoks = []
        for kc in range(NC):
            wtoks.append(ph.dma(lambda e, kc=kc: e.dma_start(
                out=wq[:, kc * D:(kc + 1) * D], in_=wqT_h[ts(kc, 128), :])))
            wtoks.append(ph.dma(lambda e, kc=kc: e.dma_start(
                out=wkv[:, kc * 512:(kc + 1) * 512], in_=wkvT_h[ts(kc, 128), :])))
        for nm, cn in [("aq", "c_r0_aq"), ("bq", "c_r0_bq"), ("ak", "c_r0_ak"), ("bk", "c_r0_bk")]:
            for half in range(2):
                wtoks.append(ph.dma(lambda e, nm=nm, cn=cn, half=half: e.dma_start(
                    out=tab[half * 64:(half + 1) * 64, TAB[nm] * T:(TAB[nm] + 1) * T],
                    in_=mk.buf(cn)[:]),))
        wtoks.append(ph.dma(lambda e: e.dma_start(out=idb[:], in_=mk.buf("c_ident_bf16")[:])))
        wtoks.append(ph.dma(lambda e: e.dma_start(out=seg[:], in_=mk.buf("c_seg32x4")[:])))
        wL = wtoks

        r_xnb = Ring(2)
        r_pj = Ring(2)
        r_f = Ring(1)
        r_qb = Ring(1)
        r_sq = Ring(1)
        r_nr = Ring(1)
        r_pn = Ring(1)
        r_vb = Ring(1)
        r_pt = Ring(2)
        r_vt = Ring(2)

        def rope_chunk(oc, tt, tloads, is_q, d_pj, i_pj, i_x):
            w_sb, outw = (wq, D) if is_q else (wkv, 512)
            mtoks = []
            for kc in range(NC):
                mtoks.append(ph.t(lambda e, kc=kc, oc=oc, i=i_pj, ix=i_x, w_sb=w_sb, outw=outw: e.matmul(
                    pj[i][:], w_sb[:, kc * outw + oc * 128: kc * outw + (oc + 1) * 128],
                    xnb[:, ix * NC * TT + kc * TT: ix * NC * TT + (kc + 1) * TT],
                    start=(kc == 0), stop=(kc == NC - 1)),
                    after=([tloads[kc]] + wL + d_pj) if kc == 0 else [tloads[kc]]))
            mm = mtoks[-1]
            i_f, d_f = r_f.acquire()
            F = lambda k: fwork[:, k * TT:(k + 1) * TT]
            ev = ph.s(lambda e, i=i_pj: e.copy(F(0), pj[i][:]), after=[mm] + d_f)
            r_pj.read(i_pj, ev)
            sh = ph.v(lambda e: e.stream_shuffle(F(1), F(0), SWAP_MASK), after=[ev])
            ta, tb = ("aq", "bq") if is_q else ("ak", "bk")
            tcol = lambda nm: tab[:, TAB[nm] * T + (tt % NTT) * TT: TAB[nm] * T + (tt % NTT + 1) * TT]
            m1 = ph.v(lambda e: e.tensor_mul(F(2), F(0), tcol(ta)), after=[ev] + wL)
            m2 = ph.v(lambda e: e.tensor_mul(F(3), F(1), tcol(tb)), after=[sh] + wL)
            i_qb, d_qb = r_qb.acquire()
            ad = ph.v(lambda e: e.tensor_add(qb[:], F(2), F(3)), after=[m1, m2] + d_qb)
            r_f.wrote(ad)
            dst = q0 if is_q else k0
            st = ph.dma(lambda e, oc=oc, tt=tt: e.dma_start(
                out=dst[tt // NTT, ts(oc, 128), ts(tt % NTT, TT)], in_=qb[:]), after=[ad])
            r_qb.wrote(st)
            i_sq, d_sq = r_sq.acquire()
            sqk = ph.s(lambda e: e.activation(sqt[:], qb[:], AF.Square), after=[ad] + d_sq)
            r_qb.read(i_qb, sqk)
            r_sq.wrote(sqk)
            i_pn, d_pn = r_pn.acquire()
            nm1 = ph.t(lambda e: e.matmul(pn0[:], seg[:], sqt[:], start=True, stop=True),
                       after=[sqk] + wL + d_pn)
            r_sq.read(i_sq, nm1)
            i_nr, d_nr = r_nr.acquire()
            sq2 = ph.s(lambda e: e.activation(nrows[:], pn0[:], AF.Sqrt), after=[nm1] + d_nr)
            r_pn.wrote(sq2)
            r_pn.read(i_pn, sq2)
            ndst, nrow0 = (qn0, 4 * oc) if is_q else (kn0, 4 * oc)
            stn = ph.dma(lambda e, oc=oc, tt=tt, nrow0=nrow0: e.dma_start(
                out=ndst[tt // NTT, nrow0:nrow0 + 4, ts(tt % NTT, TT)], in_=nrows[:]),
                after=[sq2], q="gpsimd")
            r_nr.wrote(stn)
            return mm

        def v_chunk(oc, b, tt, tloads, i_x):
            i_pj, d_pj = r_pj.acquire()
            mtoks = []
            for kc in range(NC):
                mtoks.append(ph.t(lambda e, kc=kc, oc=oc, i=i_pj, ix=i_x: e.matmul(
                    pj[i][:], wkv[:, kc * 512 + oc * 128: kc * 512 + (oc + 1) * 128],
                    xnb[:, ix * NC * TT + kc * TT: ix * NC * TT + (kc + 1) * TT],
                    start=(kc == 0), stop=(kc == NC - 1)),
                    after=([tloads[kc]] + wL + d_pj) if kc == 0 else [tloads[kc]]))
            mm = mtoks[-1]
            r_pj.wrote(mm)
            r_xnb.read(i_x, mm)
            i_vb, d_vb = r_vb.acquire()
            ev = ph.s(lambda e, i=i_pj: e.copy(vb[:], pj[i][:]), after=[mm] + d_vb)
            r_pj.read(i_pj, ev)
            r_vb.wrote(ev)
            for sub in range(4):
                i_pt, d_pt = r_pt.acquire()
                tr = ph.t(lambda e, sub=sub, ip=i_pt: e.transpose(
                    ptt[ip][:], vb[:, sub * 128:(sub + 1) * 128], idb[:]),
                    after=[ev] + wL + d_pt)
                r_pt.wrote(tr)
                r_vb.read(i_vb, tr)
                i_vt, d_vt = r_vt.acquire()
                evt = ph.v(lambda e, iv=i_vt, ip=i_pt: e.tensor_copy(
                    vt[:, iv * 128:(iv + 1) * 128], ptt[ip][:]), after=[tr] + d_vt)
                r_pt.read(i_pt, evt)
                r_vt.wrote(evt)
                for h2 in range(2):
                    kvh = (oc - 2) * 2 + h2
                    stv = ph.dma(lambda e, b=b, tt=tt, sub=sub, kvh=kvh, h2=h2, iv=i_vt:
                                 e.dma_start(
                        out=v0t[b, kvh, tt * TT + sub * 128: tt * TT + (sub + 1) * 128, :],
                        in_=vt[:, iv * 128 + h2 * 64: iv * 128 + h2 * 64 + 64]),
                        after=[evt])
                    r_vt.read(i_vt, stv)
            return mm

        for b in range(B):
            for tt in range(NTT):
                gt = b * NTT + tt
                i_x, d_x = r_xnb.acquire()
                tloads = []
                for c in range(NC):
                    tloads.append(ph.dma(lambda e, b=b, tt=tt, c=c, i=i_x: e.dma_start(
                        out=xnb[:, i * NC * TT + c * TT: i * NC * TT + (c + 1) * TT],
                        in_=xf[b, ts(c, 128), ts(tt, TT)]), after=d_x, q="gpsimd"))
                r_xnb.wrote(tloads[-1])
                for oc in range(NC):
                    i_pj, d_pj = r_pj.acquire()
                    mm = rope_chunk(oc, gt, tloads, True, d_pj, i_pj, i_x)
                    r_pj.wrote(mm)
                    r_xnb.read(i_x, mm)
                for oc in range(2):
                    i_pj, d_pj = r_pj.acquire()
                    mm = rope_chunk(oc, gt, tloads, False, d_pj, i_pj, i_x)
                    r_pj.wrote(mm)
                    r_xnb.read(i_x, mm)
                for oc in range(2, 4):
                    v_chunk(oc, b, tt, tloads, i_x)
        ph.finalize()



def emit_p2(mk: MK):
    nc = mk.nc
    q0, k0, v0t = mk.buf("q0"), mk.buf("k0"), mk.buf("v0t")
    y0 = mk.buf("y0")
    lam_h = mk.buf("L0_attn_lambda_init")
    rmsw_h = mk.buf("L0_attn_rms_weight")
    ph = Phase(mk, "p2")
    with contextlib.ExitStack() as _st:
        onr = _st.enter_context(nc.sbuf_tensor("p2_onr", [1, 128], F32))
        on64 = _st.enter_context(nc.sbuf_tensor("p2_on64", [64, 1], F32))
        antic = _st.enter_context(nc.sbuf_tensor("p2_mask", [128, 128], F32))
        lamr = _st.enter_context(nc.sbuf_tensor("p2_lamr", [1, 16], F32))
        rmsw = _st.enter_context(nc.sbuf_tensor("p2_rmsw", [1, 64], F32))
        kh = _st.enter_context(nc.sbuf_tensor("p2_kh", [64, 2 * T], BF16))
        qh_sb = _st.enter_context(nc.sbuf_tensor("p2_qh", [64, 2 * T], BF16))
        vtb = _st.enter_context(nc.sbuf_tensor("p2_vt", [128, 2, 16, 65], BF16))
        pT = _st.enter_context(nc.sbuf_tensor("p2_pT", [128, 4 * TT], BF16))
        sdb = _st.enter_context(nc.sbuf_tensor("p2_sd", [128, 2 * 128], F32))
        ysb = _st.enter_context(nc.sbuf_tensor("p2_ys", [65, 2 * 2 * TT], F32))
        yac = _st.enter_context(nc.sbuf_tensor("p2_ya", [64, 2 * TT], F32))
        ytm = _st.enter_context(nc.sbuf_tensor("p2_ytm", [64, TT], F32))
        ytm2 = _st.enter_context(nc.sbuf_tensor("p2_ytm2", [64, TT], F32))
        lamn = _st.enter_context(nc.sbuf_tensor("p2_lamn", [128, 16], F32))
        rmsc = _st.enter_context(nc.sbuf_tensor("p2_rmsc", [64, 1], F32))
        on65 = _st.enter_context(nc.sbuf_tensor("p2_on65", [65, 128], F32))
        ys2 = _st.enter_context(nc.sbuf_tensor("p2_ys2", [64, TT], F32))
        rrow = _st.enter_context(nc.sbuf_tensor("p2_rrow", [1, 2 * TT], F32))
        ynb = _st.enter_context(nc.sbuf_tensor("p2_ynb", [64, 2 * TT], BF16))
        pS0 = _st.enter_context(nc.psum_tensor("p2_s0", [128, TT], F32))
        pS1 = _st.enter_context(nc.psum_tensor("p2_s1", [128, TT], F32))
        pS2 = _st.enter_context(nc.psum_tensor("p2_s2", [128, TT], F32))
        yU1a = _st.enter_context(nc.psum_tensor("p2_y1a", [128, TT], F32))
        yU1b = _st.enter_context(nc.psum_tensor("p2_y1b", [128, TT], F32))
        yU2a = _st.enter_context(nc.psum_tensor("p2_y2a", [128, TT], F32))
        yU2b = _st.enter_context(nc.psum_tensor("p2_y2b", [128, TT], F32))
        pmisc = _st.enter_context(nc.psum_tensor("p2_misc", [128, TT], F32))
        pS = [pS0, pS1, pS2]
        yUs = [[yU1a, yU1b], [yU2a, yU2b]]
        cl = []
        cl.append(ph.dma(lambda e: e.dma_start(out=onr[:], in_=mk.buf("c_ones_row")[:])))
        cl.append(ph.dma(lambda e: e.dma_start(out=on64[:], in_=mk.buf("c_ones64_col")[:])))
        cl.append(ph.dma(lambda e: e.dma_start(out=antic[:], in_=mk.buf("c_anticausal")[:])))
        cl.append(ph.dma(lambda e: e.dma_start(
            out=lamr[:], in_=lam_h.rearrange('(o d) -> o d', o=1))))
        cl.append(ph.dma(lambda e: e.dma_start(
            out=rmsw[:], in_=rmsw_h.rearrange('(o d) -> o d', o=1))))
        sg = ph.s(lambda e: e.activation(lamr[:], lamr[:], AF.Sigmoid), after=cl)
        msv = ph.v(lambda e: e.memset(vtb[:, :, :, 64:65], 1.0))
        ms65 = ph.v(lambda e: e.memset(on65[:], 1.0))
        r_kh, r_qh, r_vt = Ring(2), Ring(2), Ring(2)
        r_pS, r_pT = Ring(3), Ring(4)
        r_sd = Ring(2)
        r_yU = [Ring(2), Ring(2)]
        r_ys, r_misc = Ring(2), Ring(1)
        r_ya, r_ytm, r_yt2, r_ys2, r_rr, r_yn = (
            Ring(1), Ring(1), Ring(1), Ring(1), Ring(1), Ring(2))
        i_m, d_m = r_misc.acquire()
        bl0 = ph.t(lambda e: e.matmul(pmisc[0:128, 0:16], onr[:], lamr[:],
                                      start=True, stop=True), after=[sg] + cl + d_m)
        r_misc.wrote(bl0)
        lvv = ph.v(lambda e: e.tensor_scalar(lamn[:], pmisc[0:128, 0:16], -1.0, None,
                                             ALU.mult), after=[bl0])
        r_misc.read(i_m, lvv)
        i_m, d_m = r_misc.acquire()
        tr0 = ph.t(lambda e: e.matmul(pmisc[0:64, 0:1], rmsw[:], onr[0:1, 0:1],
                                      start=True, stop=True), after=cl + d_m + [lvv])
        r_misc.wrote(tr0)
        rmc = ph.s(lambda e: e.copy(rmsc[:], pmisc[0:64, 0:1]), after=[tr0])
        r_misc.read(i_m, rmc)

        def emit_epi(st):
            (b, qh, qc, pv_last, iy1, iy2) = st
            i_ys, d_ys = r_ys.acquire()
            ev1 = ph.s(lambda e, i=i_ys: e.copy(ysb[:, 2 * i * TT:(2 * i + 1) * TT],
                                                yUs[0][iy1][0:65, :]),
                       after=[pv_last[0]] + d_ys)
            ev2 = ph.s(lambda e, i=i_ys: e.copy(ysb[:, (2 * i + 1) * TT:(2 * i + 2) * TT],
                                                yUs[1][iy2][0:65, :]),
                       after=[pv_last[1]] + d_ys)
            r_yU[0].read(iy1, ev1)
            r_yU[1].read(iy2, ev2)
            r_ys.wrote(ev2)
            i_m, d_m = r_misc.acquire()
            bc1 = ph.t(lambda e, i=i_ys: e.matmul(
                pmisc[0:64, :], on65[64:65, 0:64],
                ysb[64:65, 2 * i * TT:(2 * i + 1) * TT],
                start=True, stop=True), after=[ev1, ms65] + cl + d_m)
            r_misc.wrote(bc1)
            i_yt, d_yt = r_ytm.acquire()
            rb1 = ph.v(lambda e: e.reciprocal(ytm[:], pmisc[0:64, :]),
                       after=[bc1] + d_yt)
            r_misc.read(i_m, rb1)
            r_ytm.wrote(rb1)
            i_ya, d_ya = r_ya.acquire()
            v1 = ph.v(lambda e, i=i_ys, ia=i_ya: e.tensor_mul(
                yac[:, ia * TT:(ia + 1) * TT],
                ysb[0:64, 2 * i * TT:(2 * i + 1) * TT], ytm[:]),
                after=[rb1, ev1] + d_ya)
            r_ytm.read(i_yt, v1)
            r_ys.read(i_ys, v1)
            r_ya.wrote(v1)
            i_m, d_m = r_misc.acquire()
            bc2 = ph.t(lambda e, i=i_ys: e.matmul(
                pmisc[0:64, :], on65[64:65, 0:64],
                ysb[64:65, (2 * i + 1) * TT:(2 * i + 2) * TT],
                start=True, stop=True), after=[ev2, rb1, ms65] + cl + d_m)
            r_misc.wrote(bc2)
            i_yt, d_yt = r_ytm.acquire()
            rb2 = ph.v(lambda e: e.reciprocal(ytm[:], pmisc[0:64, :]),
                       after=[bc2] + d_yt)
            r_misc.read(i_m, rb2)
            r_ytm.wrote(rb2)
            i_y2t, d_y2t = r_yt2.acquire()
            v2 = ph.v(lambda e, i=i_ys, qh=qh: e.scalar_tensor_tensor(
                ytm2[:], ytm[:], lamn[0:64, qh:qh + 1],
                ysb[0:64, (2 * i + 1) * TT:(2 * i + 2) * TT],
                ALU.mult, ALU.mult), after=[rb2, ev2, lvv] + d_y2t)
            r_ytm.read(i_yt, v2)
            r_ys.read(i_ys, v2)
            r_yt2.wrote(v2)
            v3 = ph.v(lambda e, ia=i_ya: e.tensor_add(
                yac[:, ia * TT:(ia + 1) * TT], yac[:, ia * TT:(ia + 1) * TT],
                ytm2[:]), after=[v1, v2])
            r_yt2.read(i_y2t, v3)
            r_ya.wrote(v3)
            i_y2s, d_y2s = r_ys2.acquire()
            sqv = ph.v(lambda e, ia=i_ya: e.tensor_mul(
                ys2[:], yac[:, ia * TT:(ia + 1) * TT],
                yac[:, ia * TT:(ia + 1) * TT]), after=[v3] + d_y2s)
            r_ys2.wrote(sqv)
            i_m, d_m = r_misc.acquire()
            nm_ = ph.t(lambda e: e.matmul(pmisc[0:1, 0:TT], on64[:], ys2[:],
                                          start=True, stop=True),
                       after=[sqv] + cl + d_m)
            r_misc.wrote(nm_)
            r_ys2.read(i_y2s, nm_)
            i_rw, d_rw = r_rr.acquire()
            evr = ph.s(lambda e: e.copy(rrow[0:1, 0:TT], pmisc[0:1, 0:TT]),
                       after=[nm_] + d_rw)
            r_misc.read(i_m, evr)
            r_rr.wrote(evr)
            i_m, d_m = r_misc.acquire()
            bc3 = ph.t(lambda e: e.matmul(pmisc[0:64, :], onr[0:1, 0:64],
                                          rrow[0:1, 0:TT],
                                          start=True, stop=True),
                       after=[evr] + cl + d_m)
            r_misc.wrote(bc3)
            r_rr.read(i_rw, bc3)
            i_yt, d_yt = r_ytm.acquire()
            u_ = ph.v(lambda e: e.tensor_scalar(ytm[:], pmisc[0:64, :],
                                                1.0 / 64, 1e-6, ALU.mult, ALU.add),
                      after=[bc3] + d_yt)
            r_misc.read(i_m, u_)
            r_ytm.wrote(u_)
            i_y2t, d_y2t = r_yt2.acquire()
            sq2 = ph.s(lambda e: e.activation(ytm2[:], ytm[:], AF.Sqrt),
                       after=[u_] + d_y2t)
            r_ytm.read(i_yt, sq2)
            rsq = ph.v(lambda e: e.reciprocal(ytm2[:], ytm2[:]), after=[sq2])
            r_yt2.wrote(rsq)
            i_yn, d_yn = r_yn.acquire()
            yn = ph.v(lambda e, ia=i_ya, iy=i_yn: e.scalar_tensor_tensor(
                ynb[:, iy * TT:(iy + 1) * TT], ytm2[:], rmsc[0:64, 0:1],
                yac[:, ia * TT:(ia + 1) * TT], ALU.mult, ALU.mult),
                after=[rsq, v3, rmc] + d_yn)
            r_yt2.read(i_y2t, yn)
            r_ya.read(i_ya, yn)
            r_yn.wrote(yn)
            sty = ph.dma(lambda e, b=b, qh=qh, qc=qc, iy=i_yn: e.dma_start(
                out=y0[b, qh * 64:(qh + 1) * 64, ts(qc, TT)],
                in_=ynb[:, iy * TT:(iy + 1) * TT]), after=[yn])
            r_yn.read(i_yn, sty)

        pending = None
        for b in range(B):
            for qh in range(16):
                kvh = qh // 4
                if qh % 4 == 0:
                    i_kh, d_kh = r_kh.acquire()
                    lk = ph.dma(lambda e, b=b, kvh=kvh, i=i_kh: e.dma_start(
                        out=kh[:, i * T:(i + 1) * T],
                        in_=k0[b, kvh * 64:(kvh + 1) * 64, :]), after=d_kh)
                    r_kh.wrote(lk)
                    i_vt, d_vt = r_vt.acquire()
                    lv = ph.dma(lambda e, b=b, kvh=kvh, i=i_vt: e.dma_start(
                        out=vtb[:, i, :, 0:64],
                        in_=v0t[b, kvh].rearrange("(n p) d -> p n d", p=128)), after=d_vt)
                    r_vt.wrote(lv)
                i_qh, d_qh = r_qh.acquire()
                lq = ph.dma(lambda e, b=b, qh=qh, i=i_qh: e.dma_start(
                    out=qh_sb[:, i * T:(i + 1) * T],
                    in_=q0[b, qh * 64:(qh + 1) * 64, :]), after=d_qh)
                r_qh.wrote(lq)
                for qc in range(4):
                    nkt = 4 * (qc + 1)
                    i_y1, d_y1 = r_yU[0].acquire()
                    i_y2, d_y2 = r_yU[1].acquire()
                    d_y = [d_y1, d_y2]
                    iy = [i_y1, i_y2]
                    units = [(kt, s_) for kt in range(nkt) for s_ in range(2)]
                    ex_t = {}
                    pt_i = {}

                    def emit_sc(u, qc=qc, i_kh=i_kh, i_qh=i_qh, lk=lk, lq=lq,
                                ex_t=ex_t, pt_i=pt_i):
                        kt, s_ = u
                        i_ps, d_ps = r_pS.acquire()
                        sc = ph.t(lambda e, kt=kt, s_=s_, qc=qc, ik=i_kh, iq=i_qh,
                                  ip=i_ps: e.matmul(
                            pS[ip][:],
                            kh[s_ * 32:(s_ + 1) * 32, ik * T + kt * 128: ik * T + (kt + 1) * 128],
                            qh_sb[s_ * 32:(s_ + 1) * 32,
                                  iq * T + qc * TT: iq * T + (qc + 1) * TT],
                            start=True, stop=True, tile_position=(s_ * 32, 0)),
                            after=[lk, lq] + d_ps)
                        r_pS.wrote(sc)
                        r_kh.read(i_kh, sc)
                        r_qh.read(i_qh, sc)
                        i_pt, d_pt = r_pT.acquire()
                        nmask = kt - 4 * qc
                        exu = []
                        if nmask > 0:
                            mz = ph.v(lambda e, it=i_pt, nm=nmask: e.memset(
                                pT[:, it * TT: it * TT + nm * 128], 0.0), after=d_pt)
                            exu.append(mz)
                        if 0 <= nmask <= 3:
                            i_sd, d_sd = r_sd.acquire()
                            ad = ph.v(lambda e, ip=i_ps, isd=i_sd, nm=max(nmask, 0): e.tensor_add(
                                sdb[:, isd * 128:(isd + 1) * 128],
                                pS[ip][:, nm * 128:(nm + 1) * 128], antic[:]),
                                after=[sc] + cl + d_sd)
                            r_sd.wrote(ad)
                            ed = ph.s(lambda e, it=i_pt, isd=i_sd, nm=max(nmask, 0): e.activation(
                                pT[:, it * TT + nm * 128: it * TT + (nm + 1) * 128],
                                sdb[:, isd * 128:(isd + 1) * 128], AF.Exp),
                                after=[ad] + d_pt)
                            r_sd.read(i_sd, ed)
                            exu.append(ed)
                            nfull0 = nmask + 1
                        else:
                            nfull0 = 0
                        if nfull0 < 4:
                            ex = ph.s(lambda e, ip=i_ps, it=i_pt, n0=nfull0: e.activation(
                                pT[:, it * TT + n0 * 128: (it + 1) * TT],
                                pS[ip][:, n0 * 128: TT], AF.Exp), after=[sc] + d_pt)
                            exu.append(ex)
                        for t_ in exu:
                            r_pS.read(i_ps, t_)
                            r_pT.wrote(t_)
                        ex_t[u] = exu
                        pt_i[u] = i_pt

                    def emit_pv(u, nkt=nkt, i_vt=i_vt, d_y=d_y, iy=iy, lv=lv,
                                ex_t=ex_t, pt_i=pt_i):
                        kt, s_ = u
                        pv = ph.t(lambda e, kt=kt, s_=s_, iv=i_vt, it=pt_i[u],
                                  ib=iy[s_]: e.matmul(
                            yUs[s_][ib][0:65, :], vtb[:, iv, kt, :],
                            pT[:, it * TT:(it + 1) * TT],
                            start=(kt == 0), stop=(kt == nkt - 1)),
                            after=(ex_t[u] + [lv, msv] + d_y[s_]) if kt == 0 else ex_t[u])
                        r_pT.read(pt_i[u], pv)
                        r_vt.read(i_vt, pv)
                        return pv

                    emit_sc(units[0])
                    emit_sc(units[1])
                    pv_last = [None, None]
                    for iu in range(2, len(units)):
                        emit_sc(units[iu])
                        u = units[iu - 2]
                        pv_last[u[1]] = emit_pv(u)
                    for u in units[-2:]:
                        pv_last[u[1]] = emit_pv(u)
                    r_yU[0].wrote(pv_last[0])
                    r_yU[1].wrote(pv_last[1])
                    if pending is not None:
                        emit_epi(pending)
                    pending = (b, qh, qc, pv_last, i_y1, i_y2)
        emit_epi(pending)
        ph.finalize()



def emit_p3(mk: MK):
    nc = mk.nc
    y0, x_fm, x1 = mk.buf("y0"), mk.buf("x_fm"), mk.buf("x1")
    woutT_h = mk.buf("woutT")
    ph = Phase(mk, "p3")
    with contextlib.ExitStack() as _st:
        wo = _st.enter_context(nc.sbuf_tensor("p3_wo", [128, NC * D], BF16))
        yb = _st.enter_context(nc.sbuf_tensor("p3_yb", [128, 2 * NC * TT], BF16))
        xb = _st.enter_context(nc.sbuf_tensor("p3_xb", [128, 2 * TT], F32))
        ob = _st.enter_context(nc.sbuf_tensor("p3_ob", [128, 2 * TT], F32))
        pj0 = _st.enter_context(nc.psum_tensor("p3_p0", [128, TT], F32))
        pj1 = _st.enter_context(nc.psum_tensor("p3_p1", [128, TT], F32))
        pj = [pj0, pj1]
        wtoks = []
        for kc in range(NC):
            wtoks.append(ph.dma(lambda e, kc=kc: e.dma_start(
                out=wo[:, kc * D:(kc + 1) * D], in_=woutT_h[ts(kc, 128), :])))
        r_y, r_x, r_o, r_pj = Ring(2), Ring(2), Ring(2), Ring(2)
        for b in range(B):
            for tt in range(NTT):
                i_y, d_y = r_y.acquire()
                yloads = []
                for c in range(NC):
                    yloads.append(ph.dma(lambda e, b=b, tt=tt, c=c, i=i_y: e.dma_start(
                        out=yb[:, i * NC * TT + c * TT: i * NC * TT + (c + 1) * TT],
                        in_=y0[b, ts(c, 128), ts(tt, TT)]), after=d_y))
                r_y.wrote(yloads[-1])
                for oc in range(NC):
                    i_x, d_x = r_x.acquire()
                    lx = ph.dma(lambda e, b=b, tt=tt, oc=oc, i=i_x: e.dma_start(
                        out=xb[:, i * TT:(i + 1) * TT],
                        in_=x_fm[b, ts(oc, 128), ts(tt, TT)]), after=d_x, q="gpsimd")
                    r_x.wrote(lx)
                    i_pj, d_pj = r_pj.acquire()
                    mt = []
                    for kc in range(NC):
                        mt.append(ph.t(lambda e, kc=kc, oc=oc, i=i_pj, iy=i_y: e.matmul(
                            pj[i][:], wo[:, kc * D + oc * 128: kc * D + (oc + 1) * 128],
                            yb[:, iy * NC * TT + kc * TT: iy * NC * TT + (kc + 1) * TT],
                            start=(kc == 0), stop=(kc == NC - 1)),
                            after=([yloads[kc]] + wtoks + d_pj) if kc == 0 else [yloads[kc]]))
                    mm = mt[-1]
                    r_pj.wrote(mm)
                    r_y.read(i_y, mm)
                    i_o, d_o = r_o.acquire()
                    ad = ph.v(lambda e, i=i_pj, io=i_o, ix=i_x: e.tensor_add(
                        ob[:, io * TT:(io + 1) * TT], pj[i][:], xb[:, ix * TT:(ix + 1) * TT]),
                        after=[mm, lx] + d_o)
                    r_pj.read(i_pj, ad)
                    r_x.read(i_x, ad)
                    r_o.wrote(ad)
                    st = ph.dma(lambda e, b=b, tt=tt, oc=oc, io=i_o: e.dma_start(
                        out=x1[b, ts(oc, 128), ts(tt, TT)], in_=ob[:, io * TT:(io + 1) * TT]),
                        after=[ad])
                    r_o.read(i_o, st)
        ph.finalize()



def _p4_s1(mk, l, src_name):
    nc = mk.nc
    src = mk.buf(src_name)
    fq, fk, fvt = mk.buf("fq"), mk.buf("fk"), mk.buf("fvt")
    wqkv_h = mk.buf("L0_ffn_wqkv")
    qs_h, ks_h = mk.buf("L0_ffn_q_scale"), mk.buf("L0_ffn_k_scale")
    ph = Phase(mk, f"p4l{l}s1")
    with contextlib.ExitStack() as _st:
        wq = _st.enter_context(nc.sbuf_tensor(f"s1_wq_{l}", [128, NC * 1536], BF16))
        wq2 = _st.enter_context(nc.sbuf_tensor(f"s1_wq2_{l}", [128, NC * 1024], BF16))
        tabs = _st.enter_context(nc.sbuf_tensor(f"s1_tab_{l}", [128, 4 * T], F32))
        idb = _st.enter_context(nc.sbuf_tensor(f"s1_idb_{l}", [128, 128], BF16))
        seg = _st.enter_context(nc.sbuf_tensor(f"s1_seg_{l}", [128, 2], F32))
        eq = _st.enter_context(nc.sbuf_tensor(f"s1_eq_{l}", [2, 2 * 128], F32))
        qsrow = _st.enter_context(nc.sbuf_tensor(f"s1_qsr_{l}", [1, 256], F32))
        xfb = _st.enter_context(nc.sbuf_tensor(f"s1_xfb_{l}", [128, NC * T], BF16))
        fw = _st.enter_context(nc.sbuf_tensor(f"s1_f_{l}", [128, 5 * TT], F32))
        nr = _st.enter_context(nc.sbuf_tensor(f"s1_nr_{l}", [2, 2 * TT], F32))
        qb = _st.enter_context(nc.sbuf_tensor(f"s1_qb_{l}", [128, TT], BF16))
        vb = _st.enter_context(nc.sbuf_tensor(f"s1_vb_{l}", [128, TT], BF16))
        vt = _st.enter_context(nc.sbuf_tensor(f"s1_vt_{l}", [128, 2 * 128], BF16))
        pj0 = _st.enter_context(nc.psum_tensor(f"s1_p0_{l}", [128, TT], F32))
        pj1 = _st.enter_context(nc.psum_tensor(f"s1_p1_{l}", [128, TT], F32))
        pn = _st.enter_context(nc.psum_tensor(f"s1_pn_{l}", [2, TT], F32))
        psc = _st.enter_context(nc.psum_tensor(f"s1_psc_{l}", [128, TT], F32))
        pt0 = _st.enter_context(nc.psum_tensor(f"s1_pt0_{l}", [128, 128], BF16))
        pt1 = _st.enter_context(nc.psum_tensor(f"s1_pt1_{l}", [128, 128], BF16))
        pj = [pj0, pj1]
        ptt = [pt0, pt1]
        TAB = {"aq": 0, "bq": 1, "ak": 2, "bk": 3}
        cl = []
        for kc in range(NC):
            cl.append(ph.dma(lambda e, kc=kc: e.dma_start(
                out=wq[:, kc * 1536:(kc + 1) * 1536], in_=wqkv_h[l, ts(kc, 128), :]),
                q="gpsimd"))
        for nm, cn in [("aq", "c_xp_aq"), ("bq", "c_xp_bq"), ("ak", "c_xp_ak"), ("bk", "c_xp_bk")]:
            for half in range(2):
                cl.append(ph.dma(lambda e, nm=nm, cn=cn, half=half: e.dma_start(
                    out=tabs[half * 64:(half + 1) * 64, TAB[nm] * T:(TAB[nm] + 1) * T],
                    in_=mk.buf(cn)[:])))
        cl.append(ph.dma(lambda e: e.dma_start(out=idb[:], in_=mk.buf("c_ident_bf16")[:])))
        cl.append(ph.dma(lambda e: e.dma_start(out=seg[:], in_=mk.buf("c_seg64")[:])))
        for kc in range(NC):
            cl.append(ph.v(lambda e, kc=kc: e.tensor_copy(
                wq2[:, kc * 1024:(kc + 1) * 1024],
                wq[:, kc * 1536: kc * 1536 + 1024].rearrange("p (g h m) -> p g m h", g=16, h=2)),
                after=[cl[kc]]))
        mz = ph.g(lambda e: e.memset(eq[:], 0.0))
        lq_ = ph.dma(lambda e: e.dma_start(
            out=qsrow[0:1, 0:64], in_=qs_h[l].rearrange("(o d) -> o d", o=1)))
        lk_ = ph.dma(lambda e: e.dma_start(
            out=qsrow[0:1, 64:128], in_=ks_h[l].rearrange("(o d) -> o d", o=1)))
        pq = ph.v(lambda e: e.tensor_copy(
            qsrow[0:1, 128:192], qsrow[0:1, 0:64].rearrange("o (h m) -> o m h", h=2)),
            after=[lq_])
        pk = ph.v(lambda e: e.tensor_copy(
            qsrow[0:1, 192:256], qsrow[0:1, 64:128].rearrange("o (h m) -> o m h", h=2)),
            after=[lk_])
        for g in range(2):
            cl.append(ph.dma(lambda e, g=g: e.dma_start(
                out=eq[g:g + 1, g * 64: g * 64 + 64], in_=qsrow[0:1, 128:192]),
                after=[mz, pq], q="gpsimd"))
            cl.append(ph.dma(lambda e, g=g: e.dma_start(
                out=eq[g:g + 1, 128 + g * 64: 128 + g * 64 + 64], in_=qsrow[0:1, 192:256]),
                after=[mz, pk], q="gpsimd"))
        r_pj, r_f, r_qb, r_nr, r_pn, r_psc = Ring(2), Ring(1), Ring(1), Ring(1), Ring(1), Ring(1)
        r_vb, r_pt, r_vt = Ring(1), Ring(2), Ring(2)
        r_xfb = Ring(1)
        F = lambda k: fw[:, k * TT:(k + 1) * TT]

        def qk_chunk(b, tt, oc, is_q, xtoks, i_x):
            i_pj, d_pj = r_pj.acquire()
            mt = []
            for kc in range(NC):
                sh = 0 if kc < 4 else -1
                c0 = tt * TT + sh
                wsl = lambda kc, oc: wq2[:, kc * 1024 + oc * 128: kc * 1024 + (oc + 1) * 128]
                if tt == 0 and kc >= 4:
                    mt.append(ph.t(lambda e, kc=kc, oc=oc, i=i_pj, ix=i_x: e.matmul(
                        pj[i][:, 1:TT], wsl(kc, oc),
                        xfb[:, ix * NC * T + kc * T: ix * NC * T + kc * T + TT - 1],
                        start=False, stop=(kc == NC - 1)),
                        after=[xtoks[kc]]))
                else:
                    mt.append(ph.t(lambda e, kc=kc, oc=oc, i=i_pj, ix=i_x, c0=c0: e.matmul(
                        pj[i][:], wsl(kc, oc),
                        xfb[:, ix * NC * T + kc * T + c0: ix * NC * T + kc * T + c0 + TT],
                        start=(kc == 0), stop=(kc == NC - 1)),
                        after=([xtoks[kc]] + cl + d_pj) if kc == 0 else [xtoks[kc]]))
            mm = mt[-1]
            r_pj.wrote(mm)
            i_f, d_f = r_f.acquire()
            ev = ph.s(lambda e, i=i_pj: e.copy(F(0), pj[i][:]), after=[mm] + d_f)
            r_pj.read(i_pj, ev)
            sq = ph.s(lambda e: e.activation(F(1), F(0), AF.Square), after=[ev])
            i_pn, d_pn = r_pn.acquire()
            nm1 = ph.t(lambda e: e.matmul(pn[:], seg[:], F(1), start=True, stop=True),
                       after=[sq] + cl + d_pn)
            r_pn.wrote(nm1)
            NR = lambda k: nr[:, k * TT:(k + 1) * TT]
            i_nr, d_nr = r_nr.acquire()
            s1_ = ph.s(lambda e: e.activation(NR(0), pn[:], AF.Sqrt), after=[nm1] + d_nr)
            r_pn.read(i_pn, s1_)
            c1 = ph.v(lambda e: e.tensor_scalar_max(NR(0), NR(0), 1e-12), after=[s1_])
            c2 = ph.v(lambda e: e.reciprocal(NR(0), NR(0)), after=[c1])
            r_nr.wrote(c2)
            i_ps, d_ps = r_psc.acquire()
            off = 0 if is_q else 128
            bc = ph.t(lambda e, off=off: e.matmul(psc[:], eq[:, off:off + 128], NR(0),
                                                  start=True, stop=True),
                      after=[c2] + cl + d_ps)
            r_psc.wrote(bc)
            r_nr.read(i_nr, bc)
            ln = ph.v(lambda e: e.tensor_mul(F(2), F(0), psc[:]), after=[bc, ev])
            r_psc.read(i_ps, ln)
            sh2 = ph.v(lambda e: e.stream_shuffle(F(3), F(2), SWAP_MASK), after=[ln])
            ta, tb_ = ("aq", "bq") if is_q else ("ak", "bk")
            tc = lambda nm: tabs[:, TAB[nm] * T + tt * TT: TAB[nm] * T + (tt + 1) * TT]
            x1_ = ph.v(lambda e: e.tensor_mul(F(4), F(2), tc(ta)), after=[sh2] + cl)
            i_qb, d_qb = r_qb.acquire()
            x2_ = ph.v(lambda e: e.scalar_tensor_tensor(
                F(3), F(3), 1.0, tc(tb_), ALU.mult, ALU.mult), after=[sh2] + cl)
            x3_ = ph.v(lambda e: e.tensor_add(qb[:], F(4), F(3)), after=[x1_, x2_] + d_qb)
            r_f.wrote(x3_)
            dst = fq if is_q else fk
            soc = oc if is_q else oc - 4
            st = ph.dma(lambda e, b=b, tt=tt, soc=soc, dst=dst: e.dma_start(
                out=dst[b, ts(soc, 128), ts(tt, TT)], in_=qb[:]), after=[x3_])
            r_qb.wrote(st)
            return mm

        def v_chunk4(b, tt, oc, xtoks, i_x):
            i_pj, d_pj = r_pj.acquire()
            mt = []
            for kc in range(NC):
                if tt == 0 and kc >= 4:
                    mt.append(ph.t(lambda e, kc=kc, oc=oc, i=i_pj, ix=i_x: e.matmul(
                        pj[i][:, 1:TT], wq[:, kc * 1536 + (8 + oc) * 128: kc * 1536 + (9 + oc) * 128],
                        xfb[:, ix * NC * T + kc * T: ix * NC * T + kc * T + TT - 1],
                        start=False, stop=(kc == NC - 1)), after=[xtoks[kc]]))
                else:
                    sh = 0 if kc < 4 else -1
                    c0 = tt * TT + sh
                    mt.append(ph.t(lambda e, kc=kc, oc=oc, i=i_pj, ix=i_x, c0=c0: e.matmul(
                        pj[i][:], wq[:, kc * 1536 + (8 + oc) * 128: kc * 1536 + (9 + oc) * 128],
                        xfb[:, ix * NC * T + kc * T + c0: ix * NC * T + kc * T + c0 + TT],
                        start=(kc == 0), stop=(kc == NC - 1)),
                        after=([xtoks[kc]] + cl + d_pj) if kc == 0 else [xtoks[kc]]))
            mm = mt[-1]
            r_pj.wrote(mm)
            i_vb, d_vb = r_vb.acquire()
            ev = ph.s(lambda e, i=i_pj: e.copy(vb[:], pj[i][:]), after=[mm] + d_vb)
            r_pj.read(i_pj, ev)
            r_vb.wrote(ev)
            for sub in range(4):
                i_pt, d_pt = r_pt.acquire()
                tr = ph.t(lambda e, sub=sub, ip=i_pt: e.transpose(
                    ptt[ip][:], vb[:, sub * 128:(sub + 1) * 128], idb[:]),
                    after=[ev] + cl + d_pt)
                r_pt.wrote(tr)
                r_vb.read(i_vb, tr)
                i_vt, d_vt = r_vt.acquire()
                evt = ph.v(lambda e, iv=i_vt, ip=i_pt: e.tensor_copy(
                    vt[:, iv * 128:(iv + 1) * 128], ptt[ip][:]), after=[tr] + d_vt)
                r_pt.read(i_pt, evt)
                r_vt.wrote(evt)
                for h2 in range(2):
                    hh = oc * 2 + h2
                    stv = ph.dma(lambda e, b=b, tt=tt, sub=sub, hh=hh, h2=h2, iv=i_vt:
                                 e.dma_start(
                        out=fvt[b, hh, tt * TT + sub * 128: tt * TT + (sub + 1) * 128, :],
                        in_=vt[:, iv * 128 + h2 * 64: iv * 128 + h2 * 64 + 64]),
                        after=[evt])
                    r_vt.read(i_vt, stv)
            return mm

        for b in range(B):
            i_x, d_x = r_xfb.acquire()
            xtoks = []
            for c in range(NC):
                xtoks.append(ph.dma(lambda e, b=b, c=c: e.dma_start(
                    out=xfb[:, c * T:(c + 1) * T], in_=src[b, ts(c, 128), :]),
                    after=d_x, q="gpsimd"))
            for tt in range(NTT):
                for oc in range(4):
                    mm = qk_chunk(b, tt, oc, True, xtoks, 0)
                    r_xfb.read(i_x, mm)
                for oc in range(4, 8):
                    mm = qk_chunk(b, tt, oc, False, xtoks, 0)
                    r_xfb.read(i_x, mm)
                for oc in range(4):
                    mmv = v_chunk4(b, tt, oc, xtoks, 0)
                    r_xfb.read(i_x, mmv)
            r_xfb.wrote(xtoks[-1])
        ph.finalize()


def _p4_s2(mk, l):
    nc = mk.nc
    fq, fk, fvt, fo = mk.buf("fq"), mk.buf("fk"), mk.buf("fvt"), mk.buf("fo")
    ph = Phase(mk, f"p4l{l}s2")
    W = 512
    with contextlib.ExitStack() as _st:
        idb = _st.enter_context(nc.sbuf_tensor(f"s2_idb_{l}", [128, 128], BF16))
        mc = _st.enter_context(nc.sbuf_tensor(f"s2_mc_{l}", [128, 128], F32))
        ma = _st.enter_context(nc.sbuf_tensor(f"s2_ma_{l}", [128, 128], F32))
        qhb = _st.enter_context(nc.sbuf_tensor(f"s2_qh_{l}", [64, T], BF16))
        khb = _st.enter_context(nc.sbuf_tensor(f"s2_kh_{l}", [64, T], BF16))
        vtb = _st.enter_context(nc.sbuf_tensor(f"s2_vt_{l}", [128, 16 * 64], BF16))
        ssb = _st.enter_context(nc.sbuf_tensor(f"s2_s_{l}", [128, 640], F32))
        psb = _st.enter_context(nc.sbuf_tensor(f"s2_p_{l}", [128, 640], BF16))
        rr = _st.enter_context(nc.sbuf_tensor(f"s2_rr_{l}", [128, 2], F32))
        dTt = _st.enter_context(nc.sbuf_tensor(f"s2_dT_{l}", [128, 2 * 128], BF16))
        yb = _st.enter_context(nc.sbuf_tensor(f"s2_yb_{l}", [64, 128], BF16))
        psA = _st.enter_context(nc.psum_tensor(f"s2_pA_{l}", [128, TT], F32))
        psB = _st.enter_context(nc.psum_tensor(f"s2_pB_{l}", [128, TT], F32))
        pt0 = _st.enter_context(nc.psum_tensor(f"s2_pt0_{l}", [128, 128], BF16))
        pt1 = _st.enter_context(nc.psum_tensor(f"s2_pt1_{l}", [128, 128], BF16))
        yT = _st.enter_context(nc.psum_tensor(f"s2_yT_{l}", [64, 128], F32))
        ptt = [pt0, pt1]
        cl = []
        cl.append(ph.dma(lambda e: e.dma_start(out=idb[:], in_=mk.buf("c_ident_bf16")[:])))
        cl.append(ph.dma(lambda e: e.dma_start(out=mc[:], in_=mk.buf("c_causal")[:])))
        cl.append(ph.dma(lambda e: e.dma_start(out=ma[:], in_=mk.buf("c_anticausal")[:])))
        r_q, r_k, r_v = Ring(1), Ring(1), Ring(1)
        r_sA, r_sB, r_ss, r_ps, r_rr = Ring(1), Ring(1), Ring(1), Ring(1), Ring(1)
        r_pt, r_dT, r_yT, r_yb = Ring(2), Ring(2), Ring(1), Ring(1)
        for b in range(B):
            for h in range(8):
                i_q, d_q = r_q.acquire()
                lq = ph.dma(lambda e, b=b, h=h: e.dma_start(
                    out=qhb[:], in_=fq[b, h * 64:(h + 1) * 64, :]), after=d_q)
                r_q.wrote(lq)
                i_k, d_k = r_k.acquire()
                lk = ph.dma(lambda e, b=b, h=h: e.dma_start(
                    out=khb[:], in_=fk[b, h * 64:(h + 1) * 64, :]), after=d_k)
                r_k.wrote(lk)
                i_v, d_v = r_v.acquire()
                lv = ph.dma(lambda e, b=b, h=h: e.dma_start(
                    out=vtb[:].rearrange("p (n d) -> p n d", n=16),
                    in_=fvt[b, h].rearrange("(n p) d -> p n d", p=128)), after=d_v)
                r_v.wrote(lv)
                for w in range(4):
                    for qt in range(4):
                        r0 = qt * 128
                        wq0 = w * W + r0
                        wA = (W - r0) if w > 0 else 0
                        wB = r0 + 128
                        wtot = wA + wB
                        i_sA, d_sA = r_sA.acquire()
                        toks_mm = []
                        if wA > 0:
                            mA = ph.t(lambda e, wq0=wq0, w=w, r0=r0, wA=wA: e.matmul(
                                psA[:, 0:wA], qhb[:, wq0:wq0 + 128],
                                khb[:, (w - 1) * W + r0: w * W],
                                start=True, stop=True), after=[lq, lk] + d_sA)
                            toks_mm.append(mA)
                            r_sA.wrote(mA)
                        i_sB, d_sB = r_sB.acquire()
                        mB = ph.t(lambda e, wq0=wq0, w=w, wB=wB: e.matmul(
                            psB[:, 0:wB], qhb[:, wq0:wq0 + 128],
                            khb[:, w * W: w * W + wB],
                            start=True, stop=True), after=[lq, lk] + d_sB)
                        toks_mm.append(mB)
                        r_sB.wrote(mB)
                        r_q.read(i_q, mB)
                        r_k.read(i_k, mB)
                        i_ss, d_ss = r_ss.acquire()
                        ev_toks = []
                        if wA > 0:
                            e1 = ph.v(lambda e: e.tensor_add(ssb[:, 0:128], psA[:, 0:128], ma[:]),
                                      after=[toks_mm[0]] + cl + d_ss)
                            ev_toks.append(e1)
                            if wA > 128:
                                e2 = ph.s(lambda e, wA=wA: e.copy(ssb[:, 128:wA], psA[:, 128:wA]),
                                          after=[toks_mm[0], e1] + d_ss)
                                ev_toks.append(e2)
                                r_sA.read(i_sA, e2)
                            r_sA.read(i_sA, e1)
                        if wB > 128:
                            e3 = ph.s(lambda e, wA=wA, wB=wB: e.copy(
                                ssb[:, wA:wA + wB - 128], psB[:, 0:wB - 128]),
                                after=[mB] + d_ss)
                            ev_toks.append(e3)
                        e4 = ph.v(lambda e, wA=wA, wB=wB: e.tensor_add(
                            ssb[:, wA + wB - 128: wA + wB], psB[:, wB - 128:wB], mc[:]),
                            after=[mB] + cl + d_ss + ([e3] if wB > 128 else []))
                        ev_toks.append(e4)
                        r_sB.read(i_sB, e4)
                        r_ss.wrote(e4)
                        i_rr, d_rr = r_rr.acquire()
                        mx = ph.v(lambda e, wtot=wtot: e.reduce_max(
                            rr[:, 0:1], ssb[:, 0:wtot], axis=AX.X, negate=True),
                            after=ev_toks + d_rr)
                        i_ps, d_ps = r_ps.acquire()
                        ex = ph.s(lambda e, wtot=wtot: e.activation(
                            psb[:, 0:wtot], ssb[:, 0:wtot], AF.Exp, bias=rr[:, 0:1],
                            accum_out=rr[:, 1:2]), after=[mx] + ev_toks + d_ps)
                        r_ss.read(i_ss, ex)
                        rc = ph.v(lambda e: e.reciprocal(rr[:, 1:2], rr[:, 1:2]), after=[ex])
                        r_rr.wrote(rc)
                        pnrm = ph.v(lambda e, wtot=wtot: e.tensor_scalar(
                            psb[:, 0:wtot], psb[:, 0:wtot], rr[:, 1:2], None, ALU.mult),
                            after=[rc, ex] + d_ps)
                        r_ps.wrote(pnrm)
                        r_rr.read(i_rr, pnrm)
                        i_y, d_y = r_yT.acquire()
                        nblk = wtot // 128
                        pv_last = None
                        for j in range(nblk):
                            if j < wA // 128:
                                gk = 4 * (w - 1) + r0 // 128 + j
                            else:
                                gk = 4 * w + (j - wA // 128)
                            i_pt, d_pt = r_pt.acquire()
                            tr = ph.t(lambda e, j=j, ip=i_pt: e.transpose(
                                ptt[ip][:], psb[:, j * 128:(j + 1) * 128], idb[:]),
                                after=[pnrm] + cl + d_pt)
                            r_pt.wrote(tr)
                            r_ps.read(i_ps, tr)
                            i_dT, d_dT = r_dT.acquire()
                            evd = ph.v(lambda e, it=i_dT, ip=i_pt: e.tensor_copy(
                                dTt[:, it * 128:(it + 1) * 128], ptt[ip][:]),
                                after=[tr] + d_dT)
                            r_pt.read(i_pt, evd)
                            r_dT.wrote(evd)
                            pv = ph.t(lambda e, gk=gk, it=i_dT, j=j, nblk=nblk: e.matmul(
                                yT[:], vtb[:, gk * 64:(gk + 1) * 64],
                                dTt[:, it * 128:(it + 1) * 128],
                                start=(j == 0), stop=(j == nblk - 1)),
                                after=[evd, lv] + d_y)
                            r_dT.read(i_dT, pv)
                            r_v.read(i_v, pv)
                            pv_last = pv
                        r_yT.wrote(pv_last)
                        i_yb, d_yb = r_yb.acquire()
                        ey = ph.s(lambda e: e.copy(yb[:], yT[:]), after=[pv_last] + d_yb)
                        r_yT.read(i_y, ey)
                        r_yb.wrote(ey)
                        sty = ph.dma(lambda e, b=b, h=h, wq0=wq0: e.dma_start(
                            out=fo[b, h * 64:(h + 1) * 64, wq0:wq0 + 128], in_=yb[:]),
                            after=[ey])
                        r_yb.read(i_yb, sty)
        ph.finalize()


def _p4_s3(mk, l, src_name):
    nc = mk.nc
    src = mk.buf(src_name)
    fo, xfi, fhn = mk.buf("fo"), mk.buf("xfi"), mk.buf("fhn")
    wo_h = mk.buf("L0_ffn_wo")
    lg_h, lb_h = mk.buf("L0_ffn_ln_g"), mk.buf("L0_ffn_ln_b")
    ph = Phase(mk, f"p4l{l}s3")
    with contextlib.ExitStack() as _st:
        wo = _st.enter_context(nc.sbuf_tensor(f"s3_wo_{l}", [128, 4 * D], BF16))
        onc = _st.enter_context(nc.sbuf_tensor(f"s3_onc_{l}", [128, 1], F32))
        onr = _st.enter_context(nc.sbuf_tensor(f"s3_onr_{l}", [1, 128], F32))
        lgc = _st.enter_context(nc.sbuf_tensor(f"s3_lg_{l}", [128, 2 * NC], F32))
        ob = _st.enter_context(nc.sbuf_tensor(f"s3_ob_{l}", [128, 4 * TT], BF16))
        xb = _st.enter_context(nc.sbuf_tensor(f"s3_xb_{l}", [128, 2 * TT], F32))
        rb = _st.enter_context(nc.sbuf_tensor(f"s3_rb_{l}", [128, 2 * TT], F32))
        hb = _st.enter_context(nc.sbuf_tensor(f"s3_hb_{l}", [128, NC * (TT + 1)], F32))
        sq = _st.enter_context(nc.sbuf_tensor(f"s3_sq_{l}", [128, TT], F32))
        rows = _st.enter_context(nc.sbuf_tensor(f"s3_rows_{l}", [1, 8 * TT], F32))
        hw = _st.enter_context(nc.sbuf_tensor(f"s3_hw_{l}", [128, 2 * TT], F32))
        hnb = _st.enter_context(nc.sbuf_tensor(f"s3_hnb_{l}", [128, TT], BF16))
        pj0 = _st.enter_context(nc.psum_tensor(f"s3_p0_{l}", [128, TT], F32))
        pj1 = _st.enter_context(nc.psum_tensor(f"s3_p1_{l}", [128, TT], F32))
        ps_sum = _st.enter_context(nc.psum_tensor(f"s3_ps_{l}", [1, TT], F32))
        ps_sq = _st.enter_context(nc.psum_tensor(f"s3_pq_{l}", [1, TT], F32))
        pb0 = _st.enter_context(nc.psum_tensor(f"s3_pb0_{l}", [128, TT], F32))
        pb1 = _st.enter_context(nc.psum_tensor(f"s3_pb1_{l}", [128, TT], F32))
        pj = [pj0, pj1]
        cl = []
        for kc in range(4):
            cl.append(ph.dma(lambda e, kc=kc: e.dma_start(
                out=wo[:, kc * D:(kc + 1) * D], in_=wo_h[l, ts(kc, 128), :]), q="gpsimd"))
        cl.append(ph.dma(lambda e: e.dma_start(out=onc[:], in_=mk.buf("c_ones_col")[:])))
        cl.append(ph.dma(lambda e: e.dma_start(out=onr[:], in_=mk.buf("c_ones_row")[:])))
        for c in range(NC):
            cl.append(ph.dma(lambda e, c=c: e.dma_start(
                out=lgc[:, c:c + 1], in_=lg_h[l, ts(c, 128)].rearrange("(p o) -> p o", o=1)),
                q="gpsimd"))
            cl.append(ph.dma(lambda e, c=c: e.dma_start(
                out=lgc[:, NC + c:NC + c + 1], in_=lb_h[l, ts(c, 128)].rearrange("(p o) -> p o", o=1)),
                q="gpsimd"))
        r_o, r_x, r_rb, r_pj = Ring(1), Ring(2), Ring(2), Ring(2)
        for b in range(B):
            for tt in range(NTT):
                i_o, d_o = r_o.acquire()
                oloads = []
                for c in range(4):
                    oloads.append(ph.dma(lambda e, b=b, tt=tt, c=c: e.dma_start(
                        out=ob[:, c * TT:(c + 1) * TT],
                        in_=fo[b, ts(c, 128), ts(tt, TT)]), after=d_o))
                for oc in range(NC):
                    i_x, d_x = r_x.acquire()
                    lx = ph.dma(lambda e, b=b, tt=tt, oc=oc, i=i_x: e.dma_start(
                        out=xb[:, i * TT:(i + 1) * TT],
                        in_=src[b, ts(oc, 128), ts(tt, TT)]), after=d_x, q="gpsimd")
                    r_x.wrote(lx)
                    i_pj, d_pj = r_pj.acquire()
                    mt = []
                    for kc in range(4):
                        mt.append(ph.t(lambda e, kc=kc, oc=oc, i=i_pj: e.matmul(
                            pj[i][:], wo[:, kc * D + oc * 128: kc * D + (oc + 1) * 128],
                            ob[:, kc * TT:(kc + 1) * TT],
                            start=(kc == 0), stop=(kc == 3)),
                            after=([oloads[kc]] + cl + d_pj) if kc == 0 else [oloads[kc]]))
                    mm = mt[-1]
                    r_pj.wrote(mm)
                    r_o.read(i_o, mm)
                    i_rb, d_rb = r_rb.acquire()
                    ad = ph.v(lambda e, i=i_pj, ir=i_rb, ix=i_x: e.tensor_add(
                        rb[:, ir * TT:(ir + 1) * TT], pj[i][:], xb[:, ix * TT:(ix + 1) * TT]),
                        after=[mm, lx] + d_rb)
                    r_pj.read(i_pj, ad)
                    r_x.read(i_x, ad)
                    r_rb.wrote(ad)
                    st = ph.dma(lambda e, b=b, tt=tt, oc=oc, ir=i_rb: e.dma_start(
                        out=xfi[b, ts(oc, 128), ts(tt, TT)], in_=rb[:, ir * TT:(ir + 1) * TT]),
                        after=[ad])
                    r_rb.read(i_rb, st)
        ph.finalize()
        ph = Phase(mk, f"p4l{l}s3b")
        cl = []
        cl.append(ph.dma(lambda e: e.dma_start(out=onc[:], in_=mk.buf("c_ones_col")[:])))
        cl.append(ph.dma(lambda e: e.dma_start(out=onr[:], in_=mk.buf("c_ones_row")[:])))
        for c in range(NC):
            cl.append(ph.dma(lambda e, c=c: e.dma_start(
                out=lgc[:, c:c + 1], in_=lg_h[l, ts(c, 128)].rearrange("(p o) -> p o", o=1)),
                q="gpsimd"))
            cl.append(ph.dma(lambda e, c=c: e.dma_start(
                out=lgc[:, NC + c:NC + c + 1], in_=lb_h[l, ts(c, 128)].rearrange("(p o) -> p o", o=1)),
                q="gpsimd"))
        r_hb, r_sq, r_st, r_pss, r_pb, r_hw, r_hn = Ring(1), Ring(1), Ring(1), Ring(1), Ring(1), Ring(1), Ring(1)
        HTT = TT + 1
        for b in range(B):
            for tt in range(NTT):
                i_hb, d_hb = r_hb.acquire()
                hloads = []
                if tt == 0:
                    mz = ph.g(lambda e: e.memset(hb[:, 0:NC * HTT:HTT], 0.0), after=d_hb)
                    for c in range(NC):
                        hloads.append(ph.dma(lambda e, b=b, c=c: e.dma_start(
                            out=hb[:, c * HTT + 1: c * HTT + HTT],
                            in_=xfi[b, ts(c, 128), 0:TT]), after=d_hb + [mz], q="gpsimd"))
                else:
                    for c in range(NC):
                        hloads.append(ph.dma(lambda e, b=b, tt=tt, c=c: e.dma_start(
                            out=hb[:, c * HTT: c * HTT + HTT],
                            in_=xfi[b, ts(c, 128), tt * TT - 1: tt * TT + TT]),
                            after=d_hb, q="gpsimd"))
                vslice = lambda c: hb[:, c * HTT + 1: c * HTT + 1 + TT] if c < 4 else \
                    hb[:, c * HTT: c * HTT + TT]
                st_toks = []
                i_ps, d_ps = r_pss.acquire()
                for c in range(NC):
                    i_sq, d_sq = r_sq.acquire()
                    tsq = ph.s(lambda e, c=c: e.activation(sq[:], vslice(c), AF.Square),
                               after=[hloads[c]] + d_sq)
                    r_sq.wrote(tsq)
                    m1 = ph.t(lambda e, c=c: e.matmul(
                        ps_sum[:], onc[:], vslice(c), start=(c == 0), stop=(c == NC - 1)),
                        after=[hloads[c]] + cl + d_ps)
                    m2 = ph.t(lambda e, c=c: e.matmul(
                        ps_sq[:], onc[:], sq[:], start=(c == 0), stop=(c == NC - 1)),
                        after=[tsq] + d_ps)
                    r_sq.read(i_sq, m2)
                    r_hb.read(i_hb, m1)
                    st_toks += [m1, m2]
                R = lambda k: rows[:, k * TT:(k + 1) * TT]
                i_st, d_st = r_st.acquire()
                v1 = ph.v(lambda e: e.tensor_scalar_mul(R(0), ps_sum[:], -1.0 / D), after=st_toks + d_st)
                s1 = ph.s(lambda e: e.activation(R(1), ps_sum[:], AF.Square, scale=1.0 / D),
                          after=st_toks + d_st + [v1])
                v2 = ph.v(lambda e: e.tensor_scalar(R(2), ps_sq[:], 1.0 / D, LN_EPS, ALU.mult, ALU.add),
                          after=st_toks + d_st)
                v3 = ph.v(lambda e: e.tensor_sub(R(2), R(2), R(1)), after=[v2, s1])
                s2 = ph.s(lambda e: e.activation(R(3), R(2), AF.Sqrt), after=[v3])
                v4 = ph.v(lambda e: e.reciprocal(R(3), R(3)), after=[s2])
                v5 = ph.v(lambda e: e.tensor_mul(R(4), R(0), R(3)), after=[v1, v4])
                r_pss.wrote(v5)
                for tok in (v1, s1, v2):
                    r_pss.read(i_ps, tok)
                i_pb, d_pb = r_pb.acquire()
                bc1 = ph.t(lambda e: e.matmul(pb0[:], onr[:], R(3), start=True, stop=True),
                           after=[v4, v5, cl[-1]] + cl + d_pb)
                bc2 = ph.t(lambda e: e.matmul(pb1[:], onr[:], R(4), start=True, stop=True),
                           after=[v5] + cl + d_pb)
                r_pb.wrote(bc2)
                r_st.wrote(v5)
                r_st.read(i_st, bc1)
                r_st.read(i_st, bc2)
                for c in range(NC):
                    i_hw, d_hw = r_hw.acquire()
                    HW = lambda k: hw[:, k * TT:(k + 1) * TT]
                    t1 = ph.v(lambda e, c=c: e.tensor_mul(HW(0), vslice(c), pb0[:]),
                              after=[bc1, hloads[c]] + d_hw)
                    t2 = ph.v(lambda e: e.tensor_add(HW(1), HW(0), pb1[:]), after=[t1, bc2])
                    r_pb.read(i_pb, t1)
                    r_pb.read(i_pb, t2)
                    r_hb.read(i_hb, t1)
                    i_hn, d_hn = r_hn.acquire()
                    hn = ph.v(lambda e, c=c: e.tensor_scalar(
                        hnb[:], HW(1), lgc[:, c:c + 1], lgc[:, NC + c:NC + c + 1],
                        ALU.mult, ALU.add), after=[t2] + cl + d_hn)
                    r_hw.wrote(hn)
                    sth = ph.dma(lambda e, b=b, tt=tt, c=c: e.dma_start(
                        out=fhn[b, ts(c, 128), ts(tt, TT)], in_=hnb[:]), after=[hn])
                    r_hn.wrote(sth)
                r_hb.wrote(hloads[-1])
        ph.finalize()


def _p4_s4(mk, l, dst_name):
    nc = mk.nc
    fhn, xfi = mk.buf("fhn"), mk.buf("xfi")
    dst = mk.buf(dst_name)
    last = (dst_name == "x1b")
    x1 = mk.buf("x1") if last else None
    w1_h, w2_h = mk.buf("L0_ffn_w1"), mk.buf("L0_ffn_w2")
    b1_h, b2_h = mk.buf("L0_ffn_b1"), mk.buf("L0_ffn_b2")
    NJ = 22
    HID = 2730
    ph = Phase(mk, f"p4l{l}s4")
    with contextlib.ExitStack() as _st:
        w1 = _st.enter_context(nc.sbuf_tensor(f"s4_w1_{l}", [128, NC * 5460], BF16))
        w2 = _st.enter_context(nc.sbuf_tensor(f"s4_w2_{l}", [128, NJ * D], BF16))
        b1c = _st.enter_context(nc.sbuf_tensor(f"s4_b1_{l}", [128, 2 * NJ], F32))
        b2c = _st.enter_context(nc.sbuf_tensor(f"s4_b2_{l}", [128, NC], F32))
        hnb = _st.enter_context(nc.sbuf_tensor(f"s4_hn_{l}", [128, 2 * NC * TT], BF16))
        hsb = _st.enter_context(nc.sbuf_tensor(f"s4_h_{l}", [128, NJ * TT], BF16))
        gg = _st.enter_context(nc.sbuf_tensor(f"s4_gg_{l}", [128, TT], F32))
        xb = _st.enter_context(nc.sbuf_tensor(f"s4_xb_{l}", [128, 2 * TT], F32))
        x1b_ = _st.enter_context(nc.sbuf_tensor(f"s4_x1b_{l}", [128, 2 * TT], F32))
        rb = _st.enter_context(nc.sbuf_tensor(f"s4_rb_{l}", [128, 2 * TT], F32))
        pA = _st.enter_context(nc.psum_tensor(f"s4_pA_{l}", [128, TT], F32))
        pB = _st.enter_context(nc.psum_tensor(f"s4_pB_{l}", [128, TT], F32))
        pj0 = _st.enter_context(nc.psum_tensor(f"s4_p0_{l}", [128, TT], F32))
        pj1 = _st.enter_context(nc.psum_tensor(f"s4_p1_{l}", [128, TT], F32))
        pj = [pj0, pj1]
        cl = []
        for kc in range(NC):
            cl.append(ph.dma(lambda e, kc=kc: e.dma_start(
                out=w1[:, kc * 5460:(kc + 1) * 5460], in_=w1_h[l, ts(kc, 128), :]),
                q="gpsimd"))
        for j in range(NJ):
            wj = 128 if j < NJ - 1 else HID - 128 * (NJ - 1)
            cl.append(ph.dma(lambda e, j=j, wj=wj: e.dma_start(
                out=w2[0:wj, j * D:(j + 1) * D], in_=w2_h[l, j * 128: j * 128 + wj, :]),
                q="gpsimd"))
            cl.append(ph.dma(lambda e, j=j, wj=wj: e.dma_start(
                out=b1c[0:wj, j:j + 1],
                in_=b1_h[l, j * 128: j * 128 + wj].rearrange("(p o) -> p o", o=1)),
                q="gpsimd"))
            cl.append(ph.dma(lambda e, j=j, wj=wj: e.dma_start(
                out=b1c[0:wj, NJ + j:NJ + j + 1],
                in_=b1_h[l, HID + j * 128: HID + j * 128 + wj].rearrange("(p o) -> p o", o=1)),
                q="gpsimd"))
        for c in range(NC):
            cl.append(ph.dma(lambda e, c=c: e.dma_start(
                out=b2c[:, c:c + 1], in_=b2_h[l, ts(c, 128)].rearrange("(p o) -> p o", o=1)),
                q="gpsimd"))
        r_hn, r_h, r_gg = Ring(2), Ring(1), Ring(1)
        r_pA, r_pB, r_pj = Ring(1), Ring(1), Ring(2)
        r_x, r_x1, r_rb = Ring(2), Ring(2), Ring(2)
        for b in range(B):
            for tt in range(NTT):
                i_hn, d_hn = r_hn.acquire()
                hloads = []
                for c in range(NC):
                    hloads.append(ph.dma(lambda e, b=b, tt=tt, c=c, i=i_hn: e.dma_start(
                        out=hnb[:, i * NC * TT + c * TT: i * NC * TT + (c + 1) * TT],
                        in_=fhn[b, ts(c, 128), ts(tt, TT)]), after=d_hn))
                r_hn.wrote(hloads[-1])
                h_toks = {}
                for j in range(NJ):
                    wj = 128 if j < NJ - 1 else HID - 128 * (NJ - 1)
                    i_pa, d_pa = r_pA.acquire()
                    ma_t = []
                    for kc in range(NC):
                        ma_t.append(ph.t(lambda e, kc=kc, j=j, wj=wj, i=i_hn: e.matmul(
                            pA[0:wj, :], w1[0:128, kc * 5460 + j * 128: kc * 5460 + j * 128 + wj],
                            hnb[:, i * NC * TT + kc * TT: i * NC * TT + (kc + 1) * TT],
                            start=(kc == 0), stop=(kc == NC - 1)),
                            after=([hloads[kc]] + cl + d_pa) if kc == 0 else [hloads[kc]]))
                    i_pb, d_pb = r_pB.acquire()
                    mb_t = []
                    for kc in range(NC):
                        mb_t.append(ph.t(lambda e, kc=kc, j=j, wj=wj, i=i_hn: e.matmul(
                            pB[0:wj, :],
                            w1[0:128, kc * 5460 + HID + j * 128: kc * 5460 + HID + j * 128 + wj],
                            hnb[:, i * NC * TT + kc * TT: i * NC * TT + (kc + 1) * TT],
                            start=(kc == 0), stop=(kc == NC - 1)),
                            after=([hloads[kc]] + cl + d_pb) if kc == 0 else [hloads[kc]]))
                    r_pA.wrote(ma_t[-1])
                    r_pB.wrote(mb_t[-1])
                    r_hn.read(i_hn, ma_t[-1])
                    r_hn.read(i_hn, mb_t[-1])
                    i_gg, d_gg = r_gg.acquire()
                    ge = ph.s(lambda e, j=j, wj=wj: e.activation(
                        gg[0:wj, :], pB[0:wj, :], AF.Gelu, bias=b1c[0:wj, NJ + j:NJ + j + 1]),
                        after=[mb_t[-1]] + cl + d_gg)
                    r_pB.read(i_pb, ge)
                    r_gg.wrote(ge)
                    i_h, d_h = r_h.acquire()
                    hv = ph.v(lambda e, j=j, wj=wj: e.scalar_tensor_tensor(
                        hsb[0:wj, j * TT:(j + 1) * TT], pA[0:wj, :], b1c[0:wj, j:j + 1],
                        gg[0:wj, :], ALU.add, ALU.mult),
                        after=[ma_t[-1], ge] + (h_toks.get(j) or []))
                    r_pA.read(i_pa, hv)
                    r_gg.read(i_gg, hv)
                    h_toks[j] = [hv]
                for oc in range(NC):
                    i_x, d_x = r_x.acquire()
                    lx = ph.dma(lambda e, b=b, tt=tt, oc=oc, i=i_x: e.dma_start(
                        out=xb[:, i * TT:(i + 1) * TT], in_=xfi[b, ts(oc, 128), ts(tt, TT)]),
                        after=d_x, q="gpsimd")
                    r_x.wrote(lx)
                    if last:
                        i_x1, d_x1 = r_x1.acquire()
                        lx1 = ph.dma(lambda e, b=b, tt=tt, oc=oc, i=i_x1: e.dma_start(
                            out=x1b_[:, i * TT:(i + 1) * TT], in_=x1[b, ts(oc, 128), ts(tt, TT)]),
                            after=d_x1, q="gpsimd")
                        r_x1.wrote(lx1)
                    i_pj, d_pj = r_pj.acquire()
                    mt = []
                    for j in range(NJ):
                        wj = 128 if j < NJ - 1 else HID - 128 * (NJ - 1)
                        mt.append(ph.t(lambda e, j=j, oc=oc, wj=wj, i=i_pj: e.matmul(
                            pj[i][:], w2[0:wj, j * D + oc * 128: j * D + (oc + 1) * 128],
                            hsb[0:wj, j * TT:(j + 1) * TT],
                            start=(j == 0), stop=(j == NJ - 1)),
                            after=(h_toks[j] + cl + d_pj) if j == 0 else h_toks[j]))
                    mm = mt[-1]
                    r_pj.wrote(mm)
                    i_rb, d_rb = r_rb.acquire()
                    ad = ph.v(lambda e, i=i_pj, oc=oc, ir=i_rb, ix=i_x: e.scalar_tensor_tensor(
                        rb[:, ir * TT:(ir + 1) * TT], pj[i][:], b2c[:, oc:oc + 1],
                        xb[:, ix * TT:(ix + 1) * TT], ALU.add, ALU.add),
                        after=[mm, lx] + cl + d_rb)
                    r_pj.read(i_pj, ad)
                    r_x.read(i_x, ad)
                    if last:
                        ad = ph.v(lambda e, ir=i_rb, ix1=i_x1: e.tensor_add(
                            rb[:, ir * TT:(ir + 1) * TT], rb[:, ir * TT:(ir + 1) * TT],
                            x1b_[:, ix1 * TT:(ix1 + 1) * TT]), after=[ad, lx1])
                        r_x1.read(i_x1, ad)
                    r_rb.wrote(ad)
                    st = ph.dma(lambda e, b=b, tt=tt, oc=oc, ir=i_rb: e.dma_start(
                        out=dst[b, ts(oc, 128), ts(tt, TT)], in_=rb[:, ir * TT:(ir + 1) * TT]),
                        after=[ad])
                    r_rb.read(i_rb, st)
                for j in range(NJ):
                    h_toks[j] = h_toks[j] + [mm]
        ph.finalize()


def emit_p4(mk: MK):
    for l in range(2):
        src = "xf" if l == 0 else "xfi"
        _p4_s1(mk, l, src)
        _p4_s2(mk, l)
        _p4_s3(mk, l, src)
        _p4_s4(mk, l, "xfi" if l == 0 else "x1b")



def emit_p5(mk: MK):
    nc = mk.nc
    x1b, xn1 = mk.buf("x1b"), mk.buf("xn1")
    stats1 = mk.buf("stats1")
    ph = Phase(mk, "p5")
    with contextlib.ExitStack() as _st:
        onc = _st.enter_context(nc.sbuf_tensor("p5_onc", [128, 1], F32))
        onr = _st.enter_context(nc.sbuf_tensor("p5_onr", [1, 128], F32))
        xb = _st.enter_context(nc.sbuf_tensor("p5_xb", [128, 2 * NC * TT], F32))
        sq = _st.enter_context(nc.sbuf_tensor("p5_sq", [128, TT], F32))
        rows = _st.enter_context(nc.sbuf_tensor("p5_rows", [1, 8 * TT], F32))
        xnt = _st.enter_context(nc.sbuf_tensor("p5_xn", [128, 2 * TT], F32))
        xnb = _st.enter_context(nc.sbuf_tensor("p5_xnb", [128, TT], BF16))
        ps_sum = _st.enter_context(nc.psum_tensor("p5_ps", [1, TT], F32))
        ps_sq = _st.enter_context(nc.psum_tensor("p5_pq", [1, TT], F32))
        pb0 = _st.enter_context(nc.psum_tensor("p5_pb0", [128, TT], F32))
        pb1 = _st.enter_context(nc.psum_tensor("p5_pb1", [128, TT], F32))
        cl = []
        cl.append(ph.dma(lambda e: e.dma_start(out=onc[:], in_=mk.buf("c_ones_col")[:])))
        cl.append(ph.dma(lambda e: e.dma_start(out=onr[:], in_=mk.buf("c_ones_row")[:])))
        r_x, r_sq, r_st, r_ps, r_pb, r_xn, r_xnb = Ring(2), Ring(1), Ring(1), Ring(1), Ring(1), Ring(1), Ring(1)
        for b in range(B):
            for tt in range(NTT):
                i_x, d_x = r_x.acquire()
                xl = []
                for c in range(NC):
                    xl.append(ph.dma(lambda e, b=b, tt=tt, c=c, i=i_x: e.dma_start(
                        out=xb[:, i * NC * TT + c * TT: i * NC * TT + (c + 1) * TT],
                        in_=x1b[b, ts(c, 128), ts(tt, TT)]), after=d_x))
                r_x.wrote(xl[-1])
                st_toks = []
                i_ps, d_ps = r_ps.acquire()
                XS = lambda c, i: xb[:, i * NC * TT + c * TT: i * NC * TT + (c + 1) * TT]
                for c in range(NC):
                    i_sq, d_sq = r_sq.acquire()
                    tsq = ph.s(lambda e, c=c, i=i_x: e.activation(sq[:], XS(c, i), AF.Square),
                               after=[xl[c]] + d_sq)
                    r_sq.wrote(tsq)
                    m1 = ph.t(lambda e, c=c, i=i_x: e.matmul(
                        ps_sum[:], onc[:], XS(c, i), start=(c == 0), stop=(c == NC - 1)),
                        after=[xl[c]] + cl + d_ps)
                    m2 = ph.t(lambda e, c=c: e.matmul(
                        ps_sq[:], onc[:], sq[:], start=(c == 0), stop=(c == NC - 1)),
                        after=[tsq] + d_ps)
                    r_sq.read(i_sq, m2)
                    r_x.read(i_x, m1)
                    st_toks += [m1, m2]
                R = lambda k: rows[:, k * TT:(k + 1) * TT]
                i_st, d_st = r_st.acquire()
                v1 = ph.v(lambda e: e.tensor_scalar_mul(R(0), ps_sum[:], -1.0 / D), after=st_toks + d_st)
                s1 = ph.s(lambda e: e.activation(R(1), ps_sum[:], AF.Square, scale=1.0 / D),
                          after=st_toks + d_st + [v1])
                v2 = ph.v(lambda e: e.tensor_scalar(R(2), ps_sq[:], 1.0 / D, LN_EPS, ALU.mult, ALU.add),
                          after=st_toks + d_st)
                v3 = ph.v(lambda e: e.tensor_sub(R(2), R(2), R(1)), after=[v2, s1])
                s2 = ph.s(lambda e: e.activation(R(3), R(2), AF.Sqrt), after=[v3])
                v4 = ph.v(lambda e: e.reciprocal(R(3), R(3)), after=[s2])
                v5 = ph.v(lambda e: e.tensor_mul(R(4), R(0), R(3)), after=[v1, v4])
                r_ps.wrote(v5)
                for tok in (v1, s1, v2):
                    r_ps.read(i_ps, tok)
                sd1 = ph.dma(lambda e, b=b, tt=tt: e.dma_start(
                    out=stats1[b, 0:1, ts(tt, TT)], in_=R(4)), after=[v5], q="gpsimd")
                sd2 = ph.dma(lambda e, b=b, tt=tt: e.dma_start(
                    out=stats1[b, 1:2, ts(tt, TT)], in_=R(3)), after=[v4, v5], q="gpsimd")
                r_st.read(i_st, sd1)
                r_st.read(i_st, sd2)
                i_pb, d_pb = r_pb.acquire()
                bc1 = ph.t(lambda e: e.matmul(pb0[:], onr[:], R(3), start=True, stop=True),
                           after=[v4, v5] + cl + d_pb)
                bc2 = ph.t(lambda e: e.matmul(pb1[:], onr[:], R(4), start=True, stop=True),
                           after=[v5] + cl + d_pb)
                r_pb.wrote(bc2)
                r_st.wrote(v5)
                r_st.read(i_st, bc1)
                r_st.read(i_st, bc2)
                for c in range(NC):
                    i_xn, d_xn = r_xn.acquire()
                    x1_ = ph.v(lambda e, c=c, i=i_x: e.tensor_mul(xnt[:, 0:TT], XS(c, i), pb0[:]),
                               after=[bc1, xl[c]] + d_xn)
                    i_xb, d_xb = r_xnb.acquire()
                    x2_ = ph.v(lambda e: e.tensor_add(xnb[:], xnt[:, 0:TT], pb1[:]),
                               after=[x1_, bc2] + d_xb)
                    r_xn.wrote(x2_)
                    r_x.read(i_x, x1_)
                    r_pb.read(i_pb, x1_)
                    r_pb.read(i_pb, x2_)
                    st2 = ph.dma(lambda e, b=b, tt=tt, c=c: e.dma_start(
                        out=xn1[b, ts(c, 128), ts(tt, TT)], in_=xnb[:]),
                        after=[x2_], q="gpsimd")
                    r_xnb.wrote(st2)
        ph.finalize()



def emit_p6(mk: MK):
    nc = mk.nc
    xn1 = mk.buf("xn1")
    q1, k1, v1t = mk.buf("q1"), mk.buf("k1"), mk.buf("v1t")
    qn1, kn1, amem = mk.buf("qn1"), mk.buf("kn1"), mk.buf("amem")
    wq_h, wk_h, wv_h = mk.buf("L1_attn_Wq"), mk.buf("L1_attn_Wk"), mk.buf("L1_attn_Wv")
    mem_h, mn_h = mk.buf("L1_attn_memory"), mk.buf("L1_attn_memory_norm")
    ph = Phase(mk, "p6")
    with contextlib.ExitStack() as _st:
        wq = _st.enter_context(nc.sbuf_tensor("p6_wq", [128, NC * D], BF16))
        wk = _st.enter_context(nc.sbuf_tensor("p6_wk", [128, NC * 512], BF16))
        wv = _st.enter_context(nc.sbuf_tensor("p6_wv", [128, NC * 512], BF16))
        mem = _st.enter_context(nc.sbuf_tensor("p6_mem", [128, 4 * 512], BF16))
        mn = _st.enter_context(nc.sbuf_tensor("p6_mn", [128, 4], BF16))
        seg = _st.enter_context(nc.sbuf_tensor("p6_seg", [128, 2], F32))
        onr = _st.enter_context(nc.sbuf_tensor("p6_onr", [1, 128], F32))
        idb = _st.enter_context(nc.sbuf_tensor("p6_idb", [128, 128], BF16))
        xnb = _st.enter_context(nc.sbuf_tensor("p6_xnb", [128, 2 * NC * TT], BF16))
        qf = _st.enter_context(nc.sbuf_tensor("p6_qf", [128, TT], F32))
        qbf = _st.enter_context(nc.sbuf_tensor("p6_qbf", [128, TT], BF16))
        sqt = _st.enter_context(nc.sbuf_tensor("p6_sq", [128, TT], F32))
        nrows = _st.enter_context(nc.sbuf_tensor("p6_nr", [2, TT], F32))
        sig = _st.enter_context(nc.sbuf_tensor("p6_sig", [128, NC * TT], BF16))
        el = _st.enter_context(nc.sbuf_tensor("p6_el", [128, 2 * TT], F32))
        drow = _st.enter_context(nc.sbuf_tensor("p6_dr", [1, TT], F32))
        rdb = _st.enter_context(nc.sbuf_tensor("p6_rdb", [128, TT], F32))
        amc = _st.enter_context(nc.sbuf_tensor("p6_am", [128, TT], BF16))
        vb = _st.enter_context(nc.sbuf_tensor("p6_vb", [128, TT], BF16))
        vt = _st.enter_context(nc.sbuf_tensor("p6_vt", [128, 2 * 128], BF16))
        pj0 = _st.enter_context(nc.psum_tensor("p6_p0", [128, TT], F32))
        pj1 = _st.enter_context(nc.psum_tensor("p6_p1", [128, TT], F32))
        pn = _st.enter_context(nc.psum_tensor("p6_pn", [2, TT], F32))
        pd = _st.enter_context(nc.psum_tensor("p6_pd", [1, TT], F32))
        pnum = _st.enter_context(nc.psum_tensor("p6_pnum", [128, TT], F32))
        pbd = _st.enter_context(nc.psum_tensor("p6_pbd", [128, TT], F32))
        pt0 = _st.enter_context(nc.psum_tensor("p6_pt0", [128, 128], BF16))
        pt1 = _st.enter_context(nc.psum_tensor("p6_pt1", [128, 128], BF16))
        pj = [pj0, pj1]
        ptt = [pt0, pt1]
        cl = []
        for kc in range(NC):
            cl.append(ph.dma(lambda e, kc=kc: e.dma_start(
                out=wq[:, kc * D:(kc + 1) * D], in_=wq_h[ts(kc, 128), :]), q="gpsimd"))
            cl.append(ph.dma(lambda e, kc=kc: e.dma_start(
                out=wk[:, kc * 512:(kc + 1) * 512], in_=wk_h[ts(kc, 128), :]), q="gpsimd"))
            cl.append(ph.dma(lambda e, kc=kc: e.dma_start(
                out=wv[:, kc * 512:(kc + 1) * 512], in_=wv_h[ts(kc, 128), :]), q="gpsimd"))
        for kc in range(4):
            cl.append(ph.dma(lambda e, kc=kc: e.dma_start(
                out=mem[:, kc * 512:(kc + 1) * 512], in_=mem_h[ts(kc, 128), :]), q="gpsimd"))
            cl.append(ph.dma(lambda e, kc=kc: e.dma_start(
                out=mn[:, kc:kc + 1], in_=mn_h[ts(kc, 128)].rearrange("(p o) -> p o", o=1)),
                q="gpsimd"))
        cl.append(ph.dma(lambda e: e.dma_start(out=seg[:], in_=mk.buf("c_seg64")[:])))
        cl.append(ph.dma(lambda e: e.dma_start(out=onr[:], in_=mk.buf("c_ones_row")[:])))
        cl.append(ph.dma(lambda e: e.dma_start(out=idb[:], in_=mk.buf("c_ident_bf16")[:])))
        r_x = Ring(2)
        r_pj, r_qf, r_qbf, r_sq, r_nr, r_pn = Ring(2), Ring(1), Ring(1), Ring(1), Ring(1), Ring(1)
        r_sig, r_el = Ring(1), Ring(1)
        r_pd, r_pnum, r_pbd, r_dr, r_rdb, r_am = Ring(1), Ring(1), Ring(1), Ring(1), Ring(1), Ring(1)
        r_vb, r_pt, r_vt = Ring(1), Ring(2), Ring(2)

        def proj(oc, w_sb, outw, xl, i_x, d_pj, i_pj):
            mt = []
            for kc in range(NC):
                mt.append(ph.t(lambda e, kc=kc, oc=oc, i=i_pj, ix=i_x, w_sb=w_sb, outw=outw:
                               e.matmul(
                    pj[i][:], w_sb[:, kc * outw + oc * 128: kc * outw + (oc + 1) * 128],
                    xnb[:, ix * NC * TT + kc * TT: ix * NC * TT + (kc + 1) * TT],
                    start=(kc == 0), stop=(kc == NC - 1)),
                    after=([xl[kc]] + cl + d_pj) if kc == 0 else [xl[kc]]))
            return mt[-1]

        for b in range(B):
            for tt in range(NTT):
                i_x, d_x = r_x.acquire()
                xl = []
                for c in range(NC):
                    xl.append(ph.dma(lambda e, b=b, tt=tt, c=c, i=i_x: e.dma_start(
                        out=xnb[:, i * NC * TT + c * TT: i * NC * TT + (c + 1) * TT],
                        in_=xn1[b, ts(c, 128), ts(tt, TT)]), after=d_x))
                r_x.wrote(xl[-1])
                sig_toks = []
                for oc in range(NC):
                    i_pj, d_pj = r_pj.acquire()
                    mm = proj(oc, wq, D, xl, i_x, d_pj, i_pj)
                    r_pj.wrote(mm)
                    r_x.read(i_x, mm)
                    i_qf, d_qf = r_qf.acquire()
                    ev = ph.s(lambda e, i=i_pj: e.copy(qf[:], pj[i][:]), after=[mm] + d_qf)
                    r_pj.read(i_pj, ev)
                    r_qf.wrote(ev)
                    i_qb, d_qb = r_qbf.acquire()
                    cb = ph.v(lambda e: e.tensor_copy(qbf[:], qf[:]), after=[ev] + d_qb)
                    r_qf.read(i_qf, cb)
                    r_qbf.wrote(cb)
                    st = ph.dma(lambda e, b=b, tt=tt, oc=oc: e.dma_start(
                        out=q1[b, ts(oc, 128), ts(tt, TT)], in_=qbf[:]), after=[cb])
                    r_qbf.read(i_qb, st)
                    i_sq, d_sq = r_sq.acquire()
                    sk = ph.s(lambda e: e.activation(sqt[:], qbf[:], AF.Square), after=[cb] + d_sq)
                    r_qbf.read(i_qb, sk)
                    r_sq.wrote(sk)
                    i_pn, d_pn = r_pn.acquire()
                    nm1 = ph.t(lambda e: e.matmul(pn[:], seg[:], sqt[:], start=True, stop=True),
                               after=[sk] + cl + d_pn)
                    r_sq.read(i_sq, nm1)
                    r_pn.wrote(nm1)
                    i_nr, d_nr = r_nr.acquire()
                    s1_ = ph.s(lambda e: e.activation(nrows[:], pn[:], AF.Sqrt), after=[nm1] + d_nr)
                    r_pn.read(i_pn, s1_)
                    r_nr.wrote(s1_)
                    sn = ph.dma(lambda e, b=b, tt=tt, oc=oc: e.dma_start(
                        out=qn1[b, 2 * oc:2 * oc + 2, ts(tt, TT)], in_=nrows[:]),
                        after=[s1_], q="gpsimd")
                    r_nr.read(i_nr, sn)
                    EL = lambda k: el[:, k * TT:(k + 1) * TT]
                    i_el, d_el = r_el.acquire()
                    e1 = ph.v(lambda e: e.tensor_scalar_max(EL(0), qf[:], 0.0), after=[ev] + d_el)
                    e2 = ph.v(lambda e: e.tensor_scalar_min(EL(1), qf[:], 0.0), after=[ev] + d_el)
                    r_qf.read(i_qf, e1)
                    r_qf.read(i_qf, e2)
                    e3 = ph.s(lambda e: e.activation(EL(1), EL(1), AF.Exp), after=[e2])
                    i_sg, d_sg = r_sig.acquire()
                    e4 = ph.v(lambda e, oc=oc: e.scalar_tensor_tensor(
                        sig[:, oc * TT:(oc + 1) * TT], EL(1), -1.0, EL(0), ALU.add, ALU.add),
                        after=[e3, e1] + (d_sg if oc == 0 else []))
                    r_el.wrote(e4)
                    sig_toks.append(e4)
                for oc in range(4):
                    i_pj, d_pj = r_pj.acquire()
                    mm = proj(oc, wk, 512, xl, i_x, d_pj, i_pj)
                    r_pj.wrote(mm)
                    r_x.read(i_x, mm)
                    i_qb, d_qb = r_qbf.acquire()
                    ev = ph.s(lambda e, i=i_pj: e.copy(qbf[:], pj[i][:]), after=[mm] + d_qb)
                    r_pj.read(i_pj, ev)
                    r_qbf.wrote(ev)
                    st = ph.dma(lambda e, b=b, tt=tt, oc=oc: e.dma_start(
                        out=k1[b, ts(oc, 128), ts(tt, TT)], in_=qbf[:]), after=[ev])
                    r_qbf.read(i_qb, st)
                    i_sq, d_sq = r_sq.acquire()
                    sk = ph.s(lambda e: e.activation(sqt[:], qbf[:], AF.Square), after=[ev] + d_sq)
                    r_qbf.read(i_qb, sk)
                    r_sq.wrote(sk)
                    i_pn, d_pn = r_pn.acquire()
                    nm1 = ph.t(lambda e: e.matmul(pn[:], seg[:], sqt[:], start=True, stop=True),
                               after=[sk] + cl + d_pn)
                    r_sq.read(i_sq, nm1)
                    r_pn.wrote(nm1)
                    i_nr, d_nr = r_nr.acquire()
                    s1_ = ph.s(lambda e: e.activation(nrows[:], pn[:], AF.Sqrt), after=[nm1] + d_nr)
                    r_pn.read(i_pn, s1_)
                    r_nr.wrote(s1_)
                    sn = ph.dma(lambda e, b=b, tt=tt, oc=oc: e.dma_start(
                        out=kn1[b, 2 * oc:2 * oc + 2, ts(tt, TT)], in_=nrows[:]),
                        after=[s1_], q="gpsimd")
                    r_nr.read(i_nr, sn)
                for oc in range(4):
                    i_pj, d_pj = r_pj.acquire()
                    mm = proj(oc, wv, 512, xl, i_x, d_pj, i_pj)
                    r_pj.wrote(mm)
                    r_x.read(i_x, mm)
                    i_vb, d_vb = r_vb.acquire()
                    ev = ph.s(lambda e, i=i_pj: e.copy(vb[:], pj[i][:]), after=[mm] + d_vb)
                    r_pj.read(i_pj, ev)
                    r_vb.wrote(ev)
                    for sub in range(4):
                        i_pt, d_pt = r_pt.acquire()
                        tr = ph.t(lambda e, sub=sub, ip=i_pt: e.transpose(
                            ptt[ip][:], vb[:, sub * 128:(sub + 1) * 128], idb[:]),
                            after=[ev] + cl + d_pt)
                        r_pt.wrote(tr)
                        r_vb.read(i_vb, tr)
                        i_vt, d_vt = r_vt.acquire()
                        evt = ph.v(lambda e, iv=i_vt, ip=i_pt: e.tensor_copy(
                            vt[:, iv * 128:(iv + 1) * 128], ptt[ip][:]), after=[tr] + d_vt)
                        r_pt.read(i_pt, evt)
                        r_vt.wrote(evt)
                        for h2 in range(2):
                            hh = oc * 2 + h2
                            stv = ph.dma(lambda e, b=b, tt=tt, sub=sub, hh=hh, h2=h2, iv=i_vt:
                                         e.dma_start(
                                out=v1t[b, hh, tt * TT + sub * 128: tt * TT + (sub + 1) * 128, :],
                                in_=vt[:, iv * 128 + h2 * 64: iv * 128 + h2 * 64 + 64]),
                                after=[evt])
                            r_vt.read(i_vt, stv)
                for g in range(2):
                    i_pd, d_pd = r_pd.acquire()
                    dt_ = []
                    for kc in range(4):
                        dt_.append(ph.t(lambda e, kc=kc, g=g: e.matmul(
                            pd[:], mn[:, kc:kc + 1],
                            sig[:, (g * 4 + kc) * TT:(g * 4 + kc + 1) * TT],
                            start=(kc == 0), stop=(kc == 3)),
                            after=[sig_toks[g * 4 + kc]] + cl + (d_pd if kc == 0 else [])))
                    r_pd.wrote(dt_[-1])
                    i_dr, d_dr = r_dr.acquire()
                    ed = ph.v(lambda e: e.tensor_copy(drow[:], pd[:]), after=[dt_[-1]] + d_dr)
                    r_pd.read(i_pd, ed)
                    rc = ph.v(lambda e: e.reciprocal(drow[:], drow[:]), after=[ed])
                    r_dr.wrote(rc)
                    i_pbd, d_pbd = r_pbd.acquire()
                    bc = ph.t(lambda e: e.matmul(pbd[:], onr[:], drow[:], start=True, stop=True),
                              after=[rc] + cl + d_pbd)
                    r_dr.read(i_dr, bc)
                    r_pbd.wrote(bc)
                    i_rdb, d_rdb = r_rdb.acquire()
                    erd = ph.s(lambda e: e.copy(rdb[:], pbd[:]), after=[bc] + d_rdb)
                    r_pbd.read(i_pbd, erd)
                    r_rdb.wrote(erd)
                    for vc in range(4):
                        i_pnm, d_pnm = r_pnum.acquire()
                        nt = []
                        for kc in range(4):
                            nt.append(ph.t(lambda e, kc=kc, vc=vc, g=g: e.matmul(
                                pnum[:], mem[:, kc * 512 + vc * 128: kc * 512 + (vc + 1) * 128],
                                sig[:, (g * 4 + kc) * TT:(g * 4 + kc + 1) * TT],
                                start=(kc == 0), stop=(kc == 3)),
                                after=[sig_toks[g * 4 + kc]] + cl + (d_pnm if kc == 0 else [])))
                        r_pnum.wrote(nt[-1])
                        i_am, d_am = r_am.acquire()
                        am = ph.v(lambda e: e.tensor_mul(amc[:], pnum[:], rdb[:]),
                                  after=[nt[-1], erd] + d_am)
                        r_pnum.read(i_pnm, am)
                        r_rdb.read(i_rdb, am)
                        r_am.wrote(am)
                        sta = ph.dma(lambda e, b=b, tt=tt, g=g, vc=vc: e.dma_start(
                            out=amem[b, ts(g * 4 + vc, 128), ts(tt, TT)], in_=amc[:]),
                            after=[am])
                        r_am.read(i_am, sta)
                    for kc in range(4):
                        r_sig.read(0, dt_[-1])
        ph.finalize()



def emit_p7(mk: MK):
    nc = mk.nc
    q1, k1, v1t = mk.buf("q1"), mk.buf("k1"), mk.buf("v1t")
    amem, a1 = mk.buf("amem"), mk.buf("a1")
    mw_h = mk.buf("L1_attn_memory_weight")
    ph = Phase(mk, "p7")
    with contextlib.ExitStack() as _st:
        onr = _st.enter_context(nc.sbuf_tensor("p7_onr", [1, 128], F32))
        wrow = _st.enter_context(nc.sbuf_tensor("p7_wr", [1, 2], F32))
        wcol = _st.enter_context(nc.sbuf_tensor("p7_wc", [128, 2], F32))
        kh = _st.enter_context(nc.sbuf_tensor("p7_kh", [64, 2 * T], BF16))
        qsb2 = _st.enter_context(nc.sbuf_tensor("p7_q", [64, 2 * 2 * T], BF16))
        vtb = _st.enter_context(nc.sbuf_tensor("p7_vt", [128, 2, 16, 65], BF16))
        pT = _st.enter_context(nc.sbuf_tensor("p7_pT", [128, 3 * TT], BF16))
        ysb = _st.enter_context(nc.sbuf_tensor("p7_ys", [65, 2 * TT], F32))
        rz = _st.enter_context(nc.sbuf_tensor("p7_rz", [1, 2 * TT], F32))
        amt = _st.enter_context(nc.sbuf_tensor("p7_am", [64, 2 * TT], BF16))
        a1b = _st.enter_context(nc.sbuf_tensor("p7_a1", [64, 2 * TT], BF16))
        pS0 = _st.enter_context(nc.psum_tensor("p7_s0", [128, TT], F32))
        pS1 = _st.enter_context(nc.psum_tensor("p7_s1", [128, TT], F32))
        pS2 = _st.enter_context(nc.psum_tensor("p7_s2", [128, TT], F32))
        yT0 = _st.enter_context(nc.psum_tensor("p7_y0", [128, TT], F32))
        yT1 = _st.enter_context(nc.psum_tensor("p7_y1", [128, TT], F32))
        pmisc = _st.enter_context(nc.psum_tensor("p7_misc", [128, TT], F32))
        pS = [pS0, pS1, pS2]
        yTs = [yT0, yT1]
        cl = []
        cl.append(ph.dma(lambda e: e.dma_start(out=onr[:], in_=mk.buf("c_ones_row")[:])))
        cl.append(ph.dma(lambda e: e.dma_start(
            out=wrow[0:1, 0:1], in_=mw_h.rearrange("(o d) -> o d", o=1))))
        r_misc = Ring(1)
        sg = ph.s(lambda e: e.activation(wrow[0:1, 0:1], wrow[0:1, 0:1], AF.Sigmoid), after=cl)
        i_m, d_m = r_misc.acquire()
        blm = ph.t(lambda e: e.matmul(pmisc[0:128, 0:1], onr[:], wrow[0:1, 0:1],
                                      start=True, stop=True), after=[sg] + cl + d_m)
        r_misc.wrote(blm)
        wv1 = ph.v(lambda e: e.tensor_copy(wcol[:, 0:1], pmisc[0:128, 0:1]), after=[blm])
        r_misc.read(i_m, wv1)
        wv2 = ph.v(lambda e: e.tensor_scalar(wcol[:, 1:2], wcol[:, 0:1], -1.0, 1.0,
                                             ALU.mult, ALU.add), after=[wv1])
        msv = ph.v(lambda e: e.memset(vtb[:, :, :, 64:65], 1.0))
        r_kh, r_q, r_vt = Ring(2), Ring(2), Ring(2)
        r_pS, r_pT = Ring(3), Ring(3)
        r_yT = [Ring(1), Ring(1)]
        r_ys, r_rz, r_am, r_a1 = Ring(2), Ring(2), Ring(2), Ring(2)
        for b in range(B):
            for kvh in range(8):
                i_kh, d_kh = r_kh.acquire()
                lk = ph.dma(lambda e, b=b, kvh=kvh, i=i_kh: e.dma_start(
                    out=kh[:, i * T:(i + 1) * T],
                    in_=k1[b, kvh * 64:(kvh + 1) * 64, :]), after=d_kh)
                r_kh.wrote(lk)
                i_q, d_q = r_q.acquire()
                lqs = []
                for h2 in range(2):
                    lqs.append(ph.dma(lambda e, b=b, kvh=kvh, h2=h2, i=i_q: e.dma_start(
                        out=qsb2[:, (2 * i + h2) * T:(2 * i + h2 + 1) * T],
                        in_=q1[b, (2 * kvh + h2) * 64:(2 * kvh + h2 + 1) * 64, :]),
                        after=d_q))
                r_q.wrote(lqs[-1])
                i_vt, d_vt = r_vt.acquire()
                lv = ph.dma(lambda e, b=b, kvh=kvh, i=i_vt: e.dma_start(
                    out=vtb[:, i, :, 0:64],
                    in_=v1t[b, kvh].rearrange("(n p) d -> p n d", p=128)), after=d_vt)
                r_vt.wrote(lv)
                for qc in range(4):
                    for h2 in range(2):
                        i_y, d_y = r_yT[h2].acquire()
                        ex_t = [None] * 16
                        pt_i = [None] * 16
                        pv_last = None

                        def emit_sc(kt):
                            i_ps, d_ps = r_pS.acquire()
                            sc = ph.t(lambda e, kt=kt, qc=qc, h2=h2, ik=i_kh, iq=i_q,
                                      ip=i_ps: e.matmul(
                                pS[ip][:], kh[:, ik * T + kt * 128: ik * T + (kt + 1) * 128],
                                qsb2[:, (2 * iq + h2) * T + qc * TT:
                                     (2 * iq + h2) * T + (qc + 1) * TT],
                                start=True, stop=True), after=[lk, lqs[h2]] + d_ps)
                            r_pS.wrote(sc)
                            r_kh.read(i_kh, sc)
                            r_q.read(i_q, sc)
                            i_pt, d_pt = r_pT.acquire()
                            ex = ph.s(lambda e, ip=i_ps, it=i_pt: e.activation(
                                pT[:, it * TT:(it + 1) * TT], pS[ip][:], AF.Exp,
                                scale=0.125), after=[sc] + d_pt)
                            r_pS.read(i_ps, ex)
                            r_pT.wrote(ex)
                            ex_t[kt] = ex
                            pt_i[kt] = i_pt

                        def emit_pv(kt):
                            pv = ph.t(lambda e, kt=kt, h2=h2, iv=i_vt, it=pt_i[kt]: e.matmul(
                                yTs[h2][0:65, :], vtb[:, iv, kt, :],
                                pT[:, it * TT:(it + 1) * TT],
                                start=(kt == 0), stop=(kt == 15)),
                                after=([ex_t[kt], lv, msv] + d_y) if kt == 0 else [ex_t[kt]])
                            r_pT.read(pt_i[kt], pv)
                            r_vt.read(i_vt, pv)
                            return pv

                        emit_sc(0)
                        emit_sc(1)
                        for kt in range(2, 16):
                            emit_sc(kt)
                            emit_pv(kt - 2)
                        emit_pv(14)
                        pv_last = emit_pv(15)
                        r_yT[h2].wrote(pv_last)
                        qh = 2 * kvh + h2
                        i_ys, d_ys = r_ys.acquire()
                        ev = ph.s(lambda e, h2=h2, i=i_ys: e.copy(
                            ysb[:, i * TT:(i + 1) * TT], yTs[h2][0:65, :]),
                            after=[pv_last] + d_ys)
                        r_yT[h2].read(i_y, ev)
                        r_ys.wrote(ev)
                        i_rz, d_rz = r_rz.acquire()
                        rc_ = ph.v(lambda e, i=i_ys, ir=i_rz: e.reciprocal(
                            rz[:, ir * TT:(ir + 1) * TT],
                            ysb[64:65, i * TT:(i + 1) * TT]), after=[ev] + d_rz)
                        f1 = ph.v(lambda e, ir=i_rz: e.tensor_scalar(
                            rz[:, ir * TT:(ir + 1) * TT], rz[:, ir * TT:(ir + 1) * TT],
                            wcol[0:1, 1:2], None, ALU.mult), after=[rc_, wv2])
                        r_rz.wrote(f1)
                        i_m, d_m = r_misc.acquire()
                        bc = ph.t(lambda e, ir=i_rz: e.matmul(
                            pmisc[0:64, :], onr[0:1, 0:64], rz[:, ir * TT:(ir + 1) * TT],
                            start=True, stop=True), after=[f1] + cl + d_m)
                        r_misc.wrote(bc)
                        r_rz.read(i_rz, bc)
                        i_am, d_am = r_am.acquire()
                        la = ph.dma(lambda e, b=b, qh=qh, qc=qc, i=i_am: e.dma_start(
                            out=amt[:, i * TT:(i + 1) * TT],
                            in_=amem[b, qh * 64:(qh + 1) * 64, ts(qc, TT)]),
                            after=d_am, q="gpsimd")
                        r_am.wrote(la)
                        t_ = ph.v(lambda e, i=i_ys: e.tensor_mul(
                            ysb[0:64, i * TT:(i + 1) * TT],
                            ysb[0:64, i * TT:(i + 1) * TT], pmisc[0:64, :]),
                            after=[ev, bc])
                        r_misc.read(i_m, t_)
                        r_ys.read(i_ys, t_)
                        r_ys.wrote(t_)
                        i_a1, d_a1 = r_a1.acquire()
                        cb = ph.v(lambda e, i=i_ys, ia=i_am, io=i_a1: e.scalar_tensor_tensor(
                            a1b[:, io * TT:(io + 1) * TT], amt[:, ia * TT:(ia + 1) * TT],
                            wcol[0:64, 0:1], ysb[0:64, i * TT:(i + 1) * TT],
                            ALU.mult, ALU.add),
                            after=[t_, la, wv2] + d_a1)
                        r_am.read(i_am, cb)
                        r_ys.read(i_ys, cb)
                        r_a1.wrote(cb)
                        sta = ph.dma(lambda e, b=b, qh=qh, qc=qc, io=i_a1: e.dma_start(
                            out=a1[b, qh * 64:(qh + 1) * 64, ts(qc, TT)],
                            in_=a1b[:, io * TT:(io + 1) * TT]), after=[cb])
                        r_a1.read(i_a1, sta)
        ph.finalize()



def emit_p8(mk: MK):
    nc = mk.nc
    xn1, x1b, a1 = mk.buf("xn1"), mk.buf("x1b"), mk.buf("a1")
    stats1 = mk.buf("stats1")
    y_out = mk.buf("y_out")
    gw_h = mk.buf("L1_ffn_gate_w")
    w1_h, w3_h, w2_h = mk.buf("L1_ffn_w1"), mk.buf("L1_ffn_w3"), mk.buf("L1_ffn_w2")
    NQ, FQ = 4, 1024
    ph = Phase(mk, "p8")
    with contextlib.ExitStack() as _st:
        idf = _st.enter_context(nc.sbuf_tensor("p8_idf", [128, 128], F32))
        onr = _st.enter_context(nc.sbuf_tensor("p8_onr", [1, 128], F32))
        onc = _st.enter_context(nc.sbuf_tensor("p8_onc", [128, 1], F32))
        gwf = _st.enter_context(nc.sbuf_tensor("p8_gw", [128, NC * 8], F32))
        gws = _st.enter_context(nc.sbuf_tensor("p8_gws", [1, 8], F32))
        bgw = _st.enter_context(nc.sbuf_tensor("p8_bgw", [128, 8], F32))
        xg = _st.enter_context(nc.sbuf_tensor("p8_xg", [128, 3 * TT], F32))
        wi_all = _st.enter_context(nc.sbuf_tensor("p8_wi", [8, B * T], F32))
        xnb = _st.enter_context(nc.sbuf_tensor("p8_xnb", [128, 2 * NC * TT], BF16))
        lgr = _st.enter_context(nc.sbuf_tensor("p8_lgr", [10, TT], F32))
        lt = _st.enter_context(nc.sbuf_tensor("p8_lt", [128, 8], F32))
        lt10 = _st.enter_context(nc.sbuf_tensor("p8_lt10", [128, 10], F32))
        tmp8 = _st.enter_context(nc.sbuf_tensor("p8_t8", [128, 8], F32))
        g2 = _st.enter_context(nc.sbuf_tensor("p8_g2", [128, 8], F32))
        gsc = _st.enter_context(nc.sbuf_tensor("p8_gs", [128, 8], F32))
        gm = _st.enter_context(nc.sbuf_tensor("p8_gm", [128, 6], F32))
        w1q = _st.enter_context(nc.sbuf_tensor("p8_w1", [128, NC * FQ], BF16))
        w3q = _st.enter_context(nc.sbuf_tensor("p8_w3", [128, NC * FQ], BF16))
        w2q = _st.enter_context(nc.sbuf_tensor("p8_w2", [128, (FQ // 128) * D], BF16))
        y_acc = _st.enter_context(nc.sbuf_tensor("p8_y", [128, NC * T], F32))
        hsb = _st.enter_context(nc.sbuf_tensor("p8_h", [128, 2 * (FQ // 128) * TT], BF16))
        sl = _st.enter_context(nc.sbuf_tensor("p8_sl", [128, 2 * TT], F32))
        wib = _st.enter_context(nc.sbuf_tensor("p8_wib", [128, 2 * TT], F32))
        wirow = _st.enter_context(nc.sbuf_tensor("p8_wir", [1, T], F32))
        tmp = _st.enter_context(nc.sbuf_tensor("p8_tmp", [128, 2 * TT], F32))
        a1t = _st.enter_context(nc.sbuf_tensor("p8_a1t", [128, TT], BF16))
        ob = _st.enter_context(nc.sbuf_tensor("p8_ob", [128, 2 * TT], F32))
        otb = _st.enter_context(nc.sbuf_tensor("p8_otb", [128, 2 * 128], F32))
        pA0 = _st.enter_context(nc.psum_tensor("p8_pA0", [128, TT], F32))
        pA1 = _st.enter_context(nc.psum_tensor("p8_pA1", [128, TT], F32))
        pB0 = _st.enter_context(nc.psum_tensor("p8_pB0", [128, TT], F32))
        pB1 = _st.enter_context(nc.psum_tensor("p8_pB1", [128, TT], F32))
        pj0 = _st.enter_context(nc.psum_tensor("p8_p0", [128, TT], F32))
        pj1 = _st.enter_context(nc.psum_tensor("p8_p1", [128, TT], F32))
        pw = _st.enter_context(nc.psum_tensor("p8_pw", [128, TT], F32))
        pmisc = _st.enter_context(nc.psum_tensor("p8_pm", [128, TT], F32))
        pj = [pj0, pj1]
        cl = []
        cl.append(ph.dma(lambda e: e.dma_start(out=idf[:], in_=mk.buf("c_ident_f32")[:])))
        cl.append(ph.dma(lambda e: e.dma_start(out=onr[:], in_=mk.buf("c_ones_row")[:])))
        cl.append(ph.dma(lambda e: e.dma_start(out=onc[:], in_=mk.buf("c_ones_col")[:])))
        gl = []
        for kc in range(NC):
            gl.append(ph.dma(lambda e, kc=kc: e.dma_start(
                out=gwf[:, kc * 8:(kc + 1) * 8], in_=gw_h[ts(kc, 128), :]), q="gpsimd"))
        r_x = Ring(2)
        r_misc, r_pw, r_lgr, r_lt, r_g2, r_gs, r_gm, r_lt10, r_t8 = (Ring(1),) * 9
        r_xg = Ring(3)
        GM = lambda k: gm[:, k:k + 1]
        wi_toks = []
        i_m, d_m = r_misc.acquire()
        csl = []
        for kc in range(NC):
            csl.append(ph.t(lambda e, kc=kc: e.matmul(
                pmisc[0:1, 0:8], onc[:], gwf[:, kc * 8:(kc + 1) * 8],
                start=(kc == 0), stop=(kc == NC - 1)),
                after=(cl + gl + d_m) if kc == 0 else [gl[kc]]))
        r_misc.wrote(csl[-1])
        egs = ph.s(lambda e: e.copy(gws[:], pmisc[0:1, 0:8]), after=[csl[-1]])
        r_misc.read(i_m, egs)
        i_pw, d_pw = r_pw.acquire()
        bc = ph.t(lambda e: e.matmul(pw[0:128, 0:8], onr[:], gws[:],
                                     start=True, stop=True), after=[egs] + cl + d_pw)
        r_pw.wrote(bc)
        ebg = ph.v(lambda e: e.tensor_copy(bgw[:], pw[0:128, 0:8]), after=[bc])
        r_pw.read(i_pw, ebg)
        for b in range(B):
            for tt in range(NTT):
                i_m, d_m = r_misc.acquire()
                mt = []
                for kc in range(NC):
                    i_xg, d_xg = r_xg.acquire()
                    lx = ph.dma(lambda e, b=b, tt=tt, kc=kc, i=i_xg: e.dma_start(
                        out=xg[:, i * TT:(i + 1) * TT],
                        in_=x1b[b, ts(kc, 128), ts(tt, TT)]), after=d_xg)
                    r_xg.wrote(lx)
                    mt.append(ph.t(lambda e, kc=kc, i=i_xg: e.matmul(
                        pmisc[0:8, 0:TT], gwf[:, kc * 8:(kc + 1) * 8],
                        xg[:, i * TT:(i + 1) * TT],
                        start=(kc == 0), stop=(kc == NC - 1)),
                        after=([lx] + cl + gl + d_m) if kc == 0 else [lx]))
                    r_xg.read(i_xg, mt[-1])
                r_misc.wrote(mt[-1])
                i_lg, d_lg = r_lgr.acquire()
                elg = ph.s(lambda e: e.copy(lgr[0:8, :], pmisc[0:8, 0:TT]),
                           after=[mt[-1]] + d_lg)
                r_misc.read(i_m, elg)
                sld = ph.dma(lambda e, b=b, tt=tt: e.dma_start(
                    out=lgr[8:10, :], in_=stats1[b, :, ts(tt, TT)]),
                    after=d_lg, q="gpsimd")
                r_lgr.wrote(sld)
                for qs in range(4):
                    i_pw, d_pw = r_pw.acquire()
                    tf = ph.t(lambda e, qs=qs: e.matmul(
                        pw[0:128, 0:10], lgr[0:10, qs * 128:(qs + 1) * 128], idf[0:10, 0:10],
                        start=True, stop=True), after=[elg, sld] + cl + d_pw)
                    r_lgr.read(i_lg, tf)
                    r_pw.wrote(tf)
                    i_l10, d_l10 = r_lt10.acquire()
                    el10 = ph.s(lambda e: e.copy(lt10[:], pw[0:128, 0:10]), after=[tf] + d_l10)
                    r_pw.read(i_pw, el10)
                    r_lt10.wrote(el10)
                    i_lt, d_lt = r_lt.acquire()
                    i_t8, d_t8 = r_t8.acquire()
                    n1 = ph.v(lambda e: e.tensor_scalar(lt[:], lt10[:, 0:8], lt10[:, 9:10],
                                                        None, ALU.mult),
                              after=[el10] + d_lt)
                    n2 = ph.v(lambda e: e.tensor_scalar(tmp8[:], bgw[:], lt10[:, 8:9],
                                                        None, ALU.mult),
                              after=[el10, ebg] + d_t8)
                    r_t8.wrote(n2)
                    el = ph.v(lambda e: e.tensor_add(lt[:], lt[:], tmp8[:]), after=[n1, n2])
                    r_t8.read(i_t8, el)
                    r_lt10.read(i_l10, el)
                    r_lt.wrote(el)
                    i_gm, d_gm = r_gm.acquire()
                    i_gs, d_gs = r_gs.acquire()
                    i_g2, d_g2 = r_g2.acquire()
                    v1 = ph.v(lambda e: e.reduce_max(GM(0), lt[:], axis=AX.X, negate=True),
                              after=[el] + d_gm)
                    s1 = ph.s(lambda e: e.activation(gsc[:], lt[:], AF.Exp, bias=GM(0),
                                                     accum_out=GM(1)),
                              after=[v1, el] + d_gs)
                    v2 = ph.v(lambda e: e.reciprocal(GM(1), GM(1)), after=[s1])
                    v3 = ph.v(lambda e: e.tensor_scalar(gsc[:], gsc[:], GM(1), None, ALU.mult),
                              after=[v2, s1])
                    v4 = ph.v(lambda e: e.reduce_max(GM(2), gsc[:], axis=AX.X), after=[v3])
                    v5 = ph.v(lambda e: e.tensor_scalar(lt[:], gsc[:], GM(2), None, ALU.is_equal),
                              after=[v4, v3])
                    v6 = ph.v(lambda e: e.scalar_tensor_tensor(g2[:], lt[:], NEG, gsc[:],
                                                               ALU.mult, ALU.add),
                              after=[v5, v3] + d_g2)
                    v7 = ph.v(lambda e: e.reduce_max(GM(3), g2[:], axis=AX.X), after=[v6])
                    v8 = ph.v(lambda e: e.tensor_scalar(g2[:], g2[:], GM(3), None, ALU.is_equal),
                              after=[v7])
                    v9 = ph.v(lambda e: e.tensor_add(GM(4), GM(2), GM(3)), after=[v4, v7])
                    v10 = ph.v(lambda e: e.reciprocal(GM(4), GM(4)), after=[v9])
                    v11 = ph.v(lambda e: e.tensor_add(lt[:], lt[:], g2[:]), after=[v5, v8])
                    v12 = ph.v(lambda e: e.tensor_mul(lt[:], lt[:], gsc[:]), after=[v11, v3])
                    v13 = ph.v(lambda e: e.tensor_scalar(lt[:], lt[:], GM(4), None, ALU.mult),
                               after=[v12, v10])
                    r_gm.wrote(v13)
                    r_gs.wrote(v12)
                    r_g2.wrote(v11)
                    tb = ph.t(lambda e: e.matmul(pw[0:8, 0:128], lt[:], idf[:],
                                                 start=True, stop=True),
                              after=[v13] + cl + d_pw + [el])
                    r_lt.read(i_lt, tb)
                    r_lt.wrote(tb)
                    r_pw.wrote(tb)
                    ew = ph.v(lambda e, b=b, tt=tt, qs=qs: e.tensor_copy(
                        wi_all[0:8, b * T + tt * TT + qs * 128: b * T + tt * TT + (qs + 1) * 128],
                        pw[0:8, 0:128]), after=[tb])
                    r_pw.read(i_pw, ew)
                    wi_toks.append(ew)
        r_pA = [Ring(1), Ring(1)]
        r_pB = [Ring(1), Ring(1)]
        r_pj = Ring(2)
        r_sl, r_wib, r_tmp = Ring(2), Ring(2), Ring(2)
        r_y = Ring(1)
        r_a1t, r_ob, r_otb = Ring(1), Ring(2), Ring(2)
        pA = [pA0, pA1]
        pB = [pB0, pB1]
        WQ = ("sync", "gpsimd", "scalar")
        prev_pe = []
        for b in range(B):
            i_y, d_y = r_y.acquire()
            y_toks = list(d_y)
            r_wr = Ring(1)
            for e_ in range(8):
                i_wr, d_wr = r_wr.acquire()
                swr = ph.dma(lambda e, e_=e_, b=b: e.dma_start(
                    out=wirow[:], in_=wi_all[e_:e_ + 1, b * T:(b + 1) * T]),
                    after=wi_toks + d_wr, q="gpsimd")
                r_wr.wrote(swr)
                for q4 in range(NQ):
                    wl = []
                    qi = 0
                    for kc in range(NC):
                        wl.append(ph.dma(lambda e, e_=e_, q4=q4, kc=kc: e.dma_start(
                            out=w1q[:, kc * FQ:(kc + 1) * FQ],
                            in_=w1_h[e_, ts(kc, 128), q4 * FQ:(q4 + 1) * FQ]),
                            after=prev_pe, q="gpsimd"))
                        qi += 1
                        wl.append(ph.dma(lambda e, e_=e_, q4=q4, kc=kc: e.dma_start(
                            out=w3q[:, kc * FQ:(kc + 1) * FQ],
                            in_=w3_h[e_, ts(kc, 128), q4 * FQ:(q4 + 1) * FQ]),
                            after=prev_pe, q="gpsimd"))
                        qi += 1
                        wl.append(ph.dma(lambda e, e_=e_, q4=q4, kc=kc: e.dma_start(
                            out=w2q[:, kc * D:(kc + 1) * D],
                            in_=w2_h[e_, q4 * FQ + kc * 128: q4 * FQ + (kc + 1) * 128, :]),
                            after=prev_pe, q="gpsimd"))
                        qi += 1
                    for tp in range(2):
                        xis, xls = [], []
                        for hh in range(2):
                            tt = 2 * tp + hh
                            i_x, d_x = r_x.acquire()
                            xl = []
                            for c in range(NC):
                                xl.append(ph.dma(lambda e, b=b, tt=tt, c=c, i=i_x: e.dma_start(
                                    out=xnb[:, i * NC * TT + c * TT: i * NC * TT + (c + 1) * TT],
                                    in_=xn1[b, ts(c, 128), ts(tt, TT)]), after=d_x))
                            r_x.wrote(xl[-1])
                            xis.append(i_x)
                            xls.append(xl)
                        h_toks = {}
                        for fc in range(FQ // 128):
                            ipa, dpa = zip(*[r_pA[hh].acquire() for hh in range(2)])
                            mlast = [None, None]
                            for kc in range(NC):
                                for hh in range(2):
                                    mlast[hh] = ph.t(lambda e, kc=kc, fc=fc, hh=hh,
                                                     i=xis[hh]: e.matmul(
                                        pA[hh][:],
                                        w1q[:, kc * FQ + fc * 128: kc * FQ + (fc + 1) * 128],
                                        xnb[:, i * NC * TT + kc * TT: i * NC * TT + (kc + 1) * TT],
                                        start=(kc == 0), stop=(kc == NC - 1)),
                                        after=([xls[hh][kc]] + wl + list(dpa[hh]))
                                        if kc == 0 else [xls[hh][kc]])
                            malast = list(mlast)
                            for hh in range(2):
                                r_pA[hh].wrote(malast[hh])
                                r_x.read(xis[hh], malast[hh])
                            ipb, dpb = zip(*[r_pB[hh].acquire() for hh in range(2)])
                            for kc in range(NC):
                                for hh in range(2):
                                    mlast[hh] = ph.t(lambda e, kc=kc, fc=fc, hh=hh,
                                                     i=xis[hh]: e.matmul(
                                        pB[hh][:],
                                        w3q[:, kc * FQ + fc * 128: kc * FQ + (fc + 1) * 128],
                                        xnb[:, i * NC * TT + kc * TT: i * NC * TT + (kc + 1) * TT],
                                        start=(kc == 0), stop=(kc == NC - 1)),
                                        after=([xls[hh][kc]] + wl + list(dpb[hh]))
                                        if kc == 0 else [xls[hh][kc]])
                            mblast = list(mlast)
                            for hh in range(2):
                                r_pB[hh].wrote(mblast[hh])
                                r_x.read(xis[hh], mblast[hh])
                            for hh in range(2):
                                i_sl, d_sl = r_sl.acquire()
                                sv = ph.s(lambda e, hh=hh, i=i_sl: e.activation(
                                    sl[:, i * TT:(i + 1) * TT], pA[hh][:], AF.Silu),
                                    after=[malast[hh]] + d_sl)
                                r_pA[hh].read(ipa[hh], sv)
                                r_sl.wrote(sv)
                                hv = ph.v(lambda e, fc=fc, hh=hh, i=i_sl: e.tensor_mul(
                                    hsb[:, (hh * 8 + fc) * TT:(hh * 8 + fc + 1) * TT],
                                    pB[hh][:], sl[:, i * TT:(i + 1) * TT]),
                                    after=[mblast[hh], sv] + (h_toks.get((hh, fc)) or []))
                                r_pB[hh].read(ipb[hh], hv)
                                r_sl.read(i_sl, hv)
                                h_toks[(hh, fc)] = [hv]
                        ebws, wis = [], []
                        for hh in range(2):
                            tt = 2 * tp + hh
                            i_wb, d_wb = r_wib.acquire()
                            i_m, d_m = r_misc.acquire()
                            bw = ph.t(lambda e, tt=tt: e.matmul(
                                pmisc[0:128, 0:TT], onr[:],
                                wirow[0:1, tt * TT:(tt + 1) * TT],
                                start=True, stop=True), after=[swr] + cl + d_m)
                            r_misc.wrote(bw)
                            r_wr.read(i_wr, bw)
                            ebw = ph.s(lambda e, i=i_wb: e.copy(
                                wib[:, i * TT:(i + 1) * TT], pmisc[0:128, 0:TT]),
                                after=[bw] + d_wb)
                            r_misc.read(i_m, ebw)
                            r_wib.wrote(ebw)
                            ebws.append(ebw)
                            wis.append(i_wb)
                        for hh in range(2):
                            tt = 2 * tp + hh
                            for oc in range(NC):
                                i_pj, d_pj = r_pj.acquire()
                                mw = []
                                for j in range(FQ // 128):
                                    mw.append(ph.t(lambda e, j=j, oc=oc, hh=hh,
                                                   i=i_pj: e.matmul(
                                        pj[i][:],
                                        w2q[:, j * D + oc * 128: j * D + (oc + 1) * 128],
                                        hsb[:, (hh * 8 + j) * TT:(hh * 8 + j + 1) * TT],
                                        start=(j == 0), stop=(j == FQ // 128 - 1)),
                                        after=(h_toks[(hh, j)] + wl + d_pj)
                                        if j == 0 else h_toks[(hh, j)]))
                                r_pj.wrote(mw[-1])
                                first = (e_ == 0 and q4 == 0)
                                if first:
                                    yv = ph.v(lambda e, oc=oc, tt=tt, i=i_pj,
                                              iw=wis[hh]: e.tensor_mul(
                                        y_acc[:, oc * T + tt * TT: oc * T + (tt + 1) * TT],
                                        pj[i][:], wib[:, iw * TT:(iw + 1) * TT]),
                                        after=[mw[-1], ebws[hh]] + y_toks)
                                    r_pj.read(i_pj, yv)
                                    r_wib.read(wis[hh], yv)
                                    y_toks.append(yv)
                                else:
                                    i_tm, d_tm = r_tmp.acquire()
                                    tv = ph.v(lambda e, i=i_pj, it=i_tm,
                                              iw=wis[hh]: e.tensor_mul(
                                        tmp[:, it * TT:(it + 1) * TT], pj[i][:],
                                        wib[:, iw * TT:(iw + 1) * TT]),
                                        after=[mw[-1], ebws[hh]] + d_tm)
                                    r_pj.read(i_pj, tv)
                                    r_wib.read(wis[hh], tv)
                                    r_tmp.wrote(tv)
                                    yv = ph.g(lambda e, oc=oc, tt=tt, it=i_tm: e.tensor_add(
                                        y_acc[:, oc * T + tt * TT: oc * T + (tt + 1) * TT],
                                        y_acc[:, oc * T + tt * TT: oc * T + (tt + 1) * TT],
                                        tmp[:, it * TT:(it + 1) * TT]), after=[tv] + y_toks)
                                    r_tmp.read(i_tm, yv)
                                    y_toks.append(yv)
                                for j in range(FQ // 128):
                                    h_toks[(hh, j)] = h_toks[(hh, j)] + [mw[-1]]
                    prev_pe = [mw[-1]]
            for tt in range(NTT):
                for oc in range(NC):
                    i_ob, d_ob = r_ob.acquire()
                    lx = ph.dma(lambda e, b=b, tt=tt, oc=oc, io=i_ob: e.dma_start(
                        out=ob[:, io * TT:(io + 1) * TT], in_=x1b[b, ts(oc, 128), ts(tt, TT)]),
                        after=d_ob)
                    i_a, d_a = r_a1t.acquire()
                    la = ph.dma(lambda e, b=b, tt=tt, oc=oc: e.dma_start(
                        out=a1t[:], in_=a1[b, ts(oc, 128), ts(tt, TT)]), after=d_a,
                        q="gpsimd")
                    r_a1t.wrote(la)
                    ad1 = ph.v(lambda e, oc=oc, tt=tt, io=i_ob: e.tensor_add(
                        ob[:, io * TT:(io + 1) * TT], ob[:, io * TT:(io + 1) * TT],
                        y_acc[:, oc * T + tt * TT: oc * T + (tt + 1) * TT]),
                        after=[lx] + y_toks)
                    ad2 = ph.v(lambda e, io=i_ob: e.tensor_add(
                        ob[:, io * TT:(io + 1) * TT], ob[:, io * TT:(io + 1) * TT], a1t[:]),
                        after=[ad1, la])
                    r_a1t.read(i_a, ad2)
                    r_ob.wrote(ad2)
                    r_y.read(i_y, ad1)
                    for s_ in range(4):
                        i_pt, d_pt = r_pj.acquire()
                        tr = ph.t(lambda e, s_=s_, io=i_ob, ip=i_pt: e.matmul(
                            pj[ip][:, 0:128], ob[:, io * TT + s_ * 128: io * TT + (s_ + 1) * 128],
                            idf[:], start=True, stop=True), after=[ad2] + cl + d_pt)
                        r_pj.wrote(tr)
                        r_ob.read(i_ob, tr)
                        i_ot, d_ot = r_otb.acquire()
                        ev = ph.s(lambda e, io=i_ot, ip=i_pt: e.copy(
                            otb[:, io * 128:(io + 1) * 128], pj[ip][:, 0:128]), after=[tr] + d_ot)
                        r_pj.read(i_pt, ev)
                        r_otb.wrote(ev)
                        st = ph.dma(lambda e, b=b, tt=tt, oc=oc, s_=s_, io=i_ot: e.dma_start(
                            out=y_out[b, tt * TT + s_ * 128: tt * TT + (s_ + 1) * 128,
                                      oc * 128:(oc + 1) * 128],
                            in_=otb[:, io * 128:(io + 1) * 128]), after=[ev])
                        r_otb.read(i_ot, st)
        ph.finalize()



PHASE_GROUPS = [
    ("init", ["x_in", "L0_attn_Wq", "L0_attn_Wkv", "L0_attn_Wout"],
     ["x_fm", "stats0", "xf", "wqT", "wkvT", "woutT"],
     [emit_init, emit_wtrans]),
    ("p1", ["xf", "wqT", "wkvT"],
     ["q0", "k0", "v0t", "qn0", "kn0"],
     [emit_p1]),
    ("p2", ["q0", "k0", "v0t", "qn0", "kn0", "L0_attn_lambda_init", "L0_attn_rms_weight"],
     ["y0"],
     [emit_p2]),
    ("p3", ["y0", "x_fm", "woutT"],
     ["x1"],
     [emit_p3]),
    ("p4", ["xf", "x1", "L0_ffn_wqkv", "L0_ffn_q_scale", "L0_ffn_k_scale",
            "L0_ffn_wo", "L0_ffn_ln_g", "L0_ffn_ln_b", "L0_ffn_w1",
            "L0_ffn_b1", "L0_ffn_w2", "L0_ffn_b2"],
     ["x1b"],
     [emit_p4]),
    ("p5", ["x1b"], ["xn1", "stats1"], [emit_p5]),
    ("p6", ["xn1", "L1_attn_Wq", "L1_attn_Wk", "L1_attn_Wv", "L1_attn_memory",
            "L1_attn_memory_norm"],
     ["q1", "k1", "v1t", "qn1", "kn1", "amem"],
     [emit_p6]),
    ("p7", ["q1", "k1", "v1t", "qn1", "kn1", "amem", "L1_attn_memory_weight"],
     ["a1"],
     [emit_p7]),
    ("p8", ["xn1", "x1b", "stats1", "a1", "L1_ffn_gate_w", "L1_ffn_w1", "L1_ffn_w3",
            "L1_ffn_w2"],
     ["y_out"],
     [emit_p8]),
]


def _in_map_for(mk: MK, core, shard_inputs, state):
    m = {}
    for name in mk.used_inputs:
        if name in CONSTS:
            m[name] = CONSTS[name]
        elif name in WEIGHT_SPECS:
            m[name] = shard_inputs[name]
        elif name == "x_in":
            m[name] = shard_inputs["x_in"][core]
        else:
            m[name] = state[name][core]
    return m


def build_fused():
    mk = MK(reads=[], writes=["y_out"])
    for name, reads, writes, fns in PHASE_GROUPS:
        for fn in fns:
            fn(mk)
    return mk


def run_fused(shard_inputs):
    mk = build_fused()
    in_maps = [_in_map_for(mk, c, shard_inputs, {}) for c in range(8)]
    res = run_bass_kernel_spmd(mk.nc, in_maps, core_ids=list(range(8)))
    return [res.results[c]["y_out"] for c in range(8)]


def run_groups(groups, shard_inputs, state=None):
    state = state if state is not None else {}
    for name, reads, writes, fns in groups:
        mk = MK(reads=reads, writes=writes)
        for fn in fns:
            fn(mk)
        in_maps = [_in_map_for(mk, c, shard_inputs, state) for c in range(8)]
        res = run_bass_kernel_spmd(mk.nc, in_maps, core_ids=list(range(8)))
        for w in writes:
            if w == "y_out":
                state[w] = [res.results[c]["y_out"] for c in range(8)]
            else:
                state[w] = [res.results[c][w] for c in range(8)]
    return state


def kernel(**inputs):
    x = np.ascontiguousarray(inputs["L0_attn_x"])
    shard_inputs = dict(inputs)
    shard_inputs["x_in"] = [x[2 * c:2 * c + 2] for c in range(8)]
    outs = run_fused(shard_inputs)
    return np.concatenate(outs, axis=0)


# version 06 (final)
# speedup vs baseline: 1.2292x; 1.0015x over previous
import contextlib
import numpy as np
import ml_dtypes
import concourse.bass as bass
import concourse.mybir as mybir
from concourse.bass_isa import ReduceOp
from concourse import library_config as LC
from concourse.bass_utils import run_bass_kernel_spmd

F32 = mybir.dt.float32
BF16 = mybir.dt.bfloat16
AF = mybir.ActivationFunctionType
ALU = mybir.AluOpType
AX = mybir.AxisListType
BFnp = ml_dtypes.bfloat16

B = 2
T = 2048
D = 1024
TT = 512
NTT = T // TT
NC = D // 128

NEG = -1.0e30
LN_EPS = 1e-5
STRICT_SYNC = True



def _rope0_tables():
    d = np.arange(64)
    i = (d % 32) // 2
    inv = 1.0 / 10000.0 ** (np.arange(0, 32, 2) / 32.0)
    t = np.arange(T, dtype=np.float64)
    th = np.outer(inv[i], t)
    cos, sin = np.cos(th), np.sin(th)
    sgn = np.where(d % 2 == 0, -1.0, 1.0)[:, None]
    s = 32.0 ** -0.5
    return ((cos * s).astype(np.float32), (sgn * sin * s).astype(np.float32),
            cos.astype(np.float32), (sgn * sin).astype(np.float32))


def _xpos_tables():
    m = np.arange(32)
    inv = 1.0 / 10000.0 ** (2.0 * m / 64.0)
    base = (2.0 * m + 0.4 * 64.0) / (1.4 * 64.0)
    t = np.arange(T, dtype=np.float64)
    power = (t - T // 2) / (512 // 2)
    th = np.outer(inv, t)
    sc = base[:, None] ** power[None, :]
    cos, sin = np.cos(th), np.sin(th)
    e = np.arange(64)
    mm = e // 2
    sgn = np.where(e % 2 == 0, -1.0, 1.0)[:, None]
    return ((cos[mm] * sc[mm] * 8.0).astype(np.float32),
            (sgn * sin[mm] * sc[mm] * 8.0).astype(np.float32),
            (cos[mm] / sc[mm]).astype(np.float32),
            (sgn * sin[mm] / sc[mm]).astype(np.float32))


def build_consts():
    c = {}
    c["c_ident_f32"] = np.eye(128, dtype=np.float32)
    c["c_ident_bf16"] = np.eye(128, dtype=np.float32).astype(BFnp)
    c["c_ones_col"] = np.ones((128, 1), np.float32)
    c["c_ones_row"] = np.ones((1, 128), np.float32)
    c["c_ones64_col"] = np.ones((64, 1), np.float32)
    seg2 = np.zeros((128, 2), np.float32)
    seg2[0:64, 0] = 1.0
    seg2[64:128, 1] = 1.0
    c["c_seg64"] = seg2
    seg4 = np.zeros((128, 4), np.float32)
    for g in range(4):
        seg4[g * 32:(g + 1) * 32, g] = 1.0
    c["c_seg32x4"] = seg4
    ii, jj = np.meshgrid(np.arange(128), np.arange(128), indexing="ij")
    c["c_causal"] = np.where(jj > ii, NEG, 0.0).astype(np.float32)
    c["c_anticausal"] = np.where(jj < ii, NEG, 0.0).astype(np.float32)
    a, b_, ck, dk = _rope0_tables()
    c["c_r0_aq"], c["c_r0_bq"], c["c_r0_ak"], c["c_r0_bk"] = a, b_, ck, dk
    aq, bq, ak, bk = _xpos_tables()
    c["c_xp_aq"], c["c_xp_bq"], c["c_xp_ak"], c["c_xp_bk"] = aq, bq, ak, bk
    return c


CONSTS = build_consts()
SWAP_MASK = [i ^ 1 for i in range(32)]


BUF_SPECS = {
    "x_fm":   ([B, D, T], F32),
    "stats0": ([B, 2, T], F32),
    "wqT":    ([D, D], BF16),
    "wkvT":   ([D, 512], BF16),
    "woutT":  ([D, D], BF16),
    "q0":     ([B, D, T], BF16),
    "k0":     ([B, 256, T], BF16),
    "v0t":    ([B, 4, T, 64], BF16),
    "qn0":    ([B, 32, T], F32),
    "kn0":    ([B, 8, T], F32),
    "y0":     ([B, D, T], BF16),
    "x1":     ([B, D, T], F32),
    "xf":     ([B, D, T], F32),
    "fq":     ([B, 512, T], BF16),
    "fk":     ([B, 512, T], BF16),
    "fvt":    ([B, 8, T, 64], BF16),
    "fo":     ([B, 512, T], BF16),
    "fhn":    ([B, D, T], BF16),
    "x1b":    ([B, D, T], F32),
    "xfi":    ([B, D, T], F32),
    "stats1": ([B, 2, T], F32),
    "xn1":    ([B, D, T], BF16),
    "q1":     ([B, D, T], BF16),
    "k1":     ([B, 512, T], BF16),
    "v1t":    ([B, 8, T, 64], BF16),
    "qn1":    ([B, 16, T], F32),
    "kn1":    ([B, 8, T], F32),
    "amem":   ([B, D, T], BF16),
    "a1":     ([B, D, T], BF16),
}

WEIGHT_SPECS = {
    "L0_attn_Wq": ([1024, 1024], F32),
    "L0_attn_Wkv": ([512, 1024], F32),
    "L0_attn_Wout": ([1024, 1024], F32),
    "L0_attn_lambda_init": ([16], F32),
    "L0_attn_rms_weight": ([64], F32),
    "L0_ffn_wqkv": ([2, 1024, 1536], F32),
    "L0_ffn_q_scale": ([2, 64], F32),
    "L0_ffn_k_scale": ([2, 64], F32),
    "L0_ffn_wo": ([2, 512, 1024], F32),
    "L0_ffn_ln_g": ([2, 1024], F32),
    "L0_ffn_ln_b": ([2, 1024], F32),
    "L0_ffn_w1": ([2, 1024, 5460], F32),
    "L0_ffn_b1": ([2, 5460], F32),
    "L0_ffn_w2": ([2, 2730, 1024], F32),
    "L0_ffn_b2": ([2, 1024], F32),
    "L1_attn_Wq": ([1024, 1024], F32),
    "L1_attn_Wk": ([1024, 512], F32),
    "L1_attn_Wv": ([1024, 512], F32),
    "L1_attn_memory": ([512, 512], F32),
    "L1_attn_memory_norm": ([512], F32),
    "L1_attn_memory_weight": ([1], F32),
    "L1_ffn_gate_w": ([1024, 8], F32),
    "L1_ffn_w1": ([8, 1024, 4096], F32),
    "L1_ffn_w3": ([8, 1024, 4096], F32),
    "L1_ffn_w2": ([8, 4096, 1024], F32),
}

ENGINES = ("sync", "tensor", "vector", "scalar", "gpsimd")


def ts(i, n):
    return slice(i * n, (i + 1) * n)


DMA_POOL = {"sync": 20, "gpsimd": 20, "scalar": 8}


class MK:

    def __init__(self, reads=(), writes=()):
        self.nc = bass.Bass("TRN2", target_bir_lowering=False, debug=False)
        self.reads = set(reads)
        self.writes = set(writes)
        self._bufs = {}
        self.used_inputs = set()
        self._sems = {}
        self._sem_guards = []
        for e in ENGINES:
            g = self.nc.semaphore(f"prog_{e}")
            self._sems[e] = g.__enter__()
            self._sem_guards.append(g)
        self.base = {e: 0 for e in ENGINES}
        self.dpool_idx = {q: 0 for q in DMA_POOL}
        self.duses = {}
        for q, n in DMA_POOL.items():
            for j in range(n):
                g = self.nc.semaphore(f"dma_{q}_{j}")
                self._sems[("dma", q, j)] = g.__enter__()
                self._sem_guards.append(g)
                self.duses[(q, j)] = 0
        self._phase_names = set()

    def buf(self, name):
        if name in self._bufs:
            return self._bufs[name]
        if name in BUF_SPECS:
            shape, dt = BUF_SPECS[name]
            if name in self.writes:
                kind = "ExternalOutput"
            elif name in self.reads:
                kind = "ExternalInput"
                self.used_inputs.add(name)
            else:
                kind = "Internal"
        elif name in WEIGHT_SPECS:
            shape, dt = WEIGHT_SPECS[name]
            kind = "ExternalInput"
            self.used_inputs.add(name)
        elif name in CONSTS:
            arr = CONSTS[name]
            shape, dt = list(arr.shape), (BF16 if arr.dtype == BFnp else F32)
            kind = "ExternalInput"
            self.used_inputs.add(name)
        elif name == "x_in":
            shape, dt, kind = [B, T, D], F32, "ExternalInput"
            self.used_inputs.add(name)
        elif name == "y_out":
            shape, dt, kind = [B, T, D], F32, "ExternalOutput"
        else:
            raise KeyError(name)
        ap = self.nc.dram_tensor(name, list(shape), dt, kind=kind).ap()
        self._bufs[name] = ap
        return ap


class Phase:

    def __init__(self, mk: MK, name: str):
        self.mk = mk
        self.name = name
        assert name not in mk._phase_names
        mk._phase_names.add(name)
        self.ops = {e: [] for e in ENGINES}
        self.cnt = {e: 0 for e in ENGINES}
        self.dma_used = set()

    def _emit(self, eng, fn, after, amt, inc_key=None, pre_waits=()):
        seen = {}
        for tok in after:
            if tok is None:
                continue
            k2, v2 = tok
            if k2 != eng or STRICT_SYNC:
                if v2 > seen.get(k2, -1):
                    seen[k2] = v2
        waits = list(pre_waits) + list(seen.items())
        if inc_key is None:
            self.cnt[eng] += amt
            tok = (eng, self.mk.base[eng] + self.cnt[eng])
            self.ops[eng].append((fn, waits, self.mk._sems[eng], amt))
        else:
            self.mk.duses[inc_key[1:]] = self.mk.duses[inc_key[1:]] + 1
            tok = (inc_key, 16 * self.mk.duses[inc_key[1:]])
            self.ops[eng].append((fn, waits, self.mk._sems[inc_key], amt))
        return tok

    def t(self, fn, after=()):
        return self._emit("tensor", fn, after, 1)

    def v(self, fn, after=()):
        return self._emit("vector", fn, after, 1)

    def s(self, fn, after=()):
        return self._emit("scalar", fn, after, 1)

    def g(self, fn, after=()):
        return self._emit("gpsimd", fn, after, 1)

    def dma(self, fn, after=(), q="sync"):
        j = self.mk.dpool_idx[q]
        self.mk.dpool_idx[q] = (j + 1) % DMA_POOL[q]
        key = ("dma", q, j)
        prev = self.mk.duses[(q, j)]
        pre = [(key, 16 * prev)] if prev else []
        self.dma_used.add(key)
        return self._emit(q, fn, after, 16, inc_key=key, pre_waits=pre)

    def finalize(self):
        mk = self.mk
        nc = mk.nc
        by_q = {}
        for key in sorted(self.dma_used, key=str):
            by_q.setdefault(key[1], []).append(key)
        with nc.Block() as blk:
            def mkbody(eng):
                ops = self.ops[eng]
                drains = [(k, 16 * mk.duses[k[1:]]) for k in by_q.get(eng, [])]

                def body(eobj):
                    hwm = {}
                    for fn, waits, sem, amt in ops:
                        for k2, v2 in waits:
                            if v2 > hwm.get(k2, -1):
                                eobj.wait_ge(mk._sems[k2], v2)
                                hwm[k2] = v2
                        inst = fn(eobj)
                        inst.then_inc(sem, amt)
                    for k, v in drains:
                        eobj.wait_ge(mk._sems[k], v)
                return body
            for eng in ENGINES:
                if self.ops[eng] or eng in by_q:
                    getattr(blk, eng)(mkbody(eng))
        for e in ENGINES:
            mk.base[e] += self.cnt[e]
        nc.all_engine_barrier()


class Ring:

    def __init__(self, n):
        self.n = n
        self.slots = [{"w": None, "r": []} for _ in range(n)]
        self.i = -1

    def acquire(self):
        self.i = (self.i + 1) % self.n
        s = self.slots[self.i]
        deps = list(s["r"]) + ([s["w"]] if s["w"] else [])
        s["r"] = []
        s["w"] = None
        return self.i, deps

    def wrote(self, tok):
        self.slots[self.i]["w"] = tok

    def read(self, idx, tok):
        assert tok is not None
        self.slots[idx]["r"].append(tok)



def emit_init(mk: MK, stages=3, max_tiles=99):
    nc = mk.nc
    x_in, x_fm, stats0, xf = mk.buf("x_in"), mk.buf("x_fm"), mk.buf("stats0"), mk.buf("xf")
    ph = Phase(mk, "init")
    with (
        nc.sbuf_tensor("i_idf", [128, 128], F32) as idf,
        nc.sbuf_tensor("i_onc", [128, 1], F32) as onc,
        nc.sbuf_tensor("i_onr", [1, 128], F32) as onr,
        nc.sbuf_tensor("i_tm", [128, 2 * 4 * D], F32) as tm,
        nc.sbuf_tensor("i_fm", [128, 2 * NC * TT], F32) as fm,
        nc.sbuf_tensor("i_sq", [128, TT], F32) as sq,
        nc.sbuf_tensor("i_rows", [1, 8 * TT], F32) as rows,
        nc.sbuf_tensor("i_xn", [128, TT], F32) as xnt,
        nc.psum_tensor("i_pt0", [128, 512], F32) as pt0,
        nc.psum_tensor("i_pt1", [128, 512], F32) as pt1,
        nc.psum_tensor("i_psum", [1, TT], F32) as ps_sum,
        nc.psum_tensor("i_psq", [1, TT], F32) as ps_sq,
        nc.psum_tensor("i_pb0", [128, TT], F32) as pb0,
        nc.psum_tensor("i_pb1", [128, TT], F32) as pb1,
    ):
        pt = [pt0, pt1]
        tC = ph.dma(lambda e: e.dma_start(out=idf[:], in_=mk.buf("c_ident_f32")[:]))
        tC2 = ph.dma(lambda e: e.dma_start(out=onc[:], in_=mk.buf("c_ones_col")[:]))
        tC3 = ph.dma(lambda e: e.dma_start(out=onr[:], in_=mk.buf("c_ones_row")[:]))
        r_tm, r_fm, r_pt = Ring(2), Ring(2), Ring(2)
        r_sq = Ring(1)
        r_xn = Ring(1)
        r_pb = Ring(1)
        r_st = Ring(1)
        r_pss = Ring(1)

        ntile = 0
        for b in range(B):
            for tt in range(NTT):
                ntile += 1
                if ntile > max_tiles:
                    continue
                i_tm, d_tm = r_tm.acquire()
                tload = ph.dma(lambda e, b=b, tt=tt, i=i_tm: e.dma_start(
                    out=tm[:, i * 4 * D:(i + 1) * 4 * D].rearrange("p (s d) -> p s d", s=4),
                    in_=x_in[b, ts(tt, TT), :].rearrange("(s p) d -> p s d", p=128)),
                    after=d_tm)
                r_tm.wrote(tload)
                if stages < 1:
                    st0 = ph.dma(lambda e, b=b, tt=tt, i=i_tm: e.dma_start(
                        out=x_fm[b, 0:128, ts(tt, TT)], in_=tm[:, i * 4 * D: i * 4 * D + TT]),
                        after=[tload], q="gpsimd")
                    r_tm.read(i_tm, st0)
                    continue
                i_fm, d_fm = r_fm.acquire()
                ev_toks = {}
                for s_ in range(4):
                    for c in range(NC):
                        i_pt, d_pt = r_pt.acquire()
                        if stages == 25:
                            ttr = ph.v(lambda e, s_=s_, c=c, i=i_tm, ip=i_pt: e.tensor_copy(
                                pt[ip][:, 0:128],
                                tm[:, i * 4 * D + s_ * D + c * 128: i * 4 * D + s_ * D + (c + 1) * 128]),
                                after=[tload, tC] + d_pt + d_fm)
                        else:
                            ttr = ph.t(lambda e, s_=s_, c=c, i=i_tm, ip=i_pt: e.matmul(
                                pt[ip][:, 0:128],
                                tm[:, i * 4 * D + s_ * D + c * 128: i * 4 * D + s_ * D + (c + 1) * 128],
                                idf[:], start=True, stop=True),
                                after=[tload, tC] + d_pt + d_fm)
                        r_pt.wrote(ttr)
                        r_tm.read(i_tm, ttr)
                        tev = ph.v(lambda e, s_=s_, c=c, i=i_fm, ip=i_pt: e.tensor_copy(
                            fm[:, i * NC * TT + c * TT + s_ * 128: i * NC * TT + c * TT + s_ * 128 + 128],
                            pt[ip][:, 0:128]),
                            after=[ttr])
                        r_pt.read(i_pt, tev)
                        ev_toks.setdefault(c, []).append(tev)
                        r_fm.wrote(tev)
                if stages < 2:
                    for c in range(NC):
                        st1 = ph.dma(lambda e, b=b, tt=tt, c=c, i=i_fm: e.dma_start(
                            out=x_fm[b, ts(c, 128), ts(tt, TT)],
                            in_=fm[:, i * NC * TT + c * TT: i * NC * TT + (c + 1) * TT]),
                            after=ev_toks[c], q="gpsimd")
                        r_fm.read(i_fm, st1)
                    continue
                st_toks = []
                i_ps, d_ps = r_pss.acquire()
                for c in range(NC):
                    i_sq, d_sq = r_sq.acquire()
                    tsq = ph.s(lambda e, c=c, i=i_fm: e.activation(
                        sq[:], fm[:, i * NC * TT + c * TT: i * NC * TT + (c + 1) * TT], AF.Square),
                        after=ev_toks[c] + d_sq)
                    r_sq.wrote(tsq)
                    m1 = ph.t(lambda e, c=c, i=i_fm: e.matmul(
                        ps_sum[:], onc[:], fm[:, i * NC * TT + c * TT: i * NC * TT + (c + 1) * TT],
                        start=(c == 0), stop=(c == NC - 1)),
                        after=ev_toks[c] + [tC2] + d_ps)
                    m2 = ph.t(lambda e, c=c: e.matmul(
                        ps_sq[:], onc[:], sq[:], start=(c == 0), stop=(c == NC - 1)),
                        after=[tsq] + d_ps)
                    r_sq.read(i_sq, m2)
                    r_fm.read(i_fm, m1)
                    r_fm.read(i_fm, m2)
                    st_toks += [m1, m2]
                R = lambda k: rows[:, k * TT:(k + 1) * TT]
                i_st, d_st = r_st.acquire()
                if stages in (21,):
                    e1 = ph.v(lambda e: e.tensor_copy(R(3), ps_sum[:]), after=st_toks + d_st)
                    e2 = ph.v(lambda e: e.tensor_copy(R(4), ps_sq[:]), after=st_toks + d_st + [e1])
                    r_pss.wrote(e2)
                    r_pss.read(i_ps, e1)
                    r_pss.read(i_ps, e2)
                    r_st.wrote(e2)
                    d1 = ph.dma(lambda e, b=b, tt=tt: e.dma_start(out=stats0[b, 0:1, ts(tt, TT)], in_=R(4)), after=[e2], q="gpsimd")
                    d2 = ph.dma(lambda e, b=b, tt=tt: e.dma_start(out=stats0[b, 1:2, ts(tt, TT)], in_=R(3)), after=[e1, e2], q="gpsimd")
                    r_st.read(i_st, d1)
                    r_st.read(i_st, d2)
                    for c in range(NC):
                        st1 = ph.dma(lambda e, b=b, tt=tt, c=c, i=i_fm: e.dma_start(
                            out=x_fm[b, ts(c, 128), ts(tt, TT)],
                            in_=fm[:, i * NC * TT + c * TT: i * NC * TT + (c + 1) * TT]),
                            after=ev_toks[c], q="gpsimd")
                        r_fm.read(i_fm, st1)
                    continue
                v1 = ph.v(lambda e: e.tensor_scalar_mul(R(0), ps_sum[:], -1.0 / D), after=st_toks + d_st)
                s1 = ph.s(lambda e: e.activation(R(1), ps_sum[:], AF.Square, scale=1.0 / D), after=st_toks + d_st + [v1])
                v2 = ph.v(lambda e: e.tensor_scalar(R(2), ps_sq[:], 1.0 / D, LN_EPS, ALU.mult, ALU.add), after=st_toks + d_st)
                v3 = ph.v(lambda e: e.tensor_sub(R(2), R(2), R(1)), after=[v2, s1])
                if stages == 26:
                    s2 = ph.s(lambda e: e.activation(R(3), R(2), AF.Copy), after=[v3])
                    v4 = ph.v(lambda e: e.tensor_copy(R(3), R(3)), after=[s2])
                elif stages == 27:
                    s2 = ph.s(lambda e: e.activation(R(3), R(2), AF.Sqrt), after=[v3])
                    v4 = ph.v(lambda e: e.tensor_copy(R(3), R(3)), after=[s2])
                else:
                    s2 = ph.s(lambda e: e.activation(R(3), R(2), AF.Sqrt), after=[v3])
                    v4 = ph.v(lambda e: e.reciprocal(R(3), R(3)), after=[s2])
                v5 = ph.v(lambda e: e.tensor_mul(R(4), R(0), R(3)), after=[v1, v4])
                r_pss.wrote(v5)
                r_pss.read(i_ps, v1)
                r_pss.read(i_ps, s1)
                r_pss.read(i_ps, v2)
                d1 = ph.dma(lambda e, b=b, tt=tt: e.dma_start(out=stats0[b, 0:1, ts(tt, TT)], in_=R(4)), after=[v5], q="gpsimd")
                d2 = ph.dma(lambda e, b=b, tt=tt: e.dma_start(out=stats0[b, 1:2, ts(tt, TT)], in_=R(3)), after=[v4, v5], q="gpsimd")
                if stages < 3 or stages in (25, 26, 27):
                    for c in range(NC):
                        st1 = ph.dma(lambda e, b=b, tt=tt, c=c, i=i_fm: e.dma_start(
                            out=x_fm[b, ts(c, 128), ts(tt, TT)],
                            in_=fm[:, i * NC * TT + c * TT: i * NC * TT + (c + 1) * TT]),
                            after=ev_toks[c], q="gpsimd")
                        r_fm.read(i_fm, st1)
                    continue
                i_pb, d_pb = r_pb.acquire()
                bc1 = ph.t(lambda e: e.matmul(pb0[:], onr[:], R(3), start=True, stop=True),
                           after=[v4, v5, tC3] + d_pb)
                bc2 = ph.t(lambda e: e.matmul(pb1[:], onr[:], R(4), start=True, stop=True),
                           after=[v5, tC3] + d_pb)
                r_pb.wrote(bc2)
                r_st.wrote(v5)
                for tok in (d1, d2, bc1, bc2):
                    r_st.read(i_st, tok)
                for c in range(NC):
                    st1 = ph.dma(lambda e, b=b, tt=tt, c=c, i=i_fm: e.dma_start(
                        out=x_fm[b, ts(c, 128), ts(tt, TT)],
                        in_=fm[:, i * NC * TT + c * TT: i * NC * TT + (c + 1) * TT]),
                        after=ev_toks[c], q="gpsimd")
                    r_fm.read(i_fm, st1)
                    i_xn, d_xn = r_xn.acquire()
                    x1_ = ph.v(lambda e, c=c, i=i_fm: e.tensor_mul(
                        xnt[:], fm[:, i * NC * TT + c * TT: i * NC * TT + (c + 1) * TT], pb0[:]),
                        after=[bc1] + ev_toks[c] + d_xn)
                    x2_ = ph.v(lambda e: e.tensor_add(xnt[:], xnt[:], pb1[:]),
                               after=[x1_, bc2])
                    r_fm.read(i_fm, x1_)
                    r_pb.read(i_pb, x1_)
                    r_pb.read(i_pb, x2_)
                    st2 = ph.dma(lambda e, b=b, tt=tt, c=c: e.dma_start(
                        out=xf[b, ts(c, 128), ts(tt, TT)], in_=xnt[:]),
                        after=[x2_], q="gpsimd")
                    r_xn.wrote(st2)
        ph.finalize()



def emit_wtrans(mk: MK):
    nc = mk.nc
    jobs = [("L0_attn_Wq", "wqT", 1024, 1024),
            ("L0_attn_Wkv", "wkvT", 512, 1024),
            ("L0_attn_Wout", "woutT", 1024, 1024)]
    ph = Phase(mk, "wtrans")
    with (
        nc.sbuf_tensor("w_idb", [128, 128], BF16) as idb,
        nc.sbuf_tensor("w_in", [128, 2 * 1024], BF16) as win,
        nc.sbuf_tensor("w_out", [128, 2 * 128], BF16) as wout,
        nc.psum_tensor("w_pt0", [128, 128], BF16) as pt0,
        nc.psum_tensor("w_pt1", [128, 128], BF16) as pt1,
    ):
        pt = [pt0, pt1]
        tC = ph.dma(lambda e: e.dma_start(out=idb[:], in_=mk.buf("c_ident_bf16")[:]))
        r_in, r_pt, r_out = Ring(2), Ring(2), Ring(2)
        for wname, oname, R_, C_ in jobs:
            w_h = mk.buf(wname)
            o_h = mk.buf(oname)
            for rc in range(R_ // 128):
                i_in, d_in = r_in.acquire()
                ld = ph.dma(lambda e, rc=rc, i=i_in, w_h=w_h, C_=C_: e.dma_start(
                    out=win[:, i * 1024: i * 1024 + C_], in_=w_h[ts(rc, 128), :]),
                    after=d_in, q="gpsimd")
                r_in.wrote(ld)
                for cc in range(C_ // 128):
                    i_pt, d_pt = r_pt.acquire()
                    tr = ph.t(lambda e, cc=cc, i=i_in, ip=i_pt: e.transpose(
                        pt[ip][:],
                        win[:, i * 1024 + cc * 128: i * 1024 + (cc + 1) * 128], idb[:]),
                        after=[ld, tC] + d_pt)
                    r_pt.wrote(tr)
                    r_in.read(i_in, tr)
                    i_o, d_o = r_out.acquire()
                    ev = ph.v(lambda e, io=i_o, ip=i_pt: e.tensor_copy(
                        wout[:, io * 128:(io + 1) * 128], pt[ip][:]),
                        after=[tr] + d_o)
                    r_pt.read(i_pt, ev)
                    st = ph.dma(lambda e, rc=rc, cc=cc, io=i_o, o_h=o_h: e.dma_start(
                        out=o_h[ts(cc, 128), ts(rc, 128)], in_=wout[:, io * 128:(io + 1) * 128]),
                        after=[ev], q="gpsimd")
                    r_out.wrote(st)
        ph.finalize()



def emit_p1(mk: MK):
    nc = mk.nc
    xf = mk.buf("xf")
    q0, k0, v0t = mk.buf("q0"), mk.buf("k0"), mk.buf("v0t")
    qn0, kn0 = mk.buf("qn0"), mk.buf("kn0")
    wqT_h, wkvT_h = mk.buf("wqT"), mk.buf("wkvT")
    ph = Phase(mk, "p1")
    with (
        nc.sbuf_tensor("p1_wq", [128, NC * D], BF16) as wq,
        nc.sbuf_tensor("p1_wkv", [128, NC * 512], BF16) as wkv,
        nc.sbuf_tensor("p1_tab", [128, 4 * T], F32) as tab,
        nc.sbuf_tensor("p1_idb", [128, 128], BF16) as idb,
        nc.sbuf_tensor("p1_seg", [128, 4], F32) as seg,
        nc.sbuf_tensor("p1_xnb", [128, 2 * NC * TT], BF16) as xnb,
        nc.sbuf_tensor("p1_f", [128, 4 * TT], F32) as fwork,
        nc.sbuf_tensor("p1_qb", [128, TT], BF16) as qb,
        nc.sbuf_tensor("p1_sq", [128, TT], F32) as sqt,
        nc.sbuf_tensor("p1_nr", [4, TT], F32) as nrows,
        nc.sbuf_tensor("p1_vb", [128, TT], BF16) as vb,
        nc.sbuf_tensor("p1_vt", [128, 2 * 128], BF16) as vt,
        nc.psum_tensor("p1_p0", [128, TT], F32) as pj0,
        nc.psum_tensor("p1_p1", [128, TT], F32) as pj1,
        nc.psum_tensor("p1_n0", [4, TT], F32) as pn0,
        nc.psum_tensor("p1_t0", [128, 128], BF16) as pt0,
        nc.psum_tensor("p1_t1", [128, 128], BF16) as pt1,
    ):
        pj = [pj0, pj1]
        ptt = [pt0, pt1]
        TAB = {"aq": 0, "bq": 1, "ak": 2, "bk": 3}

        wtoks = []
        for kc in range(NC):
            wtoks.append(ph.dma(lambda e, kc=kc: e.dma_start(
                out=wq[:, kc * D:(kc + 1) * D], in_=wqT_h[ts(kc, 128), :])))
            wtoks.append(ph.dma(lambda e, kc=kc: e.dma_start(
                out=wkv[:, kc * 512:(kc + 1) * 512], in_=wkvT_h[ts(kc, 128), :])))
        for nm, cn in [("aq", "c_r0_aq"), ("bq", "c_r0_bq"), ("ak", "c_r0_ak"), ("bk", "c_r0_bk")]:
            for half in range(2):
                wtoks.append(ph.dma(lambda e, nm=nm, cn=cn, half=half: e.dma_start(
                    out=tab[half * 64:(half + 1) * 64, TAB[nm] * T:(TAB[nm] + 1) * T],
                    in_=mk.buf(cn)[:]),))
        wtoks.append(ph.dma(lambda e: e.dma_start(out=idb[:], in_=mk.buf("c_ident_bf16")[:])))
        wtoks.append(ph.dma(lambda e: e.dma_start(out=seg[:], in_=mk.buf("c_seg32x4")[:])))
        wL = wtoks

        r_xnb = Ring(2)
        r_pj = Ring(2)
        r_f = Ring(1)
        r_qb = Ring(1)
        r_sq = Ring(1)
        r_nr = Ring(1)
        r_pn = Ring(1)
        r_vb = Ring(1)
        r_pt = Ring(2)
        r_vt = Ring(2)

        def rope_chunk(oc, tt, tloads, is_q, d_pj, i_pj, i_x):
            w_sb, outw = (wq, D) if is_q else (wkv, 512)
            mtoks = []
            for kc in range(NC):
                mtoks.append(ph.t(lambda e, kc=kc, oc=oc, i=i_pj, ix=i_x, w_sb=w_sb, outw=outw: e.matmul(
                    pj[i][:], w_sb[:, kc * outw + oc * 128: kc * outw + (oc + 1) * 128],
                    xnb[:, ix * NC * TT + kc * TT: ix * NC * TT + (kc + 1) * TT],
                    start=(kc == 0), stop=(kc == NC - 1)),
                    after=([tloads[kc]] + wL + d_pj) if kc == 0 else [tloads[kc]]))
            mm = mtoks[-1]
            i_f, d_f = r_f.acquire()
            F = lambda k: fwork[:, k * TT:(k + 1) * TT]
            ev = ph.s(lambda e, i=i_pj: e.copy(F(0), pj[i][:]), after=[mm] + d_f)
            r_pj.read(i_pj, ev)
            sh = ph.v(lambda e: e.stream_shuffle(F(1), F(0), SWAP_MASK), after=[ev])
            ta, tb = ("aq", "bq") if is_q else ("ak", "bk")
            tcol = lambda nm: tab[:, TAB[nm] * T + (tt % NTT) * TT: TAB[nm] * T + (tt % NTT + 1) * TT]
            m1 = ph.v(lambda e: e.tensor_mul(F(2), F(0), tcol(ta)), after=[ev] + wL)
            m2 = ph.v(lambda e: e.tensor_mul(F(3), F(1), tcol(tb)), after=[sh] + wL)
            i_qb, d_qb = r_qb.acquire()
            ad = ph.v(lambda e: e.tensor_add(qb[:], F(2), F(3)), after=[m1, m2] + d_qb)
            r_f.wrote(ad)
            dst = q0 if is_q else k0
            st = ph.dma(lambda e, oc=oc, tt=tt: e.dma_start(
                out=dst[tt // NTT, ts(oc, 128), ts(tt % NTT, TT)], in_=qb[:]), after=[ad])
            r_qb.wrote(st)
            i_sq, d_sq = r_sq.acquire()
            sqk = ph.s(lambda e: e.activation(sqt[:], qb[:], AF.Square), after=[ad] + d_sq)
            r_qb.read(i_qb, sqk)
            r_sq.wrote(sqk)
            i_pn, d_pn = r_pn.acquire()
            nm1 = ph.t(lambda e: e.matmul(pn0[:], seg[:], sqt[:], start=True, stop=True),
                       after=[sqk] + wL + d_pn)
            r_sq.read(i_sq, nm1)
            i_nr, d_nr = r_nr.acquire()
            sq2 = ph.s(lambda e: e.activation(nrows[:], pn0[:], AF.Sqrt), after=[nm1] + d_nr)
            r_pn.wrote(sq2)
            r_pn.read(i_pn, sq2)
            ndst, nrow0 = (qn0, 4 * oc) if is_q else (kn0, 4 * oc)
            stn = ph.dma(lambda e, oc=oc, tt=tt, nrow0=nrow0: e.dma_start(
                out=ndst[tt // NTT, nrow0:nrow0 + 4, ts(tt % NTT, TT)], in_=nrows[:]),
                after=[sq2], q="gpsimd")
            r_nr.wrote(stn)
            return mm

        def v_chunk(oc, b, tt, tloads, i_x):
            i_pj, d_pj = r_pj.acquire()
            mtoks = []
            for kc in range(NC):
                mtoks.append(ph.t(lambda e, kc=kc, oc=oc, i=i_pj, ix=i_x: e.matmul(
                    pj[i][:], wkv[:, kc * 512 + oc * 128: kc * 512 + (oc + 1) * 128],
                    xnb[:, ix * NC * TT + kc * TT: ix * NC * TT + (kc + 1) * TT],
                    start=(kc == 0), stop=(kc == NC - 1)),
                    after=([tloads[kc]] + wL + d_pj) if kc == 0 else [tloads[kc]]))
            mm = mtoks[-1]
            r_pj.wrote(mm)
            r_xnb.read(i_x, mm)
            i_vb, d_vb = r_vb.acquire()
            ev = ph.s(lambda e, i=i_pj: e.copy(vb[:], pj[i][:]), after=[mm] + d_vb)
            r_pj.read(i_pj, ev)
            r_vb.wrote(ev)
            for sub in range(4):
                i_pt, d_pt = r_pt.acquire()
                tr = ph.t(lambda e, sub=sub, ip=i_pt: e.transpose(
                    ptt[ip][:], vb[:, sub * 128:(sub + 1) * 128], idb[:]),
                    after=[ev] + wL + d_pt)
                r_pt.wrote(tr)
                r_vb.read(i_vb, tr)
                i_vt, d_vt = r_vt.acquire()
                evt = ph.v(lambda e, iv=i_vt, ip=i_pt: e.tensor_copy(
                    vt[:, iv * 128:(iv + 1) * 128], ptt[ip][:]), after=[tr] + d_vt)
                r_pt.read(i_pt, evt)
                r_vt.wrote(evt)
                for h2 in range(2):
                    kvh = (oc - 2) * 2 + h2
                    stv = ph.dma(lambda e, b=b, tt=tt, sub=sub, kvh=kvh, h2=h2, iv=i_vt:
                                 e.dma_start(
                        out=v0t[b, kvh, tt * TT + sub * 128: tt * TT + (sub + 1) * 128, :],
                        in_=vt[:, iv * 128 + h2 * 64: iv * 128 + h2 * 64 + 64]),
                        after=[evt])
                    r_vt.read(i_vt, stv)
            return mm

        for b in range(B):
            for tt in range(NTT):
                gt = b * NTT + tt
                i_x, d_x = r_xnb.acquire()
                tloads = []
                for c in range(NC):
                    tloads.append(ph.dma(lambda e, b=b, tt=tt, c=c, i=i_x: e.dma_start(
                        out=xnb[:, i * NC * TT + c * TT: i * NC * TT + (c + 1) * TT],
                        in_=xf[b, ts(c, 128), ts(tt, TT)]), after=d_x, q="gpsimd"))
                r_xnb.wrote(tloads[-1])
                for oc in range(NC):
                    i_pj, d_pj = r_pj.acquire()
                    mm = rope_chunk(oc, gt, tloads, True, d_pj, i_pj, i_x)
                    r_pj.wrote(mm)
                    r_xnb.read(i_x, mm)
                for oc in range(2):
                    i_pj, d_pj = r_pj.acquire()
                    mm = rope_chunk(oc, gt, tloads, False, d_pj, i_pj, i_x)
                    r_pj.wrote(mm)
                    r_xnb.read(i_x, mm)
                for oc in range(2, 4):
                    v_chunk(oc, b, tt, tloads, i_x)
        ph.finalize()



def emit_p2(mk: MK):
    nc = mk.nc
    q0, k0, v0t = mk.buf("q0"), mk.buf("k0"), mk.buf("v0t")
    y0 = mk.buf("y0")
    lam_h = mk.buf("L0_attn_lambda_init")
    rmsw_h = mk.buf("L0_attn_rms_weight")
    ph = Phase(mk, "p2")
    with contextlib.ExitStack() as _st:
        onr = _st.enter_context(nc.sbuf_tensor("p2_onr", [1, 128], F32))
        on64 = _st.enter_context(nc.sbuf_tensor("p2_on64", [64, 1], F32))
        antic = _st.enter_context(nc.sbuf_tensor("p2_mask", [128, 128], F32))
        lamr = _st.enter_context(nc.sbuf_tensor("p2_lamr", [1, 16], F32))
        rmsw = _st.enter_context(nc.sbuf_tensor("p2_rmsw", [1, 64], F32))
        kh = _st.enter_context(nc.sbuf_tensor("p2_kh", [64, 2 * T], BF16))
        qh_sb = _st.enter_context(nc.sbuf_tensor("p2_qh", [64, 2 * T], BF16))
        vtb = _st.enter_context(nc.sbuf_tensor("p2_vt", [128, 2, 16, 65], BF16))
        pT = _st.enter_context(nc.sbuf_tensor("p2_pT", [128, 4 * TT], BF16))
        sdb = _st.enter_context(nc.sbuf_tensor("p2_sd", [128, 2 * 128], F32))
        ysb = _st.enter_context(nc.sbuf_tensor("p2_ys", [65, 2 * 2 * TT], F32))
        yac = _st.enter_context(nc.sbuf_tensor("p2_ya", [64, 2 * TT], F32))
        ytm = _st.enter_context(nc.sbuf_tensor("p2_ytm", [64, TT], F32))
        ytm2 = _st.enter_context(nc.sbuf_tensor("p2_ytm2", [64, TT], F32))
        lamn = _st.enter_context(nc.sbuf_tensor("p2_lamn", [128, 16], F32))
        rmsc = _st.enter_context(nc.sbuf_tensor("p2_rmsc", [64, 1], F32))
        on65 = _st.enter_context(nc.sbuf_tensor("p2_on65", [65, 128], F32))
        ys2 = _st.enter_context(nc.sbuf_tensor("p2_ys2", [64, TT], F32))
        rrow = _st.enter_context(nc.sbuf_tensor("p2_rrow", [1, 2 * TT], F32))
        ynb = _st.enter_context(nc.sbuf_tensor("p2_ynb", [64, 2 * TT], BF16))
        pS0 = _st.enter_context(nc.psum_tensor("p2_s0", [128, TT], F32))
        pS1 = _st.enter_context(nc.psum_tensor("p2_s1", [128, TT], F32))
        pS2 = _st.enter_context(nc.psum_tensor("p2_s2", [128, TT], F32))
        yU1a = _st.enter_context(nc.psum_tensor("p2_y1a", [128, TT], F32))
        yU1b = _st.enter_context(nc.psum_tensor("p2_y1b", [128, TT], F32))
        yU2a = _st.enter_context(nc.psum_tensor("p2_y2a", [128, TT], F32))
        yU2b = _st.enter_context(nc.psum_tensor("p2_y2b", [128, TT], F32))
        pmisc = _st.enter_context(nc.psum_tensor("p2_misc", [128, TT], F32))
        pS = [pS0, pS1, pS2]
        yUs = [[yU1a, yU1b], [yU2a, yU2b]]
        cl = []
        cl.append(ph.dma(lambda e: e.dma_start(out=onr[:], in_=mk.buf("c_ones_row")[:])))
        cl.append(ph.dma(lambda e: e.dma_start(out=on64[:], in_=mk.buf("c_ones64_col")[:])))
        cl.append(ph.dma(lambda e: e.dma_start(out=antic[:], in_=mk.buf("c_anticausal")[:])))
        cl.append(ph.dma(lambda e: e.dma_start(
            out=lamr[:], in_=lam_h.rearrange('(o d) -> o d', o=1))))
        cl.append(ph.dma(lambda e: e.dma_start(
            out=rmsw[:], in_=rmsw_h.rearrange('(o d) -> o d', o=1))))
        sg = ph.s(lambda e: e.activation(lamr[:], lamr[:], AF.Sigmoid), after=cl)
        msv = ph.v(lambda e: e.memset(vtb[:, :, :, 64:65], 1.0))
        ms65 = ph.v(lambda e: e.memset(on65[:], 1.0))
        r_kh, r_qh, r_vt = Ring(2), Ring(2), Ring(2)
        r_pS, r_pT = Ring(3), Ring(4)
        r_sd = Ring(2)
        r_yU = [Ring(2), Ring(2)]
        r_ys, r_misc = Ring(2), Ring(1)
        r_ya, r_ytm, r_yt2, r_ys2, r_rr, r_yn = (
            Ring(1), Ring(1), Ring(1), Ring(1), Ring(1), Ring(2))
        i_m, d_m = r_misc.acquire()
        bl0 = ph.t(lambda e: e.matmul(pmisc[0:128, 0:16], onr[:], lamr[:],
                                      start=True, stop=True), after=[sg] + cl + d_m)
        r_misc.wrote(bl0)
        lvv = ph.v(lambda e: e.tensor_scalar(lamn[:], pmisc[0:128, 0:16], -1.0, None,
                                             ALU.mult), after=[bl0])
        r_misc.read(i_m, lvv)
        i_m, d_m = r_misc.acquire()
        tr0 = ph.t(lambda e: e.matmul(pmisc[0:64, 0:1], rmsw[:], onr[0:1, 0:1],
                                      start=True, stop=True), after=cl + d_m + [lvv])
        r_misc.wrote(tr0)
        rmc = ph.s(lambda e: e.copy(rmsc[:], pmisc[0:64, 0:1]), after=[tr0])
        r_misc.read(i_m, rmc)

        def emit_epi(st):
            (b, qh, qc, pv_last, iy1, iy2) = st
            i_ys, d_ys = r_ys.acquire()
            ev1 = ph.s(lambda e, i=i_ys: e.copy(ysb[:, 2 * i * TT:(2 * i + 1) * TT],
                                                yUs[0][iy1][0:65, :]),
                       after=[pv_last[0]] + d_ys)
            ev2 = ph.s(lambda e, i=i_ys: e.copy(ysb[:, (2 * i + 1) * TT:(2 * i + 2) * TT],
                                                yUs[1][iy2][0:65, :]),
                       after=[pv_last[1]] + d_ys)
            r_yU[0].read(iy1, ev1)
            r_yU[1].read(iy2, ev2)
            r_ys.wrote(ev2)
            i_m, d_m = r_misc.acquire()
            bc1 = ph.t(lambda e, i=i_ys: e.matmul(
                pmisc[0:64, :], on65[64:65, 0:64],
                ysb[64:65, 2 * i * TT:(2 * i + 1) * TT],
                start=True, stop=True), after=[ev1, ms65] + cl + d_m)
            r_misc.wrote(bc1)
            i_yt, d_yt = r_ytm.acquire()
            rb1 = ph.v(lambda e: e.reciprocal(ytm[:], pmisc[0:64, :]),
                       after=[bc1] + d_yt)
            r_misc.read(i_m, rb1)
            r_ytm.wrote(rb1)
            i_ya, d_ya = r_ya.acquire()
            v1 = ph.v(lambda e, i=i_ys, ia=i_ya: e.tensor_mul(
                yac[:, ia * TT:(ia + 1) * TT],
                ysb[0:64, 2 * i * TT:(2 * i + 1) * TT], ytm[:]),
                after=[rb1, ev1] + d_ya)
            r_ytm.read(i_yt, v1)
            r_ys.read(i_ys, v1)
            r_ya.wrote(v1)
            i_m, d_m = r_misc.acquire()
            bc2 = ph.t(lambda e, i=i_ys: e.matmul(
                pmisc[0:64, :], on65[64:65, 0:64],
                ysb[64:65, (2 * i + 1) * TT:(2 * i + 2) * TT],
                start=True, stop=True), after=[ev2, rb1, ms65] + cl + d_m)
            r_misc.wrote(bc2)
            i_yt, d_yt = r_ytm.acquire()
            rb2 = ph.v(lambda e: e.reciprocal(ytm[:], pmisc[0:64, :]),
                       after=[bc2] + d_yt)
            r_misc.read(i_m, rb2)
            r_ytm.wrote(rb2)
            i_y2t, d_y2t = r_yt2.acquire()
            v2 = ph.v(lambda e, i=i_ys, qh=qh: e.scalar_tensor_tensor(
                ytm2[:], ytm[:], lamn[0:64, qh:qh + 1],
                ysb[0:64, (2 * i + 1) * TT:(2 * i + 2) * TT],
                ALU.mult, ALU.mult), after=[rb2, ev2, lvv] + d_y2t)
            r_ytm.read(i_yt, v2)
            r_ys.read(i_ys, v2)
            r_yt2.wrote(v2)
            v3 = ph.v(lambda e, ia=i_ya: e.tensor_add(
                yac[:, ia * TT:(ia + 1) * TT], yac[:, ia * TT:(ia + 1) * TT],
                ytm2[:]), after=[v1, v2])
            r_yt2.read(i_y2t, v3)
            r_ya.wrote(v3)
            i_y2s, d_y2s = r_ys2.acquire()
            sqv = ph.v(lambda e, ia=i_ya: e.tensor_mul(
                ys2[:], yac[:, ia * TT:(ia + 1) * TT],
                yac[:, ia * TT:(ia + 1) * TT]), after=[v3] + d_y2s)
            r_ys2.wrote(sqv)
            i_m, d_m = r_misc.acquire()
            nm_ = ph.t(lambda e: e.matmul(pmisc[0:1, 0:TT], on64[:], ys2[:],
                                          start=True, stop=True),
                       after=[sqv] + cl + d_m)
            r_misc.wrote(nm_)
            r_ys2.read(i_y2s, nm_)
            i_rw, d_rw = r_rr.acquire()
            evr = ph.s(lambda e: e.copy(rrow[0:1, 0:TT], pmisc[0:1, 0:TT]),
                       after=[nm_] + d_rw)
            r_misc.read(i_m, evr)
            r_rr.wrote(evr)
            i_m, d_m = r_misc.acquire()
            bc3 = ph.t(lambda e: e.matmul(pmisc[0:64, :], onr[0:1, 0:64],
                                          rrow[0:1, 0:TT],
                                          start=True, stop=True),
                       after=[evr] + cl + d_m)
            r_misc.wrote(bc3)
            r_rr.read(i_rw, bc3)
            i_yt, d_yt = r_ytm.acquire()
            u_ = ph.v(lambda e: e.tensor_scalar(ytm[:], pmisc[0:64, :],
                                                1.0 / 64, 1e-6, ALU.mult, ALU.add),
                      after=[bc3] + d_yt)
            r_misc.read(i_m, u_)
            r_ytm.wrote(u_)
            i_y2t, d_y2t = r_yt2.acquire()
            sq2 = ph.s(lambda e: e.activation(ytm2[:], ytm[:], AF.Sqrt),
                       after=[u_] + d_y2t)
            r_ytm.read(i_yt, sq2)
            rsq = ph.v(lambda e: e.reciprocal(ytm2[:], ytm2[:]), after=[sq2])
            r_yt2.wrote(rsq)
            i_yn, d_yn = r_yn.acquire()
            yn = ph.v(lambda e, ia=i_ya, iy=i_yn: e.scalar_tensor_tensor(
                ynb[:, iy * TT:(iy + 1) * TT], ytm2[:], rmsc[0:64, 0:1],
                yac[:, ia * TT:(ia + 1) * TT], ALU.mult, ALU.mult),
                after=[rsq, v3, rmc] + d_yn)
            r_yt2.read(i_y2t, yn)
            r_ya.read(i_ya, yn)
            r_yn.wrote(yn)
            sty = ph.dma(lambda e, b=b, qh=qh, qc=qc, iy=i_yn: e.dma_start(
                out=y0[b, qh * 64:(qh + 1) * 64, ts(qc, TT)],
                in_=ynb[:, iy * TT:(iy + 1) * TT]), after=[yn])
            r_yn.read(i_yn, sty)

        pending = None
        for b in range(B):
            for qh in range(16):
                kvh = qh // 4
                if qh % 4 == 0:
                    i_kh, d_kh = r_kh.acquire()
                    lk = ph.dma(lambda e, b=b, kvh=kvh, i=i_kh: e.dma_start(
                        out=kh[:, i * T:(i + 1) * T],
                        in_=k0[b, kvh * 64:(kvh + 1) * 64, :]), after=d_kh)
                    r_kh.wrote(lk)
                    i_vt, d_vt = r_vt.acquire()
                    lv = ph.dma(lambda e, b=b, kvh=kvh, i=i_vt: e.dma_start(
                        out=vtb[:, i, :, 0:64],
                        in_=v0t[b, kvh].rearrange("(n p) d -> p n d", p=128)), after=d_vt)
                    r_vt.wrote(lv)
                i_qh, d_qh = r_qh.acquire()
                lq = ph.dma(lambda e, b=b, qh=qh, i=i_qh: e.dma_start(
                    out=qh_sb[:, i * T:(i + 1) * T],
                    in_=q0[b, qh * 64:(qh + 1) * 64, :]), after=d_qh)
                r_qh.wrote(lq)
                for qc in range(4):
                    nkt = 4 * (qc + 1)
                    i_y1, d_y1 = r_yU[0].acquire()
                    i_y2, d_y2 = r_yU[1].acquire()
                    d_y = [d_y1, d_y2]
                    iy = [i_y1, i_y2]
                    units = [(kt, s_) for kt in range(nkt) for s_ in range(2)]
                    ex_t = {}
                    pt_i = {}

                    def emit_sc(u, qc=qc, i_kh=i_kh, i_qh=i_qh, lk=lk, lq=lq,
                                ex_t=ex_t, pt_i=pt_i):
                        kt, s_ = u
                        i_ps, d_ps = r_pS.acquire()
                        sc = ph.t(lambda e, kt=kt, s_=s_, qc=qc, ik=i_kh, iq=i_qh,
                                  ip=i_ps: e.matmul(
                            pS[ip][:],
                            kh[s_ * 32:(s_ + 1) * 32, ik * T + kt * 128: ik * T + (kt + 1) * 128],
                            qh_sb[s_ * 32:(s_ + 1) * 32,
                                  iq * T + qc * TT: iq * T + (qc + 1) * TT],
                            start=True, stop=True, tile_position=(s_ * 32, 0)),
                            after=[lk, lq] + d_ps)
                        r_pS.wrote(sc)
                        r_kh.read(i_kh, sc)
                        r_qh.read(i_qh, sc)
                        i_pt, d_pt = r_pT.acquire()
                        nmask = kt - 4 * qc
                        exu = []
                        if nmask > 0:
                            mz = ph.v(lambda e, it=i_pt, nm=nmask: e.memset(
                                pT[:, it * TT: it * TT + nm * 128], 0.0), after=d_pt)
                            exu.append(mz)
                        if 0 <= nmask <= 3:
                            i_sd, d_sd = r_sd.acquire()
                            ad = ph.v(lambda e, ip=i_ps, isd=i_sd, nm=max(nmask, 0): e.tensor_add(
                                sdb[:, isd * 128:(isd + 1) * 128],
                                pS[ip][:, nm * 128:(nm + 1) * 128], antic[:]),
                                after=[sc] + cl + d_sd)
                            r_sd.wrote(ad)
                            ed = ph.s(lambda e, it=i_pt, isd=i_sd, nm=max(nmask, 0): e.activation(
                                pT[:, it * TT + nm * 128: it * TT + (nm + 1) * 128],
                                sdb[:, isd * 128:(isd + 1) * 128], AF.Exp),
                                after=[ad] + d_pt)
                            r_sd.read(i_sd, ed)
                            exu.append(ed)
                            nfull0 = nmask + 1
                        else:
                            nfull0 = 0
                        if nfull0 < 4:
                            ex = ph.s(lambda e, ip=i_ps, it=i_pt, n0=nfull0: e.activation(
                                pT[:, it * TT + n0 * 128: (it + 1) * TT],
                                pS[ip][:, n0 * 128: TT], AF.Exp), after=[sc] + d_pt)
                            exu.append(ex)
                        for t_ in exu:
                            r_pS.read(i_ps, t_)
                            r_pT.wrote(t_)
                        ex_t[u] = exu
                        pt_i[u] = i_pt

                    def emit_pv(u, nkt=nkt, i_vt=i_vt, d_y=d_y, iy=iy, lv=lv,
                                ex_t=ex_t, pt_i=pt_i):
                        kt, s_ = u
                        pv = ph.t(lambda e, kt=kt, s_=s_, iv=i_vt, it=pt_i[u],
                                  ib=iy[s_]: e.matmul(
                            yUs[s_][ib][0:65, :], vtb[:, iv, kt, :],
                            pT[:, it * TT:(it + 1) * TT],
                            start=(kt == 0), stop=(kt == nkt - 1)),
                            after=(ex_t[u] + [lv, msv] + d_y[s_]) if kt == 0 else ex_t[u])
                        r_pT.read(pt_i[u], pv)
                        r_vt.read(i_vt, pv)
                        return pv

                    emit_sc(units[0])
                    emit_sc(units[1])
                    pv_last = [None, None]
                    for iu in range(2, len(units)):
                        emit_sc(units[iu])
                        u = units[iu - 2]
                        pv_last[u[1]] = emit_pv(u)
                    for u in units[-2:]:
                        pv_last[u[1]] = emit_pv(u)
                    r_yU[0].wrote(pv_last[0])
                    r_yU[1].wrote(pv_last[1])
                    if pending is not None:
                        emit_epi(pending)
                    pending = (b, qh, qc, pv_last, i_y1, i_y2)
        emit_epi(pending)
        ph.finalize()



def emit_p3(mk: MK):
    nc = mk.nc
    y0, x_fm, x1 = mk.buf("y0"), mk.buf("x_fm"), mk.buf("x1")
    woutT_h = mk.buf("woutT")
    ph = Phase(mk, "p3")
    with contextlib.ExitStack() as _st:
        wo = _st.enter_context(nc.sbuf_tensor("p3_wo", [128, NC * D], BF16))
        yb = _st.enter_context(nc.sbuf_tensor("p3_yb", [128, 2 * NC * TT], BF16))
        xb = _st.enter_context(nc.sbuf_tensor("p3_xb", [128, 2 * TT], F32))
        ob = _st.enter_context(nc.sbuf_tensor("p3_ob", [128, 2 * TT], F32))
        pj0 = _st.enter_context(nc.psum_tensor("p3_p0", [128, TT], F32))
        pj1 = _st.enter_context(nc.psum_tensor("p3_p1", [128, TT], F32))
        pj = [pj0, pj1]
        wtoks = []
        for kc in range(NC):
            wtoks.append(ph.dma(lambda e, kc=kc: e.dma_start(
                out=wo[:, kc * D:(kc + 1) * D], in_=woutT_h[ts(kc, 128), :])))
        r_y, r_x, r_o, r_pj = Ring(2), Ring(2), Ring(2), Ring(2)
        for b in range(B):
            for tt in range(NTT):
                i_y, d_y = r_y.acquire()
                yloads = []
                for c in range(NC):
                    yloads.append(ph.dma(lambda e, b=b, tt=tt, c=c, i=i_y: e.dma_start(
                        out=yb[:, i * NC * TT + c * TT: i * NC * TT + (c + 1) * TT],
                        in_=y0[b, ts(c, 128), ts(tt, TT)]), after=d_y))
                r_y.wrote(yloads[-1])
                for oc in range(NC):
                    i_x, d_x = r_x.acquire()
                    lx = ph.dma(lambda e, b=b, tt=tt, oc=oc, i=i_x: e.dma_start(
                        out=xb[:, i * TT:(i + 1) * TT],
                        in_=x_fm[b, ts(oc, 128), ts(tt, TT)]), after=d_x, q="gpsimd")
                    r_x.wrote(lx)
                    i_pj, d_pj = r_pj.acquire()
                    mt = []
                    for kc in range(NC):
                        mt.append(ph.t(lambda e, kc=kc, oc=oc, i=i_pj, iy=i_y: e.matmul(
                            pj[i][:], wo[:, kc * D + oc * 128: kc * D + (oc + 1) * 128],
                            yb[:, iy * NC * TT + kc * TT: iy * NC * TT + (kc + 1) * TT],
                            start=(kc == 0), stop=(kc == NC - 1)),
                            after=([yloads[kc]] + wtoks + d_pj) if kc == 0 else [yloads[kc]]))
                    mm = mt[-1]
                    r_pj.wrote(mm)
                    r_y.read(i_y, mm)
                    i_o, d_o = r_o.acquire()
                    ad = ph.v(lambda e, i=i_pj, io=i_o, ix=i_x: e.tensor_add(
                        ob[:, io * TT:(io + 1) * TT], pj[i][:], xb[:, ix * TT:(ix + 1) * TT]),
                        after=[mm, lx] + d_o)
                    r_pj.read(i_pj, ad)
                    r_x.read(i_x, ad)
                    r_o.wrote(ad)
                    st = ph.dma(lambda e, b=b, tt=tt, oc=oc, io=i_o: e.dma_start(
                        out=x1[b, ts(oc, 128), ts(tt, TT)], in_=ob[:, io * TT:(io + 1) * TT]),
                        after=[ad])
                    r_o.read(i_o, st)
        ph.finalize()



def _p4_s1(mk, l, src_name):
    nc = mk.nc
    src = mk.buf(src_name)
    fq, fk, fvt = mk.buf("fq"), mk.buf("fk"), mk.buf("fvt")
    wqkv_h = mk.buf("L0_ffn_wqkv")
    qs_h, ks_h = mk.buf("L0_ffn_q_scale"), mk.buf("L0_ffn_k_scale")
    ph = Phase(mk, f"p4l{l}s1")
    with contextlib.ExitStack() as _st:
        wq = _st.enter_context(nc.sbuf_tensor(f"s1_wq_{l}", [128, NC * 1536], BF16))
        wq2 = _st.enter_context(nc.sbuf_tensor(f"s1_wq2_{l}", [128, NC * 1024], BF16))
        tabs = _st.enter_context(nc.sbuf_tensor(f"s1_tab_{l}", [128, 4 * T], F32))
        idb = _st.enter_context(nc.sbuf_tensor(f"s1_idb_{l}", [128, 128], BF16))
        seg = _st.enter_context(nc.sbuf_tensor(f"s1_seg_{l}", [128, 2], F32))
        eq = _st.enter_context(nc.sbuf_tensor(f"s1_eq_{l}", [2, 2 * 128], F32))
        qsrow = _st.enter_context(nc.sbuf_tensor(f"s1_qsr_{l}", [1, 256], F32))
        xfb = _st.enter_context(nc.sbuf_tensor(f"s1_xfb_{l}", [128, NC * T], BF16))
        fw = _st.enter_context(nc.sbuf_tensor(f"s1_f_{l}", [128, 5 * TT], F32))
        nr = _st.enter_context(nc.sbuf_tensor(f"s1_nr_{l}", [2, 2 * TT], F32))
        qb = _st.enter_context(nc.sbuf_tensor(f"s1_qb_{l}", [128, TT], BF16))
        vb = _st.enter_context(nc.sbuf_tensor(f"s1_vb_{l}", [128, TT], BF16))
        vt = _st.enter_context(nc.sbuf_tensor(f"s1_vt_{l}", [128, 2 * 128], BF16))
        pj0 = _st.enter_context(nc.psum_tensor(f"s1_p0_{l}", [128, TT], F32))
        pj1 = _st.enter_context(nc.psum_tensor(f"s1_p1_{l}", [128, TT], F32))
        pn = _st.enter_context(nc.psum_tensor(f"s1_pn_{l}", [2, TT], F32))
        psc = _st.enter_context(nc.psum_tensor(f"s1_psc_{l}", [128, TT], F32))
        pt0 = _st.enter_context(nc.psum_tensor(f"s1_pt0_{l}", [128, 128], BF16))
        pt1 = _st.enter_context(nc.psum_tensor(f"s1_pt1_{l}", [128, 128], BF16))
        pj = [pj0, pj1]
        ptt = [pt0, pt1]
        TAB = {"aq": 0, "bq": 1, "ak": 2, "bk": 3}
        cl = []
        for kc in range(NC):
            cl.append(ph.dma(lambda e, kc=kc: e.dma_start(
                out=wq[:, kc * 1536:(kc + 1) * 1536], in_=wqkv_h[l, ts(kc, 128), :]),
                q="gpsimd"))
        for nm, cn in [("aq", "c_xp_aq"), ("bq", "c_xp_bq"), ("ak", "c_xp_ak"), ("bk", "c_xp_bk")]:
            for half in range(2):
                cl.append(ph.dma(lambda e, nm=nm, cn=cn, half=half: e.dma_start(
                    out=tabs[half * 64:(half + 1) * 64, TAB[nm] * T:(TAB[nm] + 1) * T],
                    in_=mk.buf(cn)[:])))
        cl.append(ph.dma(lambda e: e.dma_start(out=idb[:], in_=mk.buf("c_ident_bf16")[:])))
        cl.append(ph.dma(lambda e: e.dma_start(out=seg[:], in_=mk.buf("c_seg64")[:])))
        for kc in range(NC):
            cl.append(ph.v(lambda e, kc=kc: e.tensor_copy(
                wq2[:, kc * 1024:(kc + 1) * 1024],
                wq[:, kc * 1536: kc * 1536 + 1024].rearrange("p (g h m) -> p g m h", g=16, h=2)),
                after=[cl[kc]]))
        mz = ph.g(lambda e: e.memset(eq[:], 0.0))
        lq_ = ph.dma(lambda e: e.dma_start(
            out=qsrow[0:1, 0:64], in_=qs_h[l].rearrange("(o d) -> o d", o=1)))
        lk_ = ph.dma(lambda e: e.dma_start(
            out=qsrow[0:1, 64:128], in_=ks_h[l].rearrange("(o d) -> o d", o=1)))
        pq = ph.v(lambda e: e.tensor_copy(
            qsrow[0:1, 128:192], qsrow[0:1, 0:64].rearrange("o (h m) -> o m h", h=2)),
            after=[lq_])
        pk = ph.v(lambda e: e.tensor_copy(
            qsrow[0:1, 192:256], qsrow[0:1, 64:128].rearrange("o (h m) -> o m h", h=2)),
            after=[lk_])
        for g in range(2):
            cl.append(ph.dma(lambda e, g=g: e.dma_start(
                out=eq[g:g + 1, g * 64: g * 64 + 64], in_=qsrow[0:1, 128:192]),
                after=[mz, pq], q="gpsimd"))
            cl.append(ph.dma(lambda e, g=g: e.dma_start(
                out=eq[g:g + 1, 128 + g * 64: 128 + g * 64 + 64], in_=qsrow[0:1, 192:256]),
                after=[mz, pk], q="gpsimd"))
        r_pj, r_f, r_qb, r_nr, r_pn, r_psc = Ring(2), Ring(1), Ring(1), Ring(1), Ring(1), Ring(1)
        r_vb, r_pt, r_vt = Ring(1), Ring(2), Ring(2)
        r_xfb = Ring(1)
        F = lambda k: fw[:, k * TT:(k + 1) * TT]

        def qk_chunk(b, tt, oc, is_q, xtoks, i_x):
            i_pj, d_pj = r_pj.acquire()
            mt = []
            for kc in range(NC):
                sh = 0 if kc < 4 else -1
                c0 = tt * TT + sh
                wsl = lambda kc, oc: wq2[:, kc * 1024 + oc * 128: kc * 1024 + (oc + 1) * 128]
                if tt == 0 and kc >= 4:
                    mt.append(ph.t(lambda e, kc=kc, oc=oc, i=i_pj, ix=i_x: e.matmul(
                        pj[i][:, 1:TT], wsl(kc, oc),
                        xfb[:, ix * NC * T + kc * T: ix * NC * T + kc * T + TT - 1],
                        start=False, stop=(kc == NC - 1)),
                        after=[xtoks[kc]]))
                else:
                    mt.append(ph.t(lambda e, kc=kc, oc=oc, i=i_pj, ix=i_x, c0=c0: e.matmul(
                        pj[i][:], wsl(kc, oc),
                        xfb[:, ix * NC * T + kc * T + c0: ix * NC * T + kc * T + c0 + TT],
                        start=(kc == 0), stop=(kc == NC - 1)),
                        after=([xtoks[kc]] + cl + d_pj) if kc == 0 else [xtoks[kc]]))
            mm = mt[-1]
            r_pj.wrote(mm)
            i_f, d_f = r_f.acquire()
            ev = ph.s(lambda e, i=i_pj: e.copy(F(0), pj[i][:]), after=[mm] + d_f)
            r_pj.read(i_pj, ev)
            sq = ph.s(lambda e: e.activation(F(1), F(0), AF.Square), after=[ev])
            i_pn, d_pn = r_pn.acquire()
            nm1 = ph.t(lambda e: e.matmul(pn[:], seg[:], F(1), start=True, stop=True),
                       after=[sq] + cl + d_pn)
            r_pn.wrote(nm1)
            NR = lambda k: nr[:, k * TT:(k + 1) * TT]
            i_nr, d_nr = r_nr.acquire()
            s1_ = ph.s(lambda e: e.activation(NR(0), pn[:], AF.Sqrt), after=[nm1] + d_nr)
            r_pn.read(i_pn, s1_)
            c1 = ph.v(lambda e: e.tensor_scalar_max(NR(0), NR(0), 1e-12), after=[s1_])
            c2 = ph.v(lambda e: e.reciprocal(NR(0), NR(0)), after=[c1])
            r_nr.wrote(c2)
            i_ps, d_ps = r_psc.acquire()
            off = 0 if is_q else 128
            bc = ph.t(lambda e, off=off: e.matmul(psc[:], eq[:, off:off + 128], NR(0),
                                                  start=True, stop=True),
                      after=[c2] + cl + d_ps)
            r_psc.wrote(bc)
            r_nr.read(i_nr, bc)
            ln = ph.v(lambda e: e.tensor_mul(F(2), F(0), psc[:]), after=[bc, ev])
            r_psc.read(i_ps, ln)
            sh2 = ph.v(lambda e: e.stream_shuffle(F(3), F(2), SWAP_MASK), after=[ln])
            ta, tb_ = ("aq", "bq") if is_q else ("ak", "bk")
            tc = lambda nm: tabs[:, TAB[nm] * T + tt * TT: TAB[nm] * T + (tt + 1) * TT]
            x1_ = ph.v(lambda e: e.tensor_mul(F(4), F(2), tc(ta)), after=[sh2] + cl)
            i_qb, d_qb = r_qb.acquire()
            x2_ = ph.v(lambda e: e.scalar_tensor_tensor(
                F(3), F(3), 1.0, tc(tb_), ALU.mult, ALU.mult), after=[sh2] + cl)
            x3_ = ph.v(lambda e: e.tensor_add(qb[:], F(4), F(3)), after=[x1_, x2_] + d_qb)
            r_f.wrote(x3_)
            dst = fq if is_q else fk
            soc = oc if is_q else oc - 4
            st = ph.dma(lambda e, b=b, tt=tt, soc=soc, dst=dst: e.dma_start(
                out=dst[b, ts(soc, 128), ts(tt, TT)], in_=qb[:]), after=[x3_])
            r_qb.wrote(st)
            return mm

        def v_chunk4(b, tt, oc, xtoks, i_x):
            i_pj, d_pj = r_pj.acquire()
            mt = []
            for kc in range(NC):
                if tt == 0 and kc >= 4:
                    mt.append(ph.t(lambda e, kc=kc, oc=oc, i=i_pj, ix=i_x: e.matmul(
                        pj[i][:, 1:TT], wq[:, kc * 1536 + (8 + oc) * 128: kc * 1536 + (9 + oc) * 128],
                        xfb[:, ix * NC * T + kc * T: ix * NC * T + kc * T + TT - 1],
                        start=False, stop=(kc == NC - 1)), after=[xtoks[kc]]))
                else:
                    sh = 0 if kc < 4 else -1
                    c0 = tt * TT + sh
                    mt.append(ph.t(lambda e, kc=kc, oc=oc, i=i_pj, ix=i_x, c0=c0: e.matmul(
                        pj[i][:], wq[:, kc * 1536 + (8 + oc) * 128: kc * 1536 + (9 + oc) * 128],
                        xfb[:, ix * NC * T + kc * T + c0: ix * NC * T + kc * T + c0 + TT],
                        start=(kc == 0), stop=(kc == NC - 1)),
                        after=([xtoks[kc]] + cl + d_pj) if kc == 0 else [xtoks[kc]]))
            mm = mt[-1]
            r_pj.wrote(mm)
            i_vb, d_vb = r_vb.acquire()
            ev = ph.s(lambda e, i=i_pj: e.copy(vb[:], pj[i][:]), after=[mm] + d_vb)
            r_pj.read(i_pj, ev)
            r_vb.wrote(ev)
            for sub in range(4):
                i_pt, d_pt = r_pt.acquire()
                tr = ph.t(lambda e, sub=sub, ip=i_pt: e.transpose(
                    ptt[ip][:], vb[:, sub * 128:(sub + 1) * 128], idb[:]),
                    after=[ev] + cl + d_pt)
                r_pt.wrote(tr)
                r_vb.read(i_vb, tr)
                i_vt, d_vt = r_vt.acquire()
                evt = ph.v(lambda e, iv=i_vt, ip=i_pt: e.tensor_copy(
                    vt[:, iv * 128:(iv + 1) * 128], ptt[ip][:]), after=[tr] + d_vt)
                r_pt.read(i_pt, evt)
                r_vt.wrote(evt)
                for h2 in range(2):
                    hh = oc * 2 + h2
                    stv = ph.dma(lambda e, b=b, tt=tt, sub=sub, hh=hh, h2=h2, iv=i_vt:
                                 e.dma_start(
                        out=fvt[b, hh, tt * TT + sub * 128: tt * TT + (sub + 1) * 128, :],
                        in_=vt[:, iv * 128 + h2 * 64: iv * 128 + h2 * 64 + 64]),
                        after=[evt])
                    r_vt.read(i_vt, stv)
            return mm

        for b in range(B):
            i_x, d_x = r_xfb.acquire()
            xtoks = []
            for c in range(NC):
                xtoks.append(ph.dma(lambda e, b=b, c=c: e.dma_start(
                    out=xfb[:, c * T:(c + 1) * T], in_=src[b, ts(c, 128), :]),
                    after=d_x, q="gpsimd"))
            for tt in range(NTT):
                for oc in range(4):
                    mm = qk_chunk(b, tt, oc, True, xtoks, 0)
                    r_xfb.read(i_x, mm)
                for oc in range(4, 8):
                    mm = qk_chunk(b, tt, oc, False, xtoks, 0)
                    r_xfb.read(i_x, mm)
                for oc in range(4):
                    mmv = v_chunk4(b, tt, oc, xtoks, 0)
                    r_xfb.read(i_x, mmv)
            r_xfb.wrote(xtoks[-1])
        ph.finalize()


def _p4_s2(mk, l):
    nc = mk.nc
    fq, fk, fvt, fo = mk.buf("fq"), mk.buf("fk"), mk.buf("fvt"), mk.buf("fo")
    ph = Phase(mk, f"p4l{l}s2")
    W = 512
    with contextlib.ExitStack() as _st:
        idb = _st.enter_context(nc.sbuf_tensor(f"s2_idb_{l}", [128, 128], BF16))
        mc = _st.enter_context(nc.sbuf_tensor(f"s2_mc_{l}", [128, 128], F32))
        ma = _st.enter_context(nc.sbuf_tensor(f"s2_ma_{l}", [128, 128], F32))
        qhb = _st.enter_context(nc.sbuf_tensor(f"s2_qh_{l}", [64, T], BF16))
        khb = _st.enter_context(nc.sbuf_tensor(f"s2_kh_{l}", [64, T], BF16))
        vtb = _st.enter_context(nc.sbuf_tensor(f"s2_vt_{l}", [128, 16 * 64], BF16))
        ssb = _st.enter_context(nc.sbuf_tensor(f"s2_s_{l}", [128, 640], F32))
        psb = _st.enter_context(nc.sbuf_tensor(f"s2_p_{l}", [128, 640], BF16))
        rr = _st.enter_context(nc.sbuf_tensor(f"s2_rr_{l}", [128, 2], F32))
        dTt = _st.enter_context(nc.sbuf_tensor(f"s2_dT_{l}", [128, 2 * 128], BF16))
        yb = _st.enter_context(nc.sbuf_tensor(f"s2_yb_{l}", [64, 128], BF16))
        psA = _st.enter_context(nc.psum_tensor(f"s2_pA_{l}", [128, TT], F32))
        psB = _st.enter_context(nc.psum_tensor(f"s2_pB_{l}", [128, TT], F32))
        pt0 = _st.enter_context(nc.psum_tensor(f"s2_pt0_{l}", [128, 128], BF16))
        pt1 = _st.enter_context(nc.psum_tensor(f"s2_pt1_{l}", [128, 128], BF16))
        yT = _st.enter_context(nc.psum_tensor(f"s2_yT_{l}", [64, 128], F32))
        ptt = [pt0, pt1]
        cl = []
        cl.append(ph.dma(lambda e: e.dma_start(out=idb[:], in_=mk.buf("c_ident_bf16")[:])))
        cl.append(ph.dma(lambda e: e.dma_start(out=mc[:], in_=mk.buf("c_causal")[:])))
        cl.append(ph.dma(lambda e: e.dma_start(out=ma[:], in_=mk.buf("c_anticausal")[:])))
        r_q, r_k, r_v = Ring(1), Ring(1), Ring(1)
        r_sA, r_sB, r_ss, r_ps, r_rr = Ring(1), Ring(1), Ring(1), Ring(1), Ring(1)
        r_pt, r_dT, r_yT, r_yb = Ring(2), Ring(2), Ring(1), Ring(1)
        for b in range(B):
            for h in range(8):
                i_q, d_q = r_q.acquire()
                lq = ph.dma(lambda e, b=b, h=h: e.dma_start(
                    out=qhb[:], in_=fq[b, h * 64:(h + 1) * 64, :]), after=d_q)
                r_q.wrote(lq)
                i_k, d_k = r_k.acquire()
                lk = ph.dma(lambda e, b=b, h=h: e.dma_start(
                    out=khb[:], in_=fk[b, h * 64:(h + 1) * 64, :]), after=d_k)
                r_k.wrote(lk)
                i_v, d_v = r_v.acquire()
                lv = ph.dma(lambda e, b=b, h=h: e.dma_start(
                    out=vtb[:].rearrange("p (n d) -> p n d", n=16),
                    in_=fvt[b, h].rearrange("(n p) d -> p n d", p=128)), after=d_v)
                r_v.wrote(lv)
                for w in range(4):
                    for qt in range(4):
                        r0 = qt * 128
                        wq0 = w * W + r0
                        wA = (W - r0) if w > 0 else 0
                        wB = r0 + 128
                        wtot = wA + wB
                        i_sA, d_sA = r_sA.acquire()
                        toks_mm = []
                        if wA > 0:
                            mA = ph.t(lambda e, wq0=wq0, w=w, r0=r0, wA=wA: e.matmul(
                                psA[:, 0:wA], qhb[:, wq0:wq0 + 128],
                                khb[:, (w - 1) * W + r0: w * W],
                                start=True, stop=True), after=[lq, lk] + d_sA)
                            toks_mm.append(mA)
                            r_sA.wrote(mA)
                        i_sB, d_sB = r_sB.acquire()
                        mB = ph.t(lambda e, wq0=wq0, w=w, wB=wB: e.matmul(
                            psB[:, 0:wB], qhb[:, wq0:wq0 + 128],
                            khb[:, w * W: w * W + wB],
                            start=True, stop=True), after=[lq, lk] + d_sB)
                        toks_mm.append(mB)
                        r_sB.wrote(mB)
                        r_q.read(i_q, mB)
                        r_k.read(i_k, mB)
                        i_ss, d_ss = r_ss.acquire()
                        ev_toks = []
                        if wA > 0:
                            e1 = ph.v(lambda e: e.tensor_add(ssb[:, 0:128], psA[:, 0:128], ma[:]),
                                      after=[toks_mm[0]] + cl + d_ss)
                            ev_toks.append(e1)
                            if wA > 128:
                                e2 = ph.s(lambda e, wA=wA: e.copy(ssb[:, 128:wA], psA[:, 128:wA]),
                                          after=[toks_mm[0], e1] + d_ss)
                                ev_toks.append(e2)
                                r_sA.read(i_sA, e2)
                            r_sA.read(i_sA, e1)
                        if wB > 128:
                            e3 = ph.s(lambda e, wA=wA, wB=wB: e.copy(
                                ssb[:, wA:wA + wB - 128], psB[:, 0:wB - 128]),
                                after=[mB] + d_ss)
                            ev_toks.append(e3)
                        e4 = ph.v(lambda e, wA=wA, wB=wB: e.tensor_add(
                            ssb[:, wA + wB - 128: wA + wB], psB[:, wB - 128:wB], mc[:]),
                            after=[mB] + cl + d_ss + ([e3] if wB > 128 else []))
                        ev_toks.append(e4)
                        r_sB.read(i_sB, e4)
                        r_ss.wrote(e4)
                        i_rr, d_rr = r_rr.acquire()
                        mx = ph.v(lambda e, wtot=wtot: e.reduce_max(
                            rr[:, 0:1], ssb[:, 0:wtot], axis=AX.X, negate=True),
                            after=ev_toks + d_rr)
                        i_ps, d_ps = r_ps.acquire()
                        ex = ph.s(lambda e, wtot=wtot: e.activation(
                            psb[:, 0:wtot], ssb[:, 0:wtot], AF.Exp, bias=rr[:, 0:1],
                            accum_out=rr[:, 1:2]), after=[mx] + ev_toks + d_ps)
                        r_ss.read(i_ss, ex)
                        rc = ph.v(lambda e: e.reciprocal(rr[:, 1:2], rr[:, 1:2]), after=[ex])
                        r_rr.wrote(rc)
                        pnrm = ph.v(lambda e, wtot=wtot: e.tensor_scalar(
                            psb[:, 0:wtot], psb[:, 0:wtot], rr[:, 1:2], None, ALU.mult),
                            after=[rc, ex] + d_ps)
                        r_ps.wrote(pnrm)
                        r_rr.read(i_rr, pnrm)
                        i_y, d_y = r_yT.acquire()
                        nblk = wtot // 128
                        pv_last = None
                        for j in range(nblk):
                            if j < wA // 128:
                                gk = 4 * (w - 1) + r0 // 128 + j
                            else:
                                gk = 4 * w + (j - wA // 128)
                            i_pt, d_pt = r_pt.acquire()
                            tr = ph.t(lambda e, j=j, ip=i_pt: e.transpose(
                                ptt[ip][:], psb[:, j * 128:(j + 1) * 128], idb[:]),
                                after=[pnrm] + cl + d_pt)
                            r_pt.wrote(tr)
                            r_ps.read(i_ps, tr)
                            i_dT, d_dT = r_dT.acquire()
                            evd = ph.v(lambda e, it=i_dT, ip=i_pt: e.tensor_copy(
                                dTt[:, it * 128:(it + 1) * 128], ptt[ip][:]),
                                after=[tr] + d_dT)
                            r_pt.read(i_pt, evd)
                            r_dT.wrote(evd)
                            pv = ph.t(lambda e, gk=gk, it=i_dT, j=j, nblk=nblk: e.matmul(
                                yT[:], vtb[:, gk * 64:(gk + 1) * 64],
                                dTt[:, it * 128:(it + 1) * 128],
                                start=(j == 0), stop=(j == nblk - 1)),
                                after=[evd, lv] + d_y)
                            r_dT.read(i_dT, pv)
                            r_v.read(i_v, pv)
                            pv_last = pv
                        r_yT.wrote(pv_last)
                        i_yb, d_yb = r_yb.acquire()
                        ey = ph.s(lambda e: e.copy(yb[:], yT[:]), after=[pv_last] + d_yb)
                        r_yT.read(i_y, ey)
                        r_yb.wrote(ey)
                        sty = ph.dma(lambda e, b=b, h=h, wq0=wq0: e.dma_start(
                            out=fo[b, h * 64:(h + 1) * 64, wq0:wq0 + 128], in_=yb[:]),
                            after=[ey])
                        r_yb.read(i_yb, sty)
        ph.finalize()


def _p4_s3(mk, l, src_name):
    nc = mk.nc
    src = mk.buf(src_name)
    fo, xfi, fhn = mk.buf("fo"), mk.buf("xfi"), mk.buf("fhn")
    wo_h = mk.buf("L0_ffn_wo")
    lg_h, lb_h = mk.buf("L0_ffn_ln_g"), mk.buf("L0_ffn_ln_b")
    ph = Phase(mk, f"p4l{l}s3")
    with contextlib.ExitStack() as _st:
        wo = _st.enter_context(nc.sbuf_tensor(f"s3_wo_{l}", [128, 4 * D], BF16))
        onc = _st.enter_context(nc.sbuf_tensor(f"s3_onc_{l}", [128, 1], F32))
        onr = _st.enter_context(nc.sbuf_tensor(f"s3_onr_{l}", [1, 128], F32))
        lgc = _st.enter_context(nc.sbuf_tensor(f"s3_lg_{l}", [128, 2 * NC], F32))
        ob = _st.enter_context(nc.sbuf_tensor(f"s3_ob_{l}", [128, 4 * TT], BF16))
        xb = _st.enter_context(nc.sbuf_tensor(f"s3_xb_{l}", [128, 2 * TT], F32))
        rb = _st.enter_context(nc.sbuf_tensor(f"s3_rb_{l}", [128, 2 * TT], F32))
        hb = _st.enter_context(nc.sbuf_tensor(f"s3_hb_{l}", [128, 2 * NC * (TT + 1)], F32))
        sq = _st.enter_context(nc.sbuf_tensor(f"s3_sq_{l}", [128, 2 * TT], F32))
        rows = _st.enter_context(nc.sbuf_tensor(f"s3_rows_{l}", [1, 2 * 8 * TT], F32))
        hw = _st.enter_context(nc.sbuf_tensor(f"s3_hw_{l}", [128, 2 * 2 * TT], F32))
        hnb = _st.enter_context(nc.sbuf_tensor(f"s3_hnb_{l}", [128, 2 * TT], BF16))
        pj0 = _st.enter_context(nc.psum_tensor(f"s3_p0_{l}", [128, TT], F32))
        pj1 = _st.enter_context(nc.psum_tensor(f"s3_p1_{l}", [128, TT], F32))
        ps_sum_ = [_st.enter_context(nc.psum_tensor(f"s3_ps{i}_{l}", [1, TT], F32))
                   for i in range(2)]
        ps_sq_ = [_st.enter_context(nc.psum_tensor(f"s3_pq{i}_{l}", [1, TT], F32))
                  for i in range(2)]
        pb0 = _st.enter_context(nc.psum_tensor(f"s3_pb0_{l}", [128, TT], F32))
        pb1 = _st.enter_context(nc.psum_tensor(f"s3_pb1_{l}", [128, TT], F32))
        pj = [pj0, pj1]
        cl = []
        for kc in range(4):
            cl.append(ph.dma(lambda e, kc=kc: e.dma_start(
                out=wo[:, kc * D:(kc + 1) * D], in_=wo_h[l, ts(kc, 128), :]), q="gpsimd"))
        cl.append(ph.dma(lambda e: e.dma_start(out=onc[:], in_=mk.buf("c_ones_col")[:])))
        cl.append(ph.dma(lambda e: e.dma_start(out=onr[:], in_=mk.buf("c_ones_row")[:])))
        for c in range(NC):
            cl.append(ph.dma(lambda e, c=c: e.dma_start(
                out=lgc[:, c:c + 1], in_=lg_h[l, ts(c, 128)].rearrange("(p o) -> p o", o=1)),
                q="gpsimd"))
            cl.append(ph.dma(lambda e, c=c: e.dma_start(
                out=lgc[:, NC + c:NC + c + 1], in_=lb_h[l, ts(c, 128)].rearrange("(p o) -> p o", o=1)),
                q="gpsimd"))
        r_o, r_x, r_rb, r_pj = Ring(1), Ring(2), Ring(2), Ring(2)
        for b in range(B):
            for tt in range(NTT):
                i_o, d_o = r_o.acquire()
                oloads = []
                for c in range(4):
                    oloads.append(ph.dma(lambda e, b=b, tt=tt, c=c: e.dma_start(
                        out=ob[:, c * TT:(c + 1) * TT],
                        in_=fo[b, ts(c, 128), ts(tt, TT)]), after=d_o))
                for oc in range(NC):
                    i_x, d_x = r_x.acquire()
                    lx = ph.dma(lambda e, b=b, tt=tt, oc=oc, i=i_x: e.dma_start(
                        out=xb[:, i * TT:(i + 1) * TT],
                        in_=src[b, ts(oc, 128), ts(tt, TT)]), after=d_x, q="gpsimd")
                    r_x.wrote(lx)
                    i_pj, d_pj = r_pj.acquire()
                    mt = []
                    for kc in range(4):
                        mt.append(ph.t(lambda e, kc=kc, oc=oc, i=i_pj: e.matmul(
                            pj[i][:], wo[:, kc * D + oc * 128: kc * D + (oc + 1) * 128],
                            ob[:, kc * TT:(kc + 1) * TT],
                            start=(kc == 0), stop=(kc == 3)),
                            after=([oloads[kc]] + cl + d_pj) if kc == 0 else [oloads[kc]]))
                    mm = mt[-1]
                    r_pj.wrote(mm)
                    r_o.read(i_o, mm)
                    i_rb, d_rb = r_rb.acquire()
                    ad = ph.v(lambda e, i=i_pj, ir=i_rb, ix=i_x: e.tensor_add(
                        rb[:, ir * TT:(ir + 1) * TT], pj[i][:], xb[:, ix * TT:(ix + 1) * TT]),
                        after=[mm, lx] + d_rb)
                    r_pj.read(i_pj, ad)
                    r_x.read(i_x, ad)
                    r_rb.wrote(ad)
                    st = ph.dma(lambda e, b=b, tt=tt, oc=oc, ir=i_rb: e.dma_start(
                        out=xfi[b, ts(oc, 128), ts(tt, TT)], in_=rb[:, ir * TT:(ir + 1) * TT]),
                        after=[ad])
                    r_rb.read(i_rb, st)
        ph.finalize()
        ph = Phase(mk, f"p4l{l}s3b")
        cl = []
        cl.append(ph.dma(lambda e: e.dma_start(out=onc[:], in_=mk.buf("c_ones_col")[:])))
        cl.append(ph.dma(lambda e: e.dma_start(out=onr[:], in_=mk.buf("c_ones_row")[:])))
        for c in range(NC):
            cl.append(ph.dma(lambda e, c=c: e.dma_start(
                out=lgc[:, c:c + 1], in_=lg_h[l, ts(c, 128)].rearrange("(p o) -> p o", o=1)),
                q="gpsimd"))
            cl.append(ph.dma(lambda e, c=c: e.dma_start(
                out=lgc[:, NC + c:NC + c + 1], in_=lb_h[l, ts(c, 128)].rearrange("(p o) -> p o", o=1)),
                q="gpsimd"))
        r_hb, r_sq, r_st, r_pss, r_pb, r_hw, r_hn = (Ring(2), Ring(2), Ring(2),
                                                     Ring(2), Ring(1), Ring(2), Ring(2))
        HTT = TT + 1
        HBW = NC * HTT
        for b in range(B):
            for tt in range(NTT):
                i_hb, d_hb = r_hb.acquire()
                hloads = []
                if tt == 0:
                    mz = ph.g(lambda e, ih=i_hb: e.memset(
                        hb[:, ih * HBW: ih * HBW + NC * HTT:HTT], 0.0), after=d_hb)
                    for c in range(NC):
                        hloads.append(ph.dma(lambda e, b=b, c=c, ih=i_hb: e.dma_start(
                            out=hb[:, ih * HBW + c * HTT + 1: ih * HBW + c * HTT + HTT],
                            in_=xfi[b, ts(c, 128), 0:TT]), after=d_hb + [mz],
                            q=("gpsimd" if c % 2 else "sync")))
                else:
                    for c in range(NC):
                        hloads.append(ph.dma(lambda e, b=b, tt=tt, c=c, ih=i_hb: e.dma_start(
                            out=hb[:, ih * HBW + c * HTT: ih * HBW + c * HTT + HTT],
                            in_=xfi[b, ts(c, 128), tt * TT - 1: tt * TT + TT]),
                            after=d_hb, q=("gpsimd" if c % 2 else "sync")))
                r_hb.wrote(hloads[-1])
                vslice = lambda c, ih=i_hb: (
                    hb[:, ih * HBW + c * HTT + 1: ih * HBW + c * HTT + 1 + TT] if c < 4
                    else hb[:, ih * HBW + c * HTT: ih * HBW + c * HTT + TT])
                st_toks = []
                i_ps, d_ps = r_pss.acquire()
                ps_sum = ps_sum_[i_ps]
                ps_sq = ps_sq_[i_ps]
                for c in range(NC):
                    i_sq, d_sq = r_sq.acquire()
                    tsq = ph.s(lambda e, c=c, vs=vslice, i=i_sq: e.activation(
                        sq[:, i * TT:(i + 1) * TT], vs(c), AF.Square),
                        after=[hloads[c]] + d_sq)
                    r_sq.wrote(tsq)
                    m1 = ph.t(lambda e, c=c, ps_sum=ps_sum, vs=vslice: e.matmul(
                        ps_sum[:], onc[:], vs(c), start=(c == 0), stop=(c == NC - 1)),
                        after=([hloads[c]] + cl + d_ps) if c == 0 else [hloads[c]])
                    m2 = ph.t(lambda e, c=c, ps_sq=ps_sq, i=i_sq: e.matmul(
                        ps_sq[:], onc[:], sq[:, i * TT:(i + 1) * TT],
                        start=(c == 0), stop=(c == NC - 1)),
                        after=([tsq] + d_ps) if c == 0 else [tsq])
                    r_sq.read(i_sq, m2)
                    r_hb.read(i_hb, m1)
                    st_toks += [m1, m2]
                i_st, d_st = r_st.acquire()
                R = lambda k, i=i_st: rows[:, (i * 8 + k) * TT:(i * 8 + k + 1) * TT]
                v1 = ph.v(lambda e, ps_sum=ps_sum, R=R: e.tensor_scalar_mul(
                    R(0), ps_sum[:], -1.0 / D), after=st_toks + d_st)
                s1 = ph.s(lambda e, ps_sum=ps_sum, R=R: e.activation(
                    R(1), ps_sum[:], AF.Square, scale=1.0 / D),
                    after=st_toks + d_st + [v1])
                v2 = ph.v(lambda e, ps_sq=ps_sq, R=R: e.tensor_scalar(
                    R(2), ps_sq[:], 1.0 / D, LN_EPS, ALU.mult, ALU.add),
                    after=st_toks + d_st)
                v3 = ph.v(lambda e, R=R: e.tensor_sub(R(2), R(2), R(1)), after=[v2, s1])
                s2 = ph.s(lambda e, R=R: e.activation(R(3), R(2), AF.Sqrt), after=[v3])
                v4 = ph.v(lambda e, R=R: e.reciprocal(R(3), R(3)), after=[s2])
                v5 = ph.v(lambda e, R=R: e.tensor_mul(R(4), R(0), R(3)), after=[v1, v4])
                r_pss.wrote(v5)
                for tok in (v1, s1, v2):
                    r_pss.read(i_ps, tok)
                i_pb, d_pb = r_pb.acquire()
                bc1 = ph.t(lambda e, R=R: e.matmul(pb0[:], onr[:], R(3),
                                                   start=True, stop=True),
                           after=[v4, v5] + cl + d_pb)
                bc2 = ph.t(lambda e, R=R: e.matmul(pb1[:], onr[:], R(4),
                                                   start=True, stop=True),
                           after=[v5] + cl + d_pb)
                r_pb.wrote(bc2)
                r_st.wrote(v5)
                r_st.read(i_st, bc1)
                r_st.read(i_st, bc2)
                for c in range(NC):
                    i_hw, d_hw = r_hw.acquire()
                    HW = lambda k, i=i_hw: hw[:, (2 * i + k) * TT:(2 * i + k + 1) * TT]
                    t1 = ph.v(lambda e, c=c, vs=vslice, HW=HW: e.tensor_mul(
                        HW(0), vs(c), pb0[:]), after=[bc1, hloads[c]] + d_hw)
                    t2 = ph.v(lambda e, HW=HW: e.tensor_add(HW(1), HW(0), pb1[:]),
                              after=[t1, bc2])
                    r_pb.read(i_pb, t1)
                    r_pb.read(i_pb, t2)
                    r_hb.read(i_hb, t1)
                    i_hn, d_hn = r_hn.acquire()
                    hn = ph.v(lambda e, c=c, HW=HW, i=i_hn: e.tensor_scalar(
                        hnb[:, i * TT:(i + 1) * TT], HW(1),
                        lgc[:, c:c + 1], lgc[:, NC + c:NC + c + 1],
                        ALU.mult, ALU.add), after=[t2] + cl + d_hn)
                    r_hw.wrote(hn)
                    sth = ph.dma(lambda e, b=b, tt=tt, c=c, i=i_hn: e.dma_start(
                        out=fhn[b, ts(c, 128), ts(tt, TT)],
                        in_=hnb[:, i * TT:(i + 1) * TT]), after=[hn])
                    r_hn.wrote(sth)
                    r_hn.read(i_hn, sth)
        ph.finalize()


def _p4_s4(mk, l, dst_name):
    nc = mk.nc
    fhn, xfi = mk.buf("fhn"), mk.buf("xfi")
    dst = mk.buf(dst_name)
    last = (dst_name == "x1b")
    x1 = mk.buf("x1") if last else None
    w1_h, w2_h = mk.buf("L0_ffn_w1"), mk.buf("L0_ffn_w2")
    b1_h, b2_h = mk.buf("L0_ffn_b1"), mk.buf("L0_ffn_b2")
    NJ = 22
    HID = 2730
    ph = Phase(mk, f"p4l{l}s4")
    with contextlib.ExitStack() as _st:
        w1 = _st.enter_context(nc.sbuf_tensor(f"s4_w1_{l}", [128, NC * 5460], BF16))
        w2 = _st.enter_context(nc.sbuf_tensor(f"s4_w2_{l}", [128, NJ * D], BF16))
        b1c = _st.enter_context(nc.sbuf_tensor(f"s4_b1_{l}", [128, 2 * NJ], F32))
        b2c = _st.enter_context(nc.sbuf_tensor(f"s4_b2_{l}", [128, NC], F32))
        hnb = _st.enter_context(nc.sbuf_tensor(f"s4_hn_{l}", [128, 2 * NC * TT], BF16))
        hsb = _st.enter_context(nc.sbuf_tensor(f"s4_h_{l}", [128, NJ * TT], BF16))
        gg = _st.enter_context(nc.sbuf_tensor(f"s4_gg_{l}", [128, TT], F32))
        xb = _st.enter_context(nc.sbuf_tensor(f"s4_xb_{l}", [128, 2 * TT], F32))
        x1b_ = _st.enter_context(nc.sbuf_tensor(f"s4_x1b_{l}", [128, 2 * TT], F32))
        rb = _st.enter_context(nc.sbuf_tensor(f"s4_rb_{l}", [128, 2 * TT], F32))
        pA = _st.enter_context(nc.psum_tensor(f"s4_pA_{l}", [128, TT], F32))
        pB = _st.enter_context(nc.psum_tensor(f"s4_pB_{l}", [128, TT], F32))
        pj0 = _st.enter_context(nc.psum_tensor(f"s4_p0_{l}", [128, TT], F32))
        pj1 = _st.enter_context(nc.psum_tensor(f"s4_p1_{l}", [128, TT], F32))
        pj = [pj0, pj1]
        cl = []
        for kc in range(NC):
            cl.append(ph.dma(lambda e, kc=kc: e.dma_start(
                out=w1[:, kc * 5460:(kc + 1) * 5460], in_=w1_h[l, ts(kc, 128), :]),
                q="gpsimd"))
        for j in range(NJ):
            wj = 128 if j < NJ - 1 else HID - 128 * (NJ - 1)
            cl.append(ph.dma(lambda e, j=j, wj=wj: e.dma_start(
                out=w2[0:wj, j * D:(j + 1) * D], in_=w2_h[l, j * 128: j * 128 + wj, :]),
                q="gpsimd"))
            cl.append(ph.dma(lambda e, j=j, wj=wj: e.dma_start(
                out=b1c[0:wj, j:j + 1],
                in_=b1_h[l, j * 128: j * 128 + wj].rearrange("(p o) -> p o", o=1)),
                q="gpsimd"))
            cl.append(ph.dma(lambda e, j=j, wj=wj: e.dma_start(
                out=b1c[0:wj, NJ + j:NJ + j + 1],
                in_=b1_h[l, HID + j * 128: HID + j * 128 + wj].rearrange("(p o) -> p o", o=1)),
                q="gpsimd"))
        for c in range(NC):
            cl.append(ph.dma(lambda e, c=c: e.dma_start(
                out=b2c[:, c:c + 1], in_=b2_h[l, ts(c, 128)].rearrange("(p o) -> p o", o=1)),
                q="gpsimd"))
        r_hn, r_h, r_gg = Ring(2), Ring(1), Ring(1)
        r_pA, r_pB, r_pj = Ring(1), Ring(1), Ring(2)
        r_x, r_x1, r_rb = Ring(2), Ring(2), Ring(2)
        for b in range(B):
            for tt in range(NTT):
                i_hn, d_hn = r_hn.acquire()
                hloads = []
                for c in range(NC):
                    hloads.append(ph.dma(lambda e, b=b, tt=tt, c=c, i=i_hn: e.dma_start(
                        out=hnb[:, i * NC * TT + c * TT: i * NC * TT + (c + 1) * TT],
                        in_=fhn[b, ts(c, 128), ts(tt, TT)]), after=d_hn))
                r_hn.wrote(hloads[-1])
                h_toks = {}
                for j in range(NJ):
                    wj = 128 if j < NJ - 1 else HID - 128 * (NJ - 1)
                    i_pa, d_pa = r_pA.acquire()
                    ma_t = []
                    for kc in range(NC):
                        ma_t.append(ph.t(lambda e, kc=kc, j=j, wj=wj, i=i_hn: e.matmul(
                            pA[0:wj, :], w1[0:128, kc * 5460 + j * 128: kc * 5460 + j * 128 + wj],
                            hnb[:, i * NC * TT + kc * TT: i * NC * TT + (kc + 1) * TT],
                            start=(kc == 0), stop=(kc == NC - 1)),
                            after=([hloads[kc]] + cl + d_pa) if kc == 0 else [hloads[kc]]))
                    i_pb, d_pb = r_pB.acquire()
                    mb_t = []
                    for kc in range(NC):
                        mb_t.append(ph.t(lambda e, kc=kc, j=j, wj=wj, i=i_hn: e.matmul(
                            pB[0:wj, :],
                            w1[0:128, kc * 5460 + HID + j * 128: kc * 5460 + HID + j * 128 + wj],
                            hnb[:, i * NC * TT + kc * TT: i * NC * TT + (kc + 1) * TT],
                            start=(kc == 0), stop=(kc == NC - 1)),
                            after=([hloads[kc]] + cl + d_pb) if kc == 0 else [hloads[kc]]))
                    r_pA.wrote(ma_t[-1])
                    r_pB.wrote(mb_t[-1])
                    r_hn.read(i_hn, ma_t[-1])
                    r_hn.read(i_hn, mb_t[-1])
                    i_gg, d_gg = r_gg.acquire()
                    ge = ph.s(lambda e, j=j, wj=wj: e.activation(
                        gg[0:wj, :], pB[0:wj, :], AF.Gelu, bias=b1c[0:wj, NJ + j:NJ + j + 1]),
                        after=[mb_t[-1]] + cl + d_gg)
                    r_pB.read(i_pb, ge)
                    r_gg.wrote(ge)
                    i_h, d_h = r_h.acquire()
                    hv = ph.v(lambda e, j=j, wj=wj: e.scalar_tensor_tensor(
                        hsb[0:wj, j * TT:(j + 1) * TT], pA[0:wj, :], b1c[0:wj, j:j + 1],
                        gg[0:wj, :], ALU.add, ALU.mult),
                        after=[ma_t[-1], ge] + (h_toks.get(j) or []))
                    r_pA.read(i_pa, hv)
                    r_gg.read(i_gg, hv)
                    h_toks[j] = [hv]
                for oc in range(NC):
                    i_x, d_x = r_x.acquire()
                    lx = ph.dma(lambda e, b=b, tt=tt, oc=oc, i=i_x: e.dma_start(
                        out=xb[:, i * TT:(i + 1) * TT], in_=xfi[b, ts(oc, 128), ts(tt, TT)]),
                        after=d_x, q="gpsimd")
                    r_x.wrote(lx)
                    if last:
                        i_x1, d_x1 = r_x1.acquire()
                        lx1 = ph.dma(lambda e, b=b, tt=tt, oc=oc, i=i_x1: e.dma_start(
                            out=x1b_[:, i * TT:(i + 1) * TT], in_=x1[b, ts(oc, 128), ts(tt, TT)]),
                            after=d_x1, q="gpsimd")
                        r_x1.wrote(lx1)
                    i_pj, d_pj = r_pj.acquire()
                    mt = []
                    for j in range(NJ):
                        wj = 128 if j < NJ - 1 else HID - 128 * (NJ - 1)
                        mt.append(ph.t(lambda e, j=j, oc=oc, wj=wj, i=i_pj: e.matmul(
                            pj[i][:], w2[0:wj, j * D + oc * 128: j * D + (oc + 1) * 128],
                            hsb[0:wj, j * TT:(j + 1) * TT],
                            start=(j == 0), stop=(j == NJ - 1)),
                            after=(h_toks[j] + cl + d_pj) if j == 0 else h_toks[j]))
                    mm = mt[-1]
                    r_pj.wrote(mm)
                    i_rb, d_rb = r_rb.acquire()
                    ad = ph.v(lambda e, i=i_pj, oc=oc, ir=i_rb, ix=i_x: e.scalar_tensor_tensor(
                        rb[:, ir * TT:(ir + 1) * TT], pj[i][:], b2c[:, oc:oc + 1],
                        xb[:, ix * TT:(ix + 1) * TT], ALU.add, ALU.add),
                        after=[mm, lx] + cl + d_rb)
                    r_pj.read(i_pj, ad)
                    r_x.read(i_x, ad)
                    if last:
                        ad = ph.v(lambda e, ir=i_rb, ix1=i_x1: e.tensor_add(
                            rb[:, ir * TT:(ir + 1) * TT], rb[:, ir * TT:(ir + 1) * TT],
                            x1b_[:, ix1 * TT:(ix1 + 1) * TT]), after=[ad, lx1])
                        r_x1.read(i_x1, ad)
                    r_rb.wrote(ad)
                    st = ph.dma(lambda e, b=b, tt=tt, oc=oc, ir=i_rb: e.dma_start(
                        out=dst[b, ts(oc, 128), ts(tt, TT)], in_=rb[:, ir * TT:(ir + 1) * TT]),
                        after=[ad])
                    r_rb.read(i_rb, st)
                for j in range(NJ):
                    h_toks[j] = h_toks[j] + [mm]
        ph.finalize()


def emit_p4(mk: MK):
    for l in range(2):
        src = "xf" if l == 0 else "xfi"
        _p4_s1(mk, l, src)
        _p4_s2(mk, l)
        _p4_s3(mk, l, src)
        _p4_s4(mk, l, "xfi" if l == 0 else "x1b")



def emit_p5(mk: MK):
    nc = mk.nc
    x1b, xn1 = mk.buf("x1b"), mk.buf("xn1")
    stats1 = mk.buf("stats1")
    ph = Phase(mk, "p5")
    with contextlib.ExitStack() as _st:
        onc = _st.enter_context(nc.sbuf_tensor("p5_onc", [128, 1], F32))
        onr = _st.enter_context(nc.sbuf_tensor("p5_onr", [1, 128], F32))
        xb = _st.enter_context(nc.sbuf_tensor("p5_xb", [128, 2 * NC * TT], F32))
        sq = _st.enter_context(nc.sbuf_tensor("p5_sq", [128, TT], F32))
        rows = _st.enter_context(nc.sbuf_tensor("p5_rows", [1, 8 * TT], F32))
        xnt = _st.enter_context(nc.sbuf_tensor("p5_xn", [128, 2 * TT], F32))
        xnb = _st.enter_context(nc.sbuf_tensor("p5_xnb", [128, TT], BF16))
        ps_sum = _st.enter_context(nc.psum_tensor("p5_ps", [1, TT], F32))
        ps_sq = _st.enter_context(nc.psum_tensor("p5_pq", [1, TT], F32))
        pb0 = _st.enter_context(nc.psum_tensor("p5_pb0", [128, TT], F32))
        pb1 = _st.enter_context(nc.psum_tensor("p5_pb1", [128, TT], F32))
        cl = []
        cl.append(ph.dma(lambda e: e.dma_start(out=onc[:], in_=mk.buf("c_ones_col")[:])))
        cl.append(ph.dma(lambda e: e.dma_start(out=onr[:], in_=mk.buf("c_ones_row")[:])))
        r_x, r_sq, r_st, r_ps, r_pb, r_xn, r_xnb = Ring(2), Ring(1), Ring(1), Ring(1), Ring(1), Ring(1), Ring(1)
        for b in range(B):
            for tt in range(NTT):
                i_x, d_x = r_x.acquire()
                xl = []
                for c in range(NC):
                    xl.append(ph.dma(lambda e, b=b, tt=tt, c=c, i=i_x: e.dma_start(
                        out=xb[:, i * NC * TT + c * TT: i * NC * TT + (c + 1) * TT],
                        in_=x1b[b, ts(c, 128), ts(tt, TT)]), after=d_x))
                r_x.wrote(xl[-1])
                st_toks = []
                i_ps, d_ps = r_ps.acquire()
                XS = lambda c, i: xb[:, i * NC * TT + c * TT: i * NC * TT + (c + 1) * TT]
                for c in range(NC):
                    i_sq, d_sq = r_sq.acquire()
                    tsq = ph.s(lambda e, c=c, i=i_x: e.activation(sq[:], XS(c, i), AF.Square),
                               after=[xl[c]] + d_sq)
                    r_sq.wrote(tsq)
                    m1 = ph.t(lambda e, c=c, i=i_x: e.matmul(
                        ps_sum[:], onc[:], XS(c, i), start=(c == 0), stop=(c == NC - 1)),
                        after=[xl[c]] + cl + d_ps)
                    m2 = ph.t(lambda e, c=c: e.matmul(
                        ps_sq[:], onc[:], sq[:], start=(c == 0), stop=(c == NC - 1)),
                        after=[tsq] + d_ps)
                    r_sq.read(i_sq, m2)
                    r_x.read(i_x, m1)
                    st_toks += [m1, m2]
                R = lambda k: rows[:, k * TT:(k + 1) * TT]
                i_st, d_st = r_st.acquire()
                v1 = ph.v(lambda e: e.tensor_scalar_mul(R(0), ps_sum[:], -1.0 / D), after=st_toks + d_st)
                s1 = ph.s(lambda e: e.activation(R(1), ps_sum[:], AF.Square, scale=1.0 / D),
                          after=st_toks + d_st + [v1])
                v2 = ph.v(lambda e: e.tensor_scalar(R(2), ps_sq[:], 1.0 / D, LN_EPS, ALU.mult, ALU.add),
                          after=st_toks + d_st)
                v3 = ph.v(lambda e: e.tensor_sub(R(2), R(2), R(1)), after=[v2, s1])
                s2 = ph.s(lambda e: e.activation(R(3), R(2), AF.Sqrt), after=[v3])
                v4 = ph.v(lambda e: e.reciprocal(R(3), R(3)), after=[s2])
                v5 = ph.v(lambda e: e.tensor_mul(R(4), R(0), R(3)), after=[v1, v4])
                r_ps.wrote(v5)
                for tok in (v1, s1, v2):
                    r_ps.read(i_ps, tok)
                sd1 = ph.dma(lambda e, b=b, tt=tt: e.dma_start(
                    out=stats1[b, 0:1, ts(tt, TT)], in_=R(4)), after=[v5], q="gpsimd")
                sd2 = ph.dma(lambda e, b=b, tt=tt: e.dma_start(
                    out=stats1[b, 1:2, ts(tt, TT)], in_=R(3)), after=[v4, v5], q="gpsimd")
                r_st.read(i_st, sd1)
                r_st.read(i_st, sd2)
                i_pb, d_pb = r_pb.acquire()
                bc1 = ph.t(lambda e: e.matmul(pb0[:], onr[:], R(3), start=True, stop=True),
                           after=[v4, v5] + cl + d_pb)
                bc2 = ph.t(lambda e: e.matmul(pb1[:], onr[:], R(4), start=True, stop=True),
                           after=[v5] + cl + d_pb)
                r_pb.wrote(bc2)
                r_st.wrote(v5)
                r_st.read(i_st, bc1)
                r_st.read(i_st, bc2)
                for c in range(NC):
                    i_xn, d_xn = r_xn.acquire()
                    x1_ = ph.v(lambda e, c=c, i=i_x: e.tensor_mul(xnt[:, 0:TT], XS(c, i), pb0[:]),
                               after=[bc1, xl[c]] + d_xn)
                    i_xb, d_xb = r_xnb.acquire()
                    x2_ = ph.v(lambda e: e.tensor_add(xnb[:], xnt[:, 0:TT], pb1[:]),
                               after=[x1_, bc2] + d_xb)
                    r_xn.wrote(x2_)
                    r_x.read(i_x, x1_)
                    r_pb.read(i_pb, x1_)
                    r_pb.read(i_pb, x2_)
                    st2 = ph.dma(lambda e, b=b, tt=tt, c=c: e.dma_start(
                        out=xn1[b, ts(c, 128), ts(tt, TT)], in_=xnb[:]),
                        after=[x2_], q="gpsimd")
                    r_xnb.wrote(st2)
        ph.finalize()



def emit_p6(mk: MK):
    nc = mk.nc
    xn1 = mk.buf("xn1")
    q1, k1, v1t = mk.buf("q1"), mk.buf("k1"), mk.buf("v1t")
    qn1, kn1, amem = mk.buf("qn1"), mk.buf("kn1"), mk.buf("amem")
    wq_h, wk_h, wv_h = mk.buf("L1_attn_Wq"), mk.buf("L1_attn_Wk"), mk.buf("L1_attn_Wv")
    mem_h, mn_h = mk.buf("L1_attn_memory"), mk.buf("L1_attn_memory_norm")
    ph = Phase(mk, "p6")
    with contextlib.ExitStack() as _st:
        wq = _st.enter_context(nc.sbuf_tensor("p6_wq", [128, NC * D], BF16))
        wk = _st.enter_context(nc.sbuf_tensor("p6_wk", [128, NC * 512], BF16))
        wv = _st.enter_context(nc.sbuf_tensor("p6_wv", [128, NC * 512], BF16))
        mem = _st.enter_context(nc.sbuf_tensor("p6_mem", [128, 4 * 512], BF16))
        mn = _st.enter_context(nc.sbuf_tensor("p6_mn", [128, 4], BF16))
        seg = _st.enter_context(nc.sbuf_tensor("p6_seg", [128, 2], F32))
        onr = _st.enter_context(nc.sbuf_tensor("p6_onr", [1, 128], F32))
        idb = _st.enter_context(nc.sbuf_tensor("p6_idb", [128, 128], BF16))
        xnb = _st.enter_context(nc.sbuf_tensor("p6_xnb", [128, 2 * NC * TT], BF16))
        qf = _st.enter_context(nc.sbuf_tensor("p6_qf", [128, TT], F32))
        qbf = _st.enter_context(nc.sbuf_tensor("p6_qbf", [128, TT], BF16))
        sqt = _st.enter_context(nc.sbuf_tensor("p6_sq", [128, TT], F32))
        nrows = _st.enter_context(nc.sbuf_tensor("p6_nr", [2, TT], F32))
        sig = _st.enter_context(nc.sbuf_tensor("p6_sig", [128, NC * TT], BF16))
        el = _st.enter_context(nc.sbuf_tensor("p6_el", [128, 2 * TT], F32))
        drow = _st.enter_context(nc.sbuf_tensor("p6_dr", [1, TT], F32))
        rdb = _st.enter_context(nc.sbuf_tensor("p6_rdb", [128, TT], F32))
        amc = _st.enter_context(nc.sbuf_tensor("p6_am", [128, TT], BF16))
        vb = _st.enter_context(nc.sbuf_tensor("p6_vb", [128, TT], BF16))
        vt = _st.enter_context(nc.sbuf_tensor("p6_vt", [128, 2 * 128], BF16))
        pj0 = _st.enter_context(nc.psum_tensor("p6_p0", [128, TT], F32))
        pj1 = _st.enter_context(nc.psum_tensor("p6_p1", [128, TT], F32))
        pn = _st.enter_context(nc.psum_tensor("p6_pn", [2, TT], F32))
        pd = _st.enter_context(nc.psum_tensor("p6_pd", [1, TT], F32))
        pnum = _st.enter_context(nc.psum_tensor("p6_pnum", [128, TT], F32))
        pbd = _st.enter_context(nc.psum_tensor("p6_pbd", [128, TT], F32))
        pt0 = _st.enter_context(nc.psum_tensor("p6_pt0", [128, 128], BF16))
        pt1 = _st.enter_context(nc.psum_tensor("p6_pt1", [128, 128], BF16))
        pj = [pj0, pj1]
        ptt = [pt0, pt1]
        cl = []
        for kc in range(NC):
            cl.append(ph.dma(lambda e, kc=kc: e.dma_start(
                out=wq[:, kc * D:(kc + 1) * D], in_=wq_h[ts(kc, 128), :]), q="gpsimd"))
            cl.append(ph.dma(lambda e, kc=kc: e.dma_start(
                out=wk[:, kc * 512:(kc + 1) * 512], in_=wk_h[ts(kc, 128), :]), q="gpsimd"))
            cl.append(ph.dma(lambda e, kc=kc: e.dma_start(
                out=wv[:, kc * 512:(kc + 1) * 512], in_=wv_h[ts(kc, 128), :]), q="gpsimd"))
        for kc in range(4):
            cl.append(ph.dma(lambda e, kc=kc: e.dma_start(
                out=mem[:, kc * 512:(kc + 1) * 512], in_=mem_h[ts(kc, 128), :]), q="gpsimd"))
            cl.append(ph.dma(lambda e, kc=kc: e.dma_start(
                out=mn[:, kc:kc + 1], in_=mn_h[ts(kc, 128)].rearrange("(p o) -> p o", o=1)),
                q="gpsimd"))
        cl.append(ph.dma(lambda e: e.dma_start(out=seg[:], in_=mk.buf("c_seg64")[:])))
        cl.append(ph.dma(lambda e: e.dma_start(out=onr[:], in_=mk.buf("c_ones_row")[:])))
        cl.append(ph.dma(lambda e: e.dma_start(out=idb[:], in_=mk.buf("c_ident_bf16")[:])))
        r_x = Ring(2)
        r_pj, r_qf, r_qbf, r_sq, r_nr, r_pn = Ring(2), Ring(1), Ring(1), Ring(1), Ring(1), Ring(1)
        r_sig, r_el = Ring(1), Ring(1)
        r_pd, r_pnum, r_pbd, r_dr, r_rdb, r_am = Ring(1), Ring(1), Ring(1), Ring(1), Ring(1), Ring(1)
        r_vb, r_pt, r_vt = Ring(1), Ring(2), Ring(2)

        def proj(oc, w_sb, outw, xl, i_x, d_pj, i_pj):
            mt = []
            for kc in range(NC):
                mt.append(ph.t(lambda e, kc=kc, oc=oc, i=i_pj, ix=i_x, w_sb=w_sb, outw=outw:
                               e.matmul(
                    pj[i][:], w_sb[:, kc * outw + oc * 128: kc * outw + (oc + 1) * 128],
                    xnb[:, ix * NC * TT + kc * TT: ix * NC * TT + (kc + 1) * TT],
                    start=(kc == 0), stop=(kc == NC - 1)),
                    after=([xl[kc]] + cl + d_pj) if kc == 0 else [xl[kc]]))
            return mt[-1]

        for b in range(B):
            for tt in range(NTT):
                i_x, d_x = r_x.acquire()
                xl = []
                for c in range(NC):
                    xl.append(ph.dma(lambda e, b=b, tt=tt, c=c, i=i_x: e.dma_start(
                        out=xnb[:, i * NC * TT + c * TT: i * NC * TT + (c + 1) * TT],
                        in_=xn1[b, ts(c, 128), ts(tt, TT)]), after=d_x))
                r_x.wrote(xl[-1])
                sig_toks = []
                for oc in range(NC):
                    i_pj, d_pj = r_pj.acquire()
                    mm = proj(oc, wq, D, xl, i_x, d_pj, i_pj)
                    r_pj.wrote(mm)
                    r_x.read(i_x, mm)
                    i_qf, d_qf = r_qf.acquire()
                    ev = ph.s(lambda e, i=i_pj: e.copy(qf[:], pj[i][:]), after=[mm] + d_qf)
                    r_pj.read(i_pj, ev)
                    r_qf.wrote(ev)
                    i_qb, d_qb = r_qbf.acquire()
                    cb = ph.v(lambda e: e.tensor_copy(qbf[:], qf[:]), after=[ev] + d_qb)
                    r_qf.read(i_qf, cb)
                    r_qbf.wrote(cb)
                    st = ph.dma(lambda e, b=b, tt=tt, oc=oc: e.dma_start(
                        out=q1[b, ts(oc, 128), ts(tt, TT)], in_=qbf[:]), after=[cb])
                    r_qbf.read(i_qb, st)
                    i_sq, d_sq = r_sq.acquire()
                    sk = ph.s(lambda e: e.activation(sqt[:], qbf[:], AF.Square), after=[cb] + d_sq)
                    r_qbf.read(i_qb, sk)
                    r_sq.wrote(sk)
                    i_pn, d_pn = r_pn.acquire()
                    nm1 = ph.t(lambda e: e.matmul(pn[:], seg[:], sqt[:], start=True, stop=True),
                               after=[sk] + cl + d_pn)
                    r_sq.read(i_sq, nm1)
                    r_pn.wrote(nm1)
                    i_nr, d_nr = r_nr.acquire()
                    s1_ = ph.s(lambda e: e.activation(nrows[:], pn[:], AF.Sqrt), after=[nm1] + d_nr)
                    r_pn.read(i_pn, s1_)
                    r_nr.wrote(s1_)
                    sn = ph.dma(lambda e, b=b, tt=tt, oc=oc: e.dma_start(
                        out=qn1[b, 2 * oc:2 * oc + 2, ts(tt, TT)], in_=nrows[:]),
                        after=[s1_], q="gpsimd")
                    r_nr.read(i_nr, sn)
                    EL = lambda k: el[:, k * TT:(k + 1) * TT]
                    i_el, d_el = r_el.acquire()
                    e1 = ph.v(lambda e: e.tensor_scalar_max(EL(0), qf[:], 0.0), after=[ev] + d_el)
                    e2 = ph.v(lambda e: e.tensor_scalar_min(EL(1), qf[:], 0.0), after=[ev] + d_el)
                    r_qf.read(i_qf, e1)
                    r_qf.read(i_qf, e2)
                    e3 = ph.s(lambda e: e.activation(EL(1), EL(1), AF.Exp), after=[e2])
                    i_sg, d_sg = r_sig.acquire()
                    e4 = ph.v(lambda e, oc=oc: e.scalar_tensor_tensor(
                        sig[:, oc * TT:(oc + 1) * TT], EL(1), -1.0, EL(0), ALU.add, ALU.add),
                        after=[e3, e1] + (d_sg if oc == 0 else []))
                    r_el.wrote(e4)
                    sig_toks.append(e4)
                for oc in range(4):
                    i_pj, d_pj = r_pj.acquire()
                    mm = proj(oc, wk, 512, xl, i_x, d_pj, i_pj)
                    r_pj.wrote(mm)
                    r_x.read(i_x, mm)
                    i_qb, d_qb = r_qbf.acquire()
                    ev = ph.s(lambda e, i=i_pj: e.copy(qbf[:], pj[i][:]), after=[mm] + d_qb)
                    r_pj.read(i_pj, ev)
                    r_qbf.wrote(ev)
                    st = ph.dma(lambda e, b=b, tt=tt, oc=oc: e.dma_start(
                        out=k1[b, ts(oc, 128), ts(tt, TT)], in_=qbf[:]), after=[ev])
                    r_qbf.read(i_qb, st)
                    i_sq, d_sq = r_sq.acquire()
                    sk = ph.s(lambda e: e.activation(sqt[:], qbf[:], AF.Square), after=[ev] + d_sq)
                    r_qbf.read(i_qb, sk)
                    r_sq.wrote(sk)
                    i_pn, d_pn = r_pn.acquire()
                    nm1 = ph.t(lambda e: e.matmul(pn[:], seg[:], sqt[:], start=True, stop=True),
                               after=[sk] + cl + d_pn)
                    r_sq.read(i_sq, nm1)
                    r_pn.wrote(nm1)
                    i_nr, d_nr = r_nr.acquire()
                    s1_ = ph.s(lambda e: e.activation(nrows[:], pn[:], AF.Sqrt), after=[nm1] + d_nr)
                    r_pn.read(i_pn, s1_)
                    r_nr.wrote(s1_)
                    sn = ph.dma(lambda e, b=b, tt=tt, oc=oc: e.dma_start(
                        out=kn1[b, 2 * oc:2 * oc + 2, ts(tt, TT)], in_=nrows[:]),
                        after=[s1_], q="gpsimd")
                    r_nr.read(i_nr, sn)
                for oc in range(4):
                    i_pj, d_pj = r_pj.acquire()
                    mm = proj(oc, wv, 512, xl, i_x, d_pj, i_pj)
                    r_pj.wrote(mm)
                    r_x.read(i_x, mm)
                    i_vb, d_vb = r_vb.acquire()
                    ev = ph.s(lambda e, i=i_pj: e.copy(vb[:], pj[i][:]), after=[mm] + d_vb)
                    r_pj.read(i_pj, ev)
                    r_vb.wrote(ev)
                    for sub in range(4):
                        i_pt, d_pt = r_pt.acquire()
                        tr = ph.t(lambda e, sub=sub, ip=i_pt: e.transpose(
                            ptt[ip][:], vb[:, sub * 128:(sub + 1) * 128], idb[:]),
                            after=[ev] + cl + d_pt)
                        r_pt.wrote(tr)
                        r_vb.read(i_vb, tr)
                        i_vt, d_vt = r_vt.acquire()
                        evt = ph.v(lambda e, iv=i_vt, ip=i_pt: e.tensor_copy(
                            vt[:, iv * 128:(iv + 1) * 128], ptt[ip][:]), after=[tr] + d_vt)
                        r_pt.read(i_pt, evt)
                        r_vt.wrote(evt)
                        for h2 in range(2):
                            hh = oc * 2 + h2
                            stv = ph.dma(lambda e, b=b, tt=tt, sub=sub, hh=hh, h2=h2, iv=i_vt:
                                         e.dma_start(
                                out=v1t[b, hh, tt * TT + sub * 128: tt * TT + (sub + 1) * 128, :],
                                in_=vt[:, iv * 128 + h2 * 64: iv * 128 + h2 * 64 + 64]),
                                after=[evt])
                            r_vt.read(i_vt, stv)
                for g in range(2):
                    i_pd, d_pd = r_pd.acquire()
                    dt_ = []
                    for kc in range(4):
                        dt_.append(ph.t(lambda e, kc=kc, g=g: e.matmul(
                            pd[:], mn[:, kc:kc + 1],
                            sig[:, (g * 4 + kc) * TT:(g * 4 + kc + 1) * TT],
                            start=(kc == 0), stop=(kc == 3)),
                            after=[sig_toks[g * 4 + kc]] + cl + (d_pd if kc == 0 else [])))
                    r_pd.wrote(dt_[-1])
                    i_dr, d_dr = r_dr.acquire()
                    ed = ph.v(lambda e: e.tensor_copy(drow[:], pd[:]), after=[dt_[-1]] + d_dr)
                    r_pd.read(i_pd, ed)
                    rc = ph.v(lambda e: e.reciprocal(drow[:], drow[:]), after=[ed])
                    r_dr.wrote(rc)
                    i_pbd, d_pbd = r_pbd.acquire()
                    bc = ph.t(lambda e: e.matmul(pbd[:], onr[:], drow[:], start=True, stop=True),
                              after=[rc] + cl + d_pbd)
                    r_dr.read(i_dr, bc)
                    r_pbd.wrote(bc)
                    i_rdb, d_rdb = r_rdb.acquire()
                    erd = ph.s(lambda e: e.copy(rdb[:], pbd[:]), after=[bc] + d_rdb)
                    r_pbd.read(i_pbd, erd)
                    r_rdb.wrote(erd)
                    for vc in range(4):
                        i_pnm, d_pnm = r_pnum.acquire()
                        nt = []
                        for kc in range(4):
                            nt.append(ph.t(lambda e, kc=kc, vc=vc, g=g: e.matmul(
                                pnum[:], mem[:, kc * 512 + vc * 128: kc * 512 + (vc + 1) * 128],
                                sig[:, (g * 4 + kc) * TT:(g * 4 + kc + 1) * TT],
                                start=(kc == 0), stop=(kc == 3)),
                                after=[sig_toks[g * 4 + kc]] + cl + (d_pnm if kc == 0 else [])))
                        r_pnum.wrote(nt[-1])
                        i_am, d_am = r_am.acquire()
                        am = ph.v(lambda e: e.tensor_mul(amc[:], pnum[:], rdb[:]),
                                  after=[nt[-1], erd] + d_am)
                        r_pnum.read(i_pnm, am)
                        r_rdb.read(i_rdb, am)
                        r_am.wrote(am)
                        sta = ph.dma(lambda e, b=b, tt=tt, g=g, vc=vc: e.dma_start(
                            out=amem[b, ts(g * 4 + vc, 128), ts(tt, TT)], in_=amc[:]),
                            after=[am])
                        r_am.read(i_am, sta)
                    for kc in range(4):
                        r_sig.read(0, dt_[-1])
        ph.finalize()



def emit_p7(mk: MK):
    nc = mk.nc
    q1, k1, v1t = mk.buf("q1"), mk.buf("k1"), mk.buf("v1t")
    amem, a1 = mk.buf("amem"), mk.buf("a1")
    mw_h = mk.buf("L1_attn_memory_weight")
    ph = Phase(mk, "p7")
    with contextlib.ExitStack() as _st:
        onr = _st.enter_context(nc.sbuf_tensor("p7_onr", [1, 128], F32))
        wrow = _st.enter_context(nc.sbuf_tensor("p7_wr", [1, 2], F32))
        wcol = _st.enter_context(nc.sbuf_tensor("p7_wc", [128, 2], F32))
        kh = _st.enter_context(nc.sbuf_tensor("p7_kh", [64, 2 * T], BF16))
        qsb2 = _st.enter_context(nc.sbuf_tensor("p7_q", [64, 2 * 2 * T], BF16))
        vtb = _st.enter_context(nc.sbuf_tensor("p7_vt", [128, 2, 16, 65], BF16))
        pT = _st.enter_context(nc.sbuf_tensor("p7_pT", [128, 3 * TT], BF16))
        ysb = _st.enter_context(nc.sbuf_tensor("p7_ys", [65, 2 * TT], F32))
        rz = _st.enter_context(nc.sbuf_tensor("p7_rz", [1, 2 * TT], F32))
        amt = _st.enter_context(nc.sbuf_tensor("p7_am", [64, 2 * TT], BF16))
        a1b = _st.enter_context(nc.sbuf_tensor("p7_a1", [64, 2 * TT], BF16))
        pS0 = _st.enter_context(nc.psum_tensor("p7_s0", [128, TT], F32))
        pS1 = _st.enter_context(nc.psum_tensor("p7_s1", [128, TT], F32))
        pS2 = _st.enter_context(nc.psum_tensor("p7_s2", [128, TT], F32))
        yT0 = _st.enter_context(nc.psum_tensor("p7_y0", [128, TT], F32))
        yT1 = _st.enter_context(nc.psum_tensor("p7_y1", [128, TT], F32))
        pmisc = _st.enter_context(nc.psum_tensor("p7_misc", [128, TT], F32))
        pS = [pS0, pS1, pS2]
        yTs = [yT0, yT1]
        cl = []
        cl.append(ph.dma(lambda e: e.dma_start(out=onr[:], in_=mk.buf("c_ones_row")[:])))
        cl.append(ph.dma(lambda e: e.dma_start(
            out=wrow[0:1, 0:1], in_=mw_h.rearrange("(o d) -> o d", o=1))))
        r_misc = Ring(1)
        sg = ph.s(lambda e: e.activation(wrow[0:1, 0:1], wrow[0:1, 0:1], AF.Sigmoid), after=cl)
        i_m, d_m = r_misc.acquire()
        blm = ph.t(lambda e: e.matmul(pmisc[0:128, 0:1], onr[:], wrow[0:1, 0:1],
                                      start=True, stop=True), after=[sg] + cl + d_m)
        r_misc.wrote(blm)
        wv1 = ph.v(lambda e: e.tensor_copy(wcol[:, 0:1], pmisc[0:128, 0:1]), after=[blm])
        r_misc.read(i_m, wv1)
        wv2 = ph.v(lambda e: e.tensor_scalar(wcol[:, 1:2], wcol[:, 0:1], -1.0, 1.0,
                                             ALU.mult, ALU.add), after=[wv1])
        msv = ph.v(lambda e: e.memset(vtb[:, :, :, 64:65], 1.0))
        r_kh, r_q, r_vt = Ring(2), Ring(2), Ring(2)
        r_pS, r_pT = Ring(3), Ring(3)
        r_yT = [Ring(1), Ring(1)]
        r_ys, r_rz, r_am, r_a1 = Ring(2), Ring(2), Ring(2), Ring(2)
        for b in range(B):
            for kvh in range(8):
                i_kh, d_kh = r_kh.acquire()
                lk = ph.dma(lambda e, b=b, kvh=kvh, i=i_kh: e.dma_start(
                    out=kh[:, i * T:(i + 1) * T],
                    in_=k1[b, kvh * 64:(kvh + 1) * 64, :]), after=d_kh)
                r_kh.wrote(lk)
                i_q, d_q = r_q.acquire()
                lqs = []
                for h2 in range(2):
                    lqs.append(ph.dma(lambda e, b=b, kvh=kvh, h2=h2, i=i_q: e.dma_start(
                        out=qsb2[:, (2 * i + h2) * T:(2 * i + h2 + 1) * T],
                        in_=q1[b, (2 * kvh + h2) * 64:(2 * kvh + h2 + 1) * 64, :]),
                        after=d_q))
                r_q.wrote(lqs[-1])
                i_vt, d_vt = r_vt.acquire()
                lv = ph.dma(lambda e, b=b, kvh=kvh, i=i_vt: e.dma_start(
                    out=vtb[:, i, :, 0:64],
                    in_=v1t[b, kvh].rearrange("(n p) d -> p n d", p=128)), after=d_vt)
                r_vt.wrote(lv)
                for qc in range(4):
                    for h2 in range(2):
                        i_y, d_y = r_yT[h2].acquire()
                        ex_t = [None] * 16
                        pt_i = [None] * 16
                        pv_last = None

                        def emit_sc(kt):
                            i_ps, d_ps = r_pS.acquire()
                            sc = ph.t(lambda e, kt=kt, qc=qc, h2=h2, ik=i_kh, iq=i_q,
                                      ip=i_ps: e.matmul(
                                pS[ip][:], kh[:, ik * T + kt * 128: ik * T + (kt + 1) * 128],
                                qsb2[:, (2 * iq + h2) * T + qc * TT:
                                     (2 * iq + h2) * T + (qc + 1) * TT],
                                start=True, stop=True), after=[lk, lqs[h2]] + d_ps)
                            r_pS.wrote(sc)
                            r_kh.read(i_kh, sc)
                            r_q.read(i_q, sc)
                            i_pt, d_pt = r_pT.acquire()
                            ex = ph.s(lambda e, ip=i_ps, it=i_pt: e.activation(
                                pT[:, it * TT:(it + 1) * TT], pS[ip][:], AF.Exp,
                                scale=0.125), after=[sc] + d_pt)
                            r_pS.read(i_ps, ex)
                            r_pT.wrote(ex)
                            ex_t[kt] = ex
                            pt_i[kt] = i_pt

                        def emit_pv(kt):
                            pv = ph.t(lambda e, kt=kt, h2=h2, iv=i_vt, it=pt_i[kt]: e.matmul(
                                yTs[h2][0:65, :], vtb[:, iv, kt, :],
                                pT[:, it * TT:(it + 1) * TT],
                                start=(kt == 0), stop=(kt == 15)),
                                after=([ex_t[kt], lv, msv] + d_y) if kt == 0 else [ex_t[kt]])
                            r_pT.read(pt_i[kt], pv)
                            r_vt.read(i_vt, pv)
                            return pv

                        emit_sc(0)
                        emit_sc(1)
                        for kt in range(2, 16):
                            emit_sc(kt)
                            emit_pv(kt - 2)
                        emit_pv(14)
                        pv_last = emit_pv(15)
                        r_yT[h2].wrote(pv_last)
                        qh = 2 * kvh + h2
                        i_ys, d_ys = r_ys.acquire()
                        ev = ph.s(lambda e, h2=h2, i=i_ys: e.copy(
                            ysb[:, i * TT:(i + 1) * TT], yTs[h2][0:65, :]),
                            after=[pv_last] + d_ys)
                        r_yT[h2].read(i_y, ev)
                        r_ys.wrote(ev)
                        i_rz, d_rz = r_rz.acquire()
                        rc_ = ph.v(lambda e, i=i_ys, ir=i_rz: e.reciprocal(
                            rz[:, ir * TT:(ir + 1) * TT],
                            ysb[64:65, i * TT:(i + 1) * TT]), after=[ev] + d_rz)
                        f1 = ph.v(lambda e, ir=i_rz: e.tensor_scalar(
                            rz[:, ir * TT:(ir + 1) * TT], rz[:, ir * TT:(ir + 1) * TT],
                            wcol[0:1, 1:2], None, ALU.mult), after=[rc_, wv2])
                        r_rz.wrote(f1)
                        i_m, d_m = r_misc.acquire()
                        bc = ph.t(lambda e, ir=i_rz: e.matmul(
                            pmisc[0:64, :], onr[0:1, 0:64], rz[:, ir * TT:(ir + 1) * TT],
                            start=True, stop=True), after=[f1] + cl + d_m)
                        r_misc.wrote(bc)
                        r_rz.read(i_rz, bc)
                        i_am, d_am = r_am.acquire()
                        la = ph.dma(lambda e, b=b, qh=qh, qc=qc, i=i_am: e.dma_start(
                            out=amt[:, i * TT:(i + 1) * TT],
                            in_=amem[b, qh * 64:(qh + 1) * 64, ts(qc, TT)]),
                            after=d_am, q="gpsimd")
                        r_am.wrote(la)
                        t_ = ph.v(lambda e, i=i_ys: e.tensor_mul(
                            ysb[0:64, i * TT:(i + 1) * TT],
                            ysb[0:64, i * TT:(i + 1) * TT], pmisc[0:64, :]),
                            after=[ev, bc])
                        r_misc.read(i_m, t_)
                        r_ys.read(i_ys, t_)
                        r_ys.wrote(t_)
                        i_a1, d_a1 = r_a1.acquire()
                        cb = ph.v(lambda e, i=i_ys, ia=i_am, io=i_a1: e.scalar_tensor_tensor(
                            a1b[:, io * TT:(io + 1) * TT], amt[:, ia * TT:(ia + 1) * TT],
                            wcol[0:64, 0:1], ysb[0:64, i * TT:(i + 1) * TT],
                            ALU.mult, ALU.add),
                            after=[t_, la, wv2] + d_a1)
                        r_am.read(i_am, cb)
                        r_ys.read(i_ys, cb)
                        r_a1.wrote(cb)
                        sta = ph.dma(lambda e, b=b, qh=qh, qc=qc, io=i_a1: e.dma_start(
                            out=a1[b, qh * 64:(qh + 1) * 64, ts(qc, TT)],
                            in_=a1b[:, io * TT:(io + 1) * TT]), after=[cb])
                        r_a1.read(i_a1, sta)
        ph.finalize()



def emit_p8(mk: MK):
    nc = mk.nc
    xn1, x1b, a1 = mk.buf("xn1"), mk.buf("x1b"), mk.buf("a1")
    stats1 = mk.buf("stats1")
    y_out = mk.buf("y_out")
    gw_h = mk.buf("L1_ffn_gate_w")
    w1_h, w3_h, w2_h = mk.buf("L1_ffn_w1"), mk.buf("L1_ffn_w3"), mk.buf("L1_ffn_w2")
    NQ, FQ = 4, 1024
    ph = Phase(mk, "p8")
    with contextlib.ExitStack() as _st:
        idf = _st.enter_context(nc.sbuf_tensor("p8_idf", [128, 128], F32))
        onr = _st.enter_context(nc.sbuf_tensor("p8_onr", [1, 128], F32))
        onc = _st.enter_context(nc.sbuf_tensor("p8_onc", [128, 1], F32))
        gwf = _st.enter_context(nc.sbuf_tensor("p8_gw", [128, NC * 8], F32))
        gws = _st.enter_context(nc.sbuf_tensor("p8_gws", [1, 8], F32))
        bgw = _st.enter_context(nc.sbuf_tensor("p8_bgw", [128, 8], F32))
        xg = _st.enter_context(nc.sbuf_tensor("p8_xg", [128, 3 * TT], F32))
        wi_all = _st.enter_context(nc.sbuf_tensor("p8_wi", [8, B * T], F32))
        xnb = _st.enter_context(nc.sbuf_tensor("p8_xnb", [128, 2 * NC * TT], BF16))
        lgr = _st.enter_context(nc.sbuf_tensor("p8_lgr", [10, TT], F32))
        lt = _st.enter_context(nc.sbuf_tensor("p8_lt", [128, 8], F32))
        lt10 = _st.enter_context(nc.sbuf_tensor("p8_lt10", [128, 10], F32))
        tmp8 = _st.enter_context(nc.sbuf_tensor("p8_t8", [128, 8], F32))
        g2 = _st.enter_context(nc.sbuf_tensor("p8_g2", [128, 8], F32))
        gsc = _st.enter_context(nc.sbuf_tensor("p8_gs", [128, 8], F32))
        gm = _st.enter_context(nc.sbuf_tensor("p8_gm", [128, 6], F32))
        w1q = _st.enter_context(nc.sbuf_tensor("p8_w1", [128, NC * FQ], BF16))
        w3q = _st.enter_context(nc.sbuf_tensor("p8_w3", [128, NC * FQ], BF16))
        w2q = _st.enter_context(nc.sbuf_tensor("p8_w2", [128, (FQ // 128) * D], BF16))
        y_acc = _st.enter_context(nc.sbuf_tensor("p8_y", [128, NC * T], F32))
        hsb = _st.enter_context(nc.sbuf_tensor("p8_h", [128, 2 * (FQ // 128) * TT], BF16))
        sl = _st.enter_context(nc.sbuf_tensor("p8_sl", [128, 2 * TT], F32))
        wib = _st.enter_context(nc.sbuf_tensor("p8_wib", [128, 2 * TT], F32))
        wirow = _st.enter_context(nc.sbuf_tensor("p8_wir", [1, T], F32))
        tmp = _st.enter_context(nc.sbuf_tensor("p8_tmp", [128, 2 * TT], F32))
        a1t = _st.enter_context(nc.sbuf_tensor("p8_a1t", [128, TT], BF16))
        ob = _st.enter_context(nc.sbuf_tensor("p8_ob", [128, 2 * TT], F32))
        otb = _st.enter_context(nc.sbuf_tensor("p8_otb", [128, 2 * 128], F32))
        pA0 = _st.enter_context(nc.psum_tensor("p8_pA0", [128, TT], F32))
        pA1 = _st.enter_context(nc.psum_tensor("p8_pA1", [128, TT], F32))
        pB0 = _st.enter_context(nc.psum_tensor("p8_pB0", [128, TT], F32))
        pB1 = _st.enter_context(nc.psum_tensor("p8_pB1", [128, TT], F32))
        pj0 = _st.enter_context(nc.psum_tensor("p8_p0", [128, TT], F32))
        pj1 = _st.enter_context(nc.psum_tensor("p8_p1", [128, TT], F32))
        pw = _st.enter_context(nc.psum_tensor("p8_pw", [128, TT], F32))
        pmisc = _st.enter_context(nc.psum_tensor("p8_pm", [128, TT], F32))
        pj = [pj0, pj1]
        cl = []
        cl.append(ph.dma(lambda e: e.dma_start(out=idf[:], in_=mk.buf("c_ident_f32")[:])))
        cl.append(ph.dma(lambda e: e.dma_start(out=onr[:], in_=mk.buf("c_ones_row")[:])))
        cl.append(ph.dma(lambda e: e.dma_start(out=onc[:], in_=mk.buf("c_ones_col")[:])))
        gl = []
        for kc in range(NC):
            gl.append(ph.dma(lambda e, kc=kc: e.dma_start(
                out=gwf[:, kc * 8:(kc + 1) * 8], in_=gw_h[ts(kc, 128), :]), q="gpsimd"))
        r_x = Ring(2)
        r_misc, r_pw, r_lgr, r_lt, r_g2, r_gs, r_gm, r_lt10, r_t8 = (Ring(1),) * 9
        r_xg = Ring(3)
        GM = lambda k: gm[:, k:k + 1]
        wi_toks = []
        i_m, d_m = r_misc.acquire()
        csl = []
        for kc in range(NC):
            csl.append(ph.t(lambda e, kc=kc: e.matmul(
                pmisc[0:1, 0:8], onc[:], gwf[:, kc * 8:(kc + 1) * 8],
                start=(kc == 0), stop=(kc == NC - 1)),
                after=(cl + gl + d_m) if kc == 0 else [gl[kc]]))
        r_misc.wrote(csl[-1])
        egs = ph.s(lambda e: e.copy(gws[:], pmisc[0:1, 0:8]), after=[csl[-1]])
        r_misc.read(i_m, egs)
        i_pw, d_pw = r_pw.acquire()
        bc = ph.t(lambda e: e.matmul(pw[0:128, 0:8], onr[:], gws[:],
                                     start=True, stop=True), after=[egs] + cl + d_pw)
        r_pw.wrote(bc)
        ebg = ph.v(lambda e: e.tensor_copy(bgw[:], pw[0:128, 0:8]), after=[bc])
        r_pw.read(i_pw, ebg)
        for b in range(B):
            for tt in range(NTT):
                i_m, d_m = r_misc.acquire()
                mt = []
                for kc in range(NC):
                    i_xg, d_xg = r_xg.acquire()
                    lx = ph.dma(lambda e, b=b, tt=tt, kc=kc, i=i_xg: e.dma_start(
                        out=xg[:, i * TT:(i + 1) * TT],
                        in_=x1b[b, ts(kc, 128), ts(tt, TT)]), after=d_xg)
                    r_xg.wrote(lx)
                    mt.append(ph.t(lambda e, kc=kc, i=i_xg: e.matmul(
                        pmisc[0:8, 0:TT], gwf[:, kc * 8:(kc + 1) * 8],
                        xg[:, i * TT:(i + 1) * TT],
                        start=(kc == 0), stop=(kc == NC - 1)),
                        after=([lx] + cl + gl + d_m) if kc == 0 else [lx]))
                    r_xg.read(i_xg, mt[-1])
                r_misc.wrote(mt[-1])
                i_lg, d_lg = r_lgr.acquire()
                elg = ph.s(lambda e: e.copy(lgr[0:8, :], pmisc[0:8, 0:TT]),
                           after=[mt[-1]] + d_lg)
                r_misc.read(i_m, elg)
                sld = ph.dma(lambda e, b=b, tt=tt: e.dma_start(
                    out=lgr[8:10, :], in_=stats1[b, :, ts(tt, TT)]),
                    after=d_lg, q="gpsimd")
                r_lgr.wrote(sld)
                for qs in range(4):
                    i_pw, d_pw = r_pw.acquire()
                    tf = ph.t(lambda e, qs=qs: e.matmul(
                        pw[0:128, 0:10], lgr[0:10, qs * 128:(qs + 1) * 128], idf[0:10, 0:10],
                        start=True, stop=True), after=[elg, sld] + cl + d_pw)
                    r_lgr.read(i_lg, tf)
                    r_pw.wrote(tf)
                    i_l10, d_l10 = r_lt10.acquire()
                    el10 = ph.s(lambda e: e.copy(lt10[:], pw[0:128, 0:10]), after=[tf] + d_l10)
                    r_pw.read(i_pw, el10)
                    r_lt10.wrote(el10)
                    i_lt, d_lt = r_lt.acquire()
                    i_t8, d_t8 = r_t8.acquire()
                    n1 = ph.v(lambda e: e.tensor_scalar(lt[:], lt10[:, 0:8], lt10[:, 9:10],
                                                        None, ALU.mult),
                              after=[el10] + d_lt)
                    n2 = ph.v(lambda e: e.tensor_scalar(tmp8[:], bgw[:], lt10[:, 8:9],
                                                        None, ALU.mult),
                              after=[el10, ebg] + d_t8)
                    r_t8.wrote(n2)
                    el = ph.v(lambda e: e.tensor_add(lt[:], lt[:], tmp8[:]), after=[n1, n2])
                    r_t8.read(i_t8, el)
                    r_lt10.read(i_l10, el)
                    r_lt.wrote(el)
                    i_gm, d_gm = r_gm.acquire()
                    i_gs, d_gs = r_gs.acquire()
                    i_g2, d_g2 = r_g2.acquire()
                    v1 = ph.v(lambda e: e.reduce_max(GM(0), lt[:], axis=AX.X, negate=True),
                              after=[el] + d_gm)
                    s1 = ph.s(lambda e: e.activation(gsc[:], lt[:], AF.Exp, bias=GM(0),
                                                     accum_out=GM(1)),
                              after=[v1, el] + d_gs)
                    v2 = ph.v(lambda e: e.reciprocal(GM(1), GM(1)), after=[s1])
                    v3 = ph.v(lambda e: e.tensor_scalar(gsc[:], gsc[:], GM(1), None, ALU.mult),
                              after=[v2, s1])
                    v4 = ph.v(lambda e: e.reduce_max(GM(2), gsc[:], axis=AX.X), after=[v3])
                    v5 = ph.v(lambda e: e.tensor_scalar(lt[:], gsc[:], GM(2), None, ALU.is_equal),
                              after=[v4, v3])
                    v6 = ph.v(lambda e: e.scalar_tensor_tensor(g2[:], lt[:], NEG, gsc[:],
                                                               ALU.mult, ALU.add),
                              after=[v5, v3] + d_g2)
                    v7 = ph.v(lambda e: e.reduce_max(GM(3), g2[:], axis=AX.X), after=[v6])
                    v8 = ph.v(lambda e: e.tensor_scalar(g2[:], g2[:], GM(3), None, ALU.is_equal),
                              after=[v7])
                    v9 = ph.v(lambda e: e.tensor_add(GM(4), GM(2), GM(3)), after=[v4, v7])
                    v10 = ph.v(lambda e: e.reciprocal(GM(4), GM(4)), after=[v9])
                    v11 = ph.v(lambda e: e.tensor_add(lt[:], lt[:], g2[:]), after=[v5, v8])
                    v12 = ph.v(lambda e: e.tensor_mul(lt[:], lt[:], gsc[:]), after=[v11, v3])
                    v13 = ph.v(lambda e: e.tensor_scalar(lt[:], lt[:], GM(4), None, ALU.mult),
                               after=[v12, v10])
                    r_gm.wrote(v13)
                    r_gs.wrote(v12)
                    r_g2.wrote(v11)
                    tb = ph.t(lambda e: e.matmul(pw[0:8, 0:128], lt[:], idf[:],
                                                 start=True, stop=True),
                              after=[v13] + cl + d_pw + [el])
                    r_lt.read(i_lt, tb)
                    r_lt.wrote(tb)
                    r_pw.wrote(tb)
                    ew = ph.v(lambda e, b=b, tt=tt, qs=qs: e.tensor_copy(
                        wi_all[0:8, b * T + tt * TT + qs * 128: b * T + tt * TT + (qs + 1) * 128],
                        pw[0:8, 0:128]), after=[tb])
                    r_pw.read(i_pw, ew)
                    wi_toks.append(ew)
        r_pA = [Ring(1), Ring(1)]
        r_pB = [Ring(1), Ring(1)]
        r_pj = Ring(2)
        r_sl, r_wib, r_tmp = Ring(2), Ring(2), Ring(2)
        r_y = Ring(1)
        r_a1t, r_ob, r_otb = Ring(1), Ring(2), Ring(2)
        pA = [pA0, pA1]
        pB = [pB0, pB1]
        WQ = ("sync", "gpsimd", "scalar")
        prev_pe = []
        for b in range(B):
            i_y, d_y = r_y.acquire()
            y_toks = list(d_y)
            r_wr = Ring(1)
            for e_ in range(8):
                i_wr, d_wr = r_wr.acquire()
                swr = ph.dma(lambda e, e_=e_, b=b: e.dma_start(
                    out=wirow[:], in_=wi_all[e_:e_ + 1, b * T:(b + 1) * T]),
                    after=wi_toks + d_wr, q="gpsimd")
                r_wr.wrote(swr)
                for q4 in range(NQ):
                    wl = []
                    qi = 0
                    for kc in range(NC):
                        wl.append(ph.dma(lambda e, e_=e_, q4=q4, kc=kc: e.dma_start(
                            out=w1q[:, kc * FQ:(kc + 1) * FQ],
                            in_=w1_h[e_, ts(kc, 128), q4 * FQ:(q4 + 1) * FQ]),
                            after=prev_pe, q="gpsimd"))
                        qi += 1
                        wl.append(ph.dma(lambda e, e_=e_, q4=q4, kc=kc: e.dma_start(
                            out=w3q[:, kc * FQ:(kc + 1) * FQ],
                            in_=w3_h[e_, ts(kc, 128), q4 * FQ:(q4 + 1) * FQ]),
                            after=prev_pe, q="gpsimd"))
                        qi += 1
                        wl.append(ph.dma(lambda e, e_=e_, q4=q4, kc=kc: e.dma_start(
                            out=w2q[:, kc * D:(kc + 1) * D],
                            in_=w2_h[e_, q4 * FQ + kc * 128: q4 * FQ + (kc + 1) * 128, :]),
                            after=prev_pe, q="gpsimd"))
                        qi += 1
                    for tp in range(2):
                        xis, xls = [], []
                        for hh in range(2):
                            tt = 2 * tp + hh
                            i_x, d_x = r_x.acquire()
                            xl = []
                            for c in range(NC):
                                xl.append(ph.dma(lambda e, b=b, tt=tt, c=c, i=i_x: e.dma_start(
                                    out=xnb[:, i * NC * TT + c * TT: i * NC * TT + (c + 1) * TT],
                                    in_=xn1[b, ts(c, 128), ts(tt, TT)]), after=d_x))
                            r_x.wrote(xl[-1])
                            xis.append(i_x)
                            xls.append(xl)
                        h_toks = {}
                        for fc in range(FQ // 128):
                            ipa, dpa = zip(*[r_pA[hh].acquire() for hh in range(2)])
                            mlast = [None, None]
                            for kc in range(NC):
                                for hh in range(2):
                                    mlast[hh] = ph.t(lambda e, kc=kc, fc=fc, hh=hh,
                                                     i=xis[hh]: e.matmul(
                                        pA[hh][:],
                                        w1q[:, kc * FQ + fc * 128: kc * FQ + (fc + 1) * 128],
                                        xnb[:, i * NC * TT + kc * TT: i * NC * TT + (kc + 1) * TT],
                                        start=(kc == 0), stop=(kc == NC - 1)),
                                        after=([xls[hh][kc]] + wl + list(dpa[hh]))
                                        if kc == 0 else [xls[hh][kc]])
                            malast = list(mlast)
                            for hh in range(2):
                                r_pA[hh].wrote(malast[hh])
                                r_x.read(xis[hh], malast[hh])
                            ipb, dpb = zip(*[r_pB[hh].acquire() for hh in range(2)])
                            for kc in range(NC):
                                for hh in range(2):
                                    mlast[hh] = ph.t(lambda e, kc=kc, fc=fc, hh=hh,
                                                     i=xis[hh]: e.matmul(
                                        pB[hh][:],
                                        w3q[:, kc * FQ + fc * 128: kc * FQ + (fc + 1) * 128],
                                        xnb[:, i * NC * TT + kc * TT: i * NC * TT + (kc + 1) * TT],
                                        start=(kc == 0), stop=(kc == NC - 1)),
                                        after=([xls[hh][kc]] + wl + list(dpb[hh]))
                                        if kc == 0 else [xls[hh][kc]])
                            mblast = list(mlast)
                            for hh in range(2):
                                r_pB[hh].wrote(mblast[hh])
                                r_x.read(xis[hh], mblast[hh])
                            for hh in range(2):
                                i_sl, d_sl = r_sl.acquire()
                                sv = ph.s(lambda e, hh=hh, i=i_sl: e.activation(
                                    sl[:, i * TT:(i + 1) * TT], pA[hh][:], AF.Silu),
                                    after=[malast[hh]] + d_sl)
                                r_pA[hh].read(ipa[hh], sv)
                                r_sl.wrote(sv)
                                hv = ph.v(lambda e, fc=fc, hh=hh, i=i_sl: e.tensor_mul(
                                    hsb[:, (hh * 8 + fc) * TT:(hh * 8 + fc + 1) * TT],
                                    pB[hh][:], sl[:, i * TT:(i + 1) * TT]),
                                    after=[mblast[hh], sv] + (h_toks.get((hh, fc)) or []))
                                r_pB[hh].read(ipb[hh], hv)
                                r_sl.read(i_sl, hv)
                                h_toks[(hh, fc)] = [hv]
                        ebws, wis = [], []
                        for hh in range(2):
                            tt = 2 * tp + hh
                            i_wb, d_wb = r_wib.acquire()
                            i_m, d_m = r_misc.acquire()
                            bw = ph.t(lambda e, tt=tt: e.matmul(
                                pmisc[0:128, 0:TT], onr[:],
                                wirow[0:1, tt * TT:(tt + 1) * TT],
                                start=True, stop=True), after=[swr] + cl + d_m)
                            r_misc.wrote(bw)
                            r_wr.read(i_wr, bw)
                            ebw = ph.s(lambda e, i=i_wb: e.copy(
                                wib[:, i * TT:(i + 1) * TT], pmisc[0:128, 0:TT]),
                                after=[bw] + d_wb)
                            r_misc.read(i_m, ebw)
                            r_wib.wrote(ebw)
                            ebws.append(ebw)
                            wis.append(i_wb)
                        for hh in range(2):
                            tt = 2 * tp + hh
                            for oc in range(NC):
                                i_pj, d_pj = r_pj.acquire()
                                mw = []
                                for j in range(FQ // 128):
                                    mw.append(ph.t(lambda e, j=j, oc=oc, hh=hh,
                                                   i=i_pj: e.matmul(
                                        pj[i][:],
                                        w2q[:, j * D + oc * 128: j * D + (oc + 1) * 128],
                                        hsb[:, (hh * 8 + j) * TT:(hh * 8 + j + 1) * TT],
                                        start=(j == 0), stop=(j == FQ // 128 - 1)),
                                        after=(h_toks[(hh, j)] + wl + d_pj)
                                        if j == 0 else h_toks[(hh, j)]))
                                r_pj.wrote(mw[-1])
                                first = (e_ == 0 and q4 == 0)
                                if first:
                                    yv = ph.v(lambda e, oc=oc, tt=tt, i=i_pj,
                                              iw=wis[hh]: e.tensor_mul(
                                        y_acc[:, oc * T + tt * TT: oc * T + (tt + 1) * TT],
                                        pj[i][:], wib[:, iw * TT:(iw + 1) * TT]),
                                        after=[mw[-1], ebws[hh]] + y_toks)
                                    r_pj.read(i_pj, yv)
                                    r_wib.read(wis[hh], yv)
                                    y_toks.append(yv)
                                else:
                                    i_tm, d_tm = r_tmp.acquire()
                                    tv = ph.v(lambda e, i=i_pj, it=i_tm,
                                              iw=wis[hh]: e.tensor_mul(
                                        tmp[:, it * TT:(it + 1) * TT], pj[i][:],
                                        wib[:, iw * TT:(iw + 1) * TT]),
                                        after=[mw[-1], ebws[hh]] + d_tm)
                                    r_pj.read(i_pj, tv)
                                    r_wib.read(wis[hh], tv)
                                    r_tmp.wrote(tv)
                                    yv = ph.g(lambda e, oc=oc, tt=tt, it=i_tm: e.tensor_add(
                                        y_acc[:, oc * T + tt * TT: oc * T + (tt + 1) * TT],
                                        y_acc[:, oc * T + tt * TT: oc * T + (tt + 1) * TT],
                                        tmp[:, it * TT:(it + 1) * TT]), after=[tv] + y_toks)
                                    r_tmp.read(i_tm, yv)
                                    y_toks.append(yv)
                                for j in range(FQ // 128):
                                    h_toks[(hh, j)] = h_toks[(hh, j)] + [mw[-1]]
                    prev_pe = [mw[-1]]
            for tt in range(NTT):
                for oc in range(NC):
                    i_ob, d_ob = r_ob.acquire()
                    lx = ph.dma(lambda e, b=b, tt=tt, oc=oc, io=i_ob: e.dma_start(
                        out=ob[:, io * TT:(io + 1) * TT], in_=x1b[b, ts(oc, 128), ts(tt, TT)]),
                        after=d_ob)
                    i_a, d_a = r_a1t.acquire()
                    la = ph.dma(lambda e, b=b, tt=tt, oc=oc: e.dma_start(
                        out=a1t[:], in_=a1[b, ts(oc, 128), ts(tt, TT)]), after=d_a,
                        q="gpsimd")
                    r_a1t.wrote(la)
                    ad1 = ph.v(lambda e, oc=oc, tt=tt, io=i_ob: e.tensor_add(
                        ob[:, io * TT:(io + 1) * TT], ob[:, io * TT:(io + 1) * TT],
                        y_acc[:, oc * T + tt * TT: oc * T + (tt + 1) * TT]),
                        after=[lx] + y_toks)
                    ad2 = ph.v(lambda e, io=i_ob: e.tensor_add(
                        ob[:, io * TT:(io + 1) * TT], ob[:, io * TT:(io + 1) * TT], a1t[:]),
                        after=[ad1, la])
                    r_a1t.read(i_a, ad2)
                    r_ob.wrote(ad2)
                    r_y.read(i_y, ad1)
                    for s_ in range(4):
                        i_pt, d_pt = r_pj.acquire()
                        tr = ph.t(lambda e, s_=s_, io=i_ob, ip=i_pt: e.matmul(
                            pj[ip][:, 0:128], ob[:, io * TT + s_ * 128: io * TT + (s_ + 1) * 128],
                            idf[:], start=True, stop=True), after=[ad2] + cl + d_pt)
                        r_pj.wrote(tr)
                        r_ob.read(i_ob, tr)
                        i_ot, d_ot = r_otb.acquire()
                        ev = ph.s(lambda e, io=i_ot, ip=i_pt: e.copy(
                            otb[:, io * 128:(io + 1) * 128], pj[ip][:, 0:128]), after=[tr] + d_ot)
                        r_pj.read(i_pt, ev)
                        r_otb.wrote(ev)
                        st = ph.dma(lambda e, b=b, tt=tt, oc=oc, s_=s_, io=i_ot: e.dma_start(
                            out=y_out[b, tt * TT + s_ * 128: tt * TT + (s_ + 1) * 128,
                                      oc * 128:(oc + 1) * 128],
                            in_=otb[:, io * 128:(io + 1) * 128]), after=[ev])
                        r_otb.read(i_ot, st)
        ph.finalize()



PHASE_GROUPS = [
    ("init", ["x_in", "L0_attn_Wq", "L0_attn_Wkv", "L0_attn_Wout"],
     ["x_fm", "stats0", "xf", "wqT", "wkvT", "woutT"],
     [emit_init, emit_wtrans]),
    ("p1", ["xf", "wqT", "wkvT"],
     ["q0", "k0", "v0t", "qn0", "kn0"],
     [emit_p1]),
    ("p2", ["q0", "k0", "v0t", "qn0", "kn0", "L0_attn_lambda_init", "L0_attn_rms_weight"],
     ["y0"],
     [emit_p2]),
    ("p3", ["y0", "x_fm", "woutT"],
     ["x1"],
     [emit_p3]),
    ("p4", ["xf", "x1", "L0_ffn_wqkv", "L0_ffn_q_scale", "L0_ffn_k_scale",
            "L0_ffn_wo", "L0_ffn_ln_g", "L0_ffn_ln_b", "L0_ffn_w1",
            "L0_ffn_b1", "L0_ffn_w2", "L0_ffn_b2"],
     ["x1b"],
     [emit_p4]),
    ("p5", ["x1b"], ["xn1", "stats1"], [emit_p5]),
    ("p6", ["xn1", "L1_attn_Wq", "L1_attn_Wk", "L1_attn_Wv", "L1_attn_memory",
            "L1_attn_memory_norm"],
     ["q1", "k1", "v1t", "qn1", "kn1", "amem"],
     [emit_p6]),
    ("p7", ["q1", "k1", "v1t", "qn1", "kn1", "amem", "L1_attn_memory_weight"],
     ["a1"],
     [emit_p7]),
    ("p8", ["xn1", "x1b", "stats1", "a1", "L1_ffn_gate_w", "L1_ffn_w1", "L1_ffn_w3",
            "L1_ffn_w2"],
     ["y_out"],
     [emit_p8]),
]


def _in_map_for(mk: MK, core, shard_inputs, state):
    m = {}
    for name in mk.used_inputs:
        if name in CONSTS:
            m[name] = CONSTS[name]
        elif name in WEIGHT_SPECS:
            m[name] = shard_inputs[name]
        elif name == "x_in":
            m[name] = shard_inputs["x_in"][core]
        else:
            m[name] = state[name][core]
    return m


def build_fused():
    mk = MK(reads=[], writes=["y_out"])
    for name, reads, writes, fns in PHASE_GROUPS:
        for fn in fns:
            fn(mk)
    return mk


def run_fused(shard_inputs):
    mk = build_fused()
    in_maps = [_in_map_for(mk, c, shard_inputs, {}) for c in range(8)]
    res = run_bass_kernel_spmd(mk.nc, in_maps, core_ids=list(range(8)))
    return [res.results[c]["y_out"] for c in range(8)]


def run_groups(groups, shard_inputs, state=None):
    state = state if state is not None else {}
    for name, reads, writes, fns in groups:
        mk = MK(reads=reads, writes=writes)
        for fn in fns:
            fn(mk)
        in_maps = [_in_map_for(mk, c, shard_inputs, state) for c in range(8)]
        res = run_bass_kernel_spmd(mk.nc, in_maps, core_ids=list(range(8)))
        for w in writes:
            if w == "y_out":
                state[w] = [res.results[c]["y_out"] for c in range(8)]
            else:
                state[w] = [res.results[c][w] for c in range(8)]
    return state


def kernel(**inputs):
    x = np.ascontiguousarray(inputs["L0_attn_x"])
    shard_inputs = dict(inputs)
    shard_inputs["x_in"] = [x[2 * c:2 * c + 2] for c in range(8)]
    outs = run_fused(shard_inputs)
    return np.concatenate(outs, axis=0)
